# version 01
# baseline (speedup 1.0000x reference)
import contextlib
import math
import numpy as np

import concourse.bass as bass
import concourse.mybir as mybir
from concourse.bass import IndirectOffsetOnAxis as IOA
from concourse.bass_utils import run_bass_kernel_spmd
from concourse.bass import create_sync_update

F32 = mybir.dt.float32
BF16 = mybir.dt.bfloat16
U32 = mybir.dt.uint32
AF = mybir.ActivationFunctionType
ALU = mybir.AluOpType
AX = mybir.AxisListType.X


class CFG:
    def __init__(self, S=2048, D=1024, H=16, E=8, NB=2, n_cores=8,
                 cap_factor=1.2):
        self.S = S
        self.D = D
        self.H = H
        self.DH = 64
        self.E = E
        self.NB = NB
        self.n_cores = n_cores
        self.DC = D // 128
        self.TT = S // 128
        self.QCW = 512
        self.QC = S // self.QCW
        self.N = S + 1
        self.capacity = int(cap_factor * n_cores * NB * S / E)
        self.NCH = (S + E * 127 + 127) // 128
        self.NSLOT = self.NCH * 128
        self.SENT = 1 << 20
        self.GA = 4 if H >= 4 else 2
        self.WS = max(3 * self.GA * 64, D)
        self.WS2 = 3 * self.GA * 64


FULL = CFG()


class Sem:
    def __init__(self, es, nc, name):
        self.h = es.enter_context(nc.semaphore(name))
        self.v = 0

    def inc(self, n=1):
        self.v += n
        return self.h


def ceil_div(a, b):
    return (a + b - 1) // b



class L1:
    def __init__(self, cfg: CFG, upto="all"):
        self.cfg = cfg
        self.upto = upto
        nc = bass.Bass("TRN2", target_bir_lowering=False)
        nc.detect_race_conditions = False
        self.nc = nc
        c = cfg
        S, D, H, E, NB = c.S, c.D, c.H, c.E, c.NB
        DC, NCH, NSLOT = c.DC, c.NCH, c.NSLOT
        HP = H // 2

        dt = nc.dram_tensor
        self.xT = dt("xT", [NB, DC, 128, S], F32, kind="ExternalInput").ap()
        self.gtok = dt("gtok", [DC, 128], F32, kind="ExternalInput").ap()
        self.qkv_wT = dt("qkv_wT", [DC, 128, 3 * D], F32, kind="ExternalInput").ap()
        self.proj_wT = dt("proj_wT", [DC, 128, D], F32, kind="ExternalInput").ap()
        self.proj_bT = dt("proj_bT", [128, DC], F32, kind="ExternalInput").ap()
        self.WqkvT = dt("WqkvT", [DC, 128, 3 * D], F32, kind="ExternalInput").ap()
        self.Wv_nat = dt("Wv_nat", [H * 64, D], F32, kind="ExternalInput").ap()
        self.WswT = dt("WswT", [D, E], F32, kind="ExternalInput").ap()
        self.bswr = dt("bswr", [1, E], F32, kind="ExternalInput").ap()
        self.bqkvT = dt("bqkvT", [128, 3 * HP], F32, kind="ExternalInput").ap()
        self.bvr = dt("bvr", [1, H * 64], F32, kind="ExternalInput").ap()
        self.WeT = dt("WeT", [E, DC, 128, D], F32, kind="ExternalInput").ap()
        self.ber = dt("ber", [1, E * D], F32, kind="ExternalInput").ap()
        self.identf = dt("identf", [128, 128], F32, kind="ExternalInput").ap()
        self.amask0 = dt("amask0", [128, 128], F32, kind="ExternalInput").ap()
        self.amask1 = dt("amask1", [128, 128], F32, kind="ExternalInput").ap()
        self.fmask0 = dt("fmask0", [128, 128], F32, kind="ExternalInput").ap()
        self.sufones = dt("sufones", [128, 129], F32, kind="ExternalInput").ap()
        self.lonesr = dt("lonesr", [E, E], F32, kind="ExternalInput").ap()

        self.x1_out = dt("x1_out", [NB, DC, 128, S], F32, kind="ExternalOutput").ap()
        self.hid_out = dt("hid_out", [NB, S, D], BF16, kind="ExternalOutput").ap()
        self.eout_out = dt("eout_out", [NB, NSLOT, D], BF16, kind="ExternalOutput").ap()
        self.gtbl_out = dt("gtbl_out", [NB, 128, NCH], U32, kind="ExternalOutput").ap()
        self.cnt_out = dt("cnt_out", [NB, E, 1], F32, kind="ExternalOutput").ap()
        self.probs_out = dt("probs_out", [NB, 128, c.TT], F32, kind="ExternalOutput").ap()
        self.ln3_out = dt("ln3_out", [NB, 2, S], F32, kind="ExternalOutput").ap()

        self.ao_hbm = dt("ao_hbm", [NB, DC, 128, S], BF16).ap()
        self.e3_hbm = dt("e3_hbm", [NB, DC, 128, S], BF16).ap()

        es = contextlib.ExitStack()
        self.es = es

        def sbuf(name, p, cc, dty):
            return es.enter_context(nc.sbuf_tensor(name, [p, cc], dty))
        self.sbuf = sbuf

        self.idf = sbuf("idf", 128, 128, F32)
        self.idb = sbuf("idb", 128, 128, BF16)
        self.am0 = sbuf("am0", 128, 128, BF16)
        self.am1 = sbuf("am1", 128, 128, BF16)
        self.fm0 = sbuf("fm0", 128, 128, BF16)
        self.sfo = sbuf("sfo", 128, 129, BF16)
        self.lones = sbuf("lones", E, E, F32)
        self.onesc = sbuf("onesc", 128, 1, BF16)
        self.rowb = sbuf("rowb", 1, 1024, F32)
        self.mu1 = sbuf("mu1", 1, S, F32)
        self.rs1 = sbuf("rs1", 1, S, F32)
        self.mu3 = sbuf("mu3", 1, S, F32)
        self.rs3 = sbuf("rs3", 1, S, F32)
        self.bqkv_sb = sbuf("bqkv_sb", 128, 3 * HP, F32)
        self.bsw_sb = sbuf("bsw_sb", 1, E, F32)
        self.bswb = sbuf("bswb", 128, E, F32)
        self.gidx = sbuf("gidx", 128, NB * NCH, U32)
        self.echw = sbuf("echw", 1, NB * NCH, U32)
        self.echb = sbuf("echb", 1, NB * NCH, U32)
        self.ech_f = sbuf("ech_f", 1, NB * NCH, F32)
        self.iotaNCH = sbuf("iotaNCH", E, NCH, F32)

        self.psA = es.enter_context(nc.psum_tensor("psA", [128, 512], F32))
        self.psB = es.enter_context(nc.psum_tensor("psB", [128, 512], F32))
        self.psC = es.enter_context(nc.psum_tensor("psC", [128, 512], F32))
        self.psD = es.enter_context(nc.psum_tensor("psD", [128, 512], F32))
        self.tp_ps = es.enter_context(nc.psum_tensor("tp_ps", [128, 512], BF16))
        self.st_ps = es.enter_context(nc.psum_tensor("st_ps", [1, 1024], F32))

        self.s_hd = Sem(es, nc, "s_hd")
        self.s_gd = Sem(es, nc, "s_gd")
        self.s_pe = Sem(es, nc, "s_pe")
        self.s_dv = Sem(es, nc, "s_dv")
        self.s_ac = Sem(es, nc, "s_ac")
        self.s_gp = Sem(es, nc, "s_gp")
        self.sems = [self.s_hd, self.s_gd, self.s_pe, self.s_dv, self.s_ac,
                     self.s_gp]

        self.block = es.enter_context(nc.Block())

    def wait_all(self, e):
        for s in self.sems:
            e.wait_ge(s.h, s.v)

    def pe(self, f, inc=True):
        s = self.s_pe

        def g(t):
            f(t)
            if inc:
                t.maybe_drain_then_inc((s.h, 1))
                s.inc()
        self.block.tensor(g)

    def dv(self, f, inc=True):
        s = self.s_dv

        def g(v):
            f(v)
            if inc:
                v.maybe_drain_then_inc((s.h, 1))
                s.inc()
        self.block.vector(g)

    def ac(self, f, inc=True):
        s = self.s_ac

        def g(e):
            f(e)
            if inc:
                e.maybe_drain_then_inc((s.h, 1))
                s.inc()
        self.block.scalar(g)

    def gp(self, f, inc=False):
        s = self.s_gp

        def g(e):
            f(e)
            if inc:
                e.maybe_drain_then_inc((s.h, 1))
                s.inc()
        self.block.gpsimd(g)

    def sy(self, f):
        self.block.sync(f)

    def barrier(self):
        self.pe(lambda t: None, inc=True)
        self.dv(lambda v: None, inc=True)
        self.ac(lambda e: None, inc=True)
        self.gp(lambda e: None, inc=True)
        for emit in (self.pe, self.dv, self.ac, self.gp):
            emit(lambda e: self.wait_all(e), inc=False)
        self.sy(lambda e: self.wait_all(e))

    def wpe(self, e):
        e.wait_ge(self.s_pe.h, self.s_pe.v)

    def wdv(self, e):
        e.wait_ge(self.s_dv.h, self.s_dv.v)

    def wac(self, e):
        e.wait_ge(self.s_ac.h, self.s_ac.v)

    def wgd(self, e):
        e.wait_ge(self.s_gd.h, self.s_gd.v)

    def whd(self, e):
        e.wait_ge(self.s_hd.h, self.s_hd.v)

    def wgp(self, e):
        e.wait_ge(self.s_gp.h, self.s_gp.v)

    def row_bcast(self, row_ap, w, out_ap):
        def mm(t):
            self.wdv(t)
            t.matmul(self.psD[:, 0:w], self.onesrf[0:1, 0:128], row_ap,
                     start=True, stop=True)
        self.pe(mm)

        def cp(v):
            self.wpe(v)
            v.tensor_copy(out_ap, self.psD[:, 0:w])
        self.dv(cp)

    def stats(self, src_fn, n_tok, mu_row, rs_row, src_bf16=False, wf=None):
        c = self.cfg
        for j in range(ceil_div(n_tok, 512)):
            w = min(512, n_tok - j * 512)
            c0 = j * 512

            if not src_bf16:
                def cpb(v, w=w, c0=c0):
                    self.wpe(v)
                    if wf is not None:
                        wf(v)
                    for dc in range(c.DC):
                        v.tensor_copy(self.stg_b[:, dc * 512:dc * 512 + w],
                                      src_fn(dc)[:, c0:c0 + w])
                self.dv(cpb)

            def sqx(e, w=w, c0=c0):
                self.wpe(e)
                self.wdv(e)
                if wf is not None:
                    wf(e)
                for dc in range(c.DC):
                    e.activation(self.sqc[:, dc * 512:dc * 512 + w],
                                 src_fn(dc)[:, c0:c0 + w], AF.Square)
            self.ac(sqx)

            def mm(t, w=w, c0=c0):
                self.wdv(t)
                self.wac(t)
                for dc in range(c.DC):
                    sap = (src_fn(dc)[:, c0:c0 + w] if src_bf16
                           else self.stg_b[:, dc * 512:dc * 512 + w])
                    t.matmul(self.st_ps[0:1, 0:w], self.onesc[:, 0:1],
                             sap, start=(dc == 0), stop=(dc == c.DC - 1))
                for dc in range(c.DC):
                    t.matmul(self.st_ps[0:1, 512:512 + w], self.onesc[:, 0:1],
                             self.sqc[:, dc * 512:dc * 512 + w],
                             start=(dc == 0), stop=(dc == c.DC - 1))
            self.pe(mm)

            def fin(v, w=w, c0=c0):
                self.wpe(v)
                v.tensor_scalar(mu_row[0:1, c0:c0 + w], self.st_ps[0:1, 0:w],
                                1.0 / c.D, None, ALU.mult)
                v.tensor_scalar(self.rowb[0:1, 0:w], self.st_ps[0:1, 512:512 + w],
                                1.0 / c.D, None, ALU.mult)
            self.dv(fin)

            def fin_b(v, w=w, c0=c0):
                self.wdv(v)
                v.tensor_tensor(self.rowb[0:1, 512:512 + w],
                                mu_row[0:1, c0:c0 + w], mu_row[0:1, c0:c0 + w],
                                ALU.mult)
            self.dv(fin_b)

            def fin_c(v, w=w, c0=c0):
                self.wdv(v)
                v.tensor_tensor(self.rowb[0:1, 0:w], self.rowb[0:1, 0:w],
                                self.rowb[0:1, 512:512 + w], ALU.subtract)
            self.dv(fin_c)

            def fin_d(v, w=w, c0=c0):
                self.wdv(v)
                v.tensor_scalar(self.rowb[0:1, 0:w], self.rowb[0:1, 0:w],
                                1e-5, None, ALU.add)
            self.dv(fin_d)

            def sq(e, w=w, c0=c0):
                self.wdv(e)
                e.activation(self.rowb[0:1, 0:w], self.rowb[0:1, 0:w], AF.Sqrt)
            self.ac(sq)

            def rec(v, w=w, c0=c0):
                self.wac(v)
                v.reciprocal(rs_row[0:1, c0:c0 + w], self.rowb[0:1, 0:w])
            self.dv(rec)

    def build(self):
        self.prologue()
        with contextlib.ExitStack() as ph:
            self.alloc_attn(ph)
            for b in range(self.cfg.NB):
                self.attn_batch(b)
                if self.upto in ("ln1", "grp", "load", "stats"):
                    break
        if self.upto in ("attn", "ln1", "grp", "load", "stats"):
            self.sy(lambda e: self.wait_all(e))
            self.es.close()
            return self.nc
        with contextlib.ExitStack() as ph:
            self.alloc_ffn(ph)
            for b in range(self.cfg.NB):
                self.ffn_batch(b)
                if self.upto in ("e3chk", "grp2"):
                    break
        if self.upto in ("ffn", "e3chk", "grp2"):
            self.sy(lambda e: self.wait_all(e))
            self.es.close()
            return self.nc
        with contextlib.ExitStack() as ph:
            self.alloc_moe(ph)
            for b in range(self.cfg.NB):
                self.moe_batch(b)
        self.sy(lambda e: self.wait_all(e))
        self.es.close()
        return self.nc

    def prologue(self):
        L = self

        def loads(g):
            g.dma_start(out=L.idf[:, :], in_=L.identf).then_inc(L.s_gd.inc(16), 16)
            g.dma_start(out=L.idb[:, :], in_=L.identf).then_inc(L.s_gd.inc(16), 16)
            g.dma_start(out=L.am0[:, :], in_=L.amask0).then_inc(L.s_gd.inc(16), 16)
            g.dma_start(out=L.am1[:, :], in_=L.amask1).then_inc(L.s_gd.inc(16), 16)
            g.dma_start(out=L.fm0[:, :], in_=L.fmask0).then_inc(L.s_gd.inc(16), 16)
            g.dma_start(out=L.sfo[:, :], in_=L.sufones).then_inc(L.s_gd.inc(16), 16)
            g.dma_start(out=L.lones[:, :], in_=L.lonesr).then_inc(L.s_gd.inc(16), 16)
            g.dma_start(out=L.bqkv_sb[:, :], in_=L.bqkvT).then_inc(L.s_gd.inc(16), 16)
            g.dma_start(out=L.bsw_sb[:, :], in_=L.bswr).then_inc(L.s_gd.inc(16), 16)
            g.memset(L.onesc[:, :], 1.0)
        self.gp(loads, inc=True)
        self.onescf = self.sbuf("onescf", 128, 1, F32)
        self.onesrf = self.sbuf("onesrf", 128, 128, F32)
        self.gp(lambda g: g.memset(L.onescf[:, :], 1.0), inc=True)
        self.gp(lambda g: g.memset(L.onesrf[:, :], 1.0), inc=True)
        self.barrier()

        def mm(t):
            t.matmul(L.psA[:, 0:L.cfg.E], L.onesrf[0:1, 0:128], L.bsw_sb[0:1, :],
                     start=True, stop=True)
        self.pe(mm)

        def cp(v):
            L.wpe(v)
            v.tensor_copy(L.bswb[:, :], L.psA[:, 0:L.cfg.E])
        self.dv(cp)
        self.barrier()

    def alloc_attn(self, ph):
        nc, c = self.nc, self.cfg
        en = ph.enter_context
        self.x_sb = en(nc.sbuf_tensor("x_sb", [128, c.DC * c.S], F32))
        self.xb_sb = en(nc.sbuf_tensor("xb_sb", [128, c.DC * c.S], BF16))
        self.wg = en(nc.sbuf_tensor("wg", [128, c.DC * c.WS], BF16))
        self.stg_b = en(nc.sbuf_tensor("stg_b", [128, c.DC * 512], BF16))
        self.sqc = en(nc.sbuf_tensor("sqc", [128, c.DC * 512], BF16))
        self.aoj = en(nc.sbuf_tensor("aoj", [128, c.DC * 512], BF16))
        GA = c.GA
        self.kbuf = en(nc.sbuf_tensor("kbuf", [128, (GA // 2) * c.N + 2], BF16))
        self.vbuf = en(nc.sbuf_tensor("vbuf", [128, (c.TT + 1) * GA * 65], BF16))
        self.qbuf = en(nc.sbuf_tensor("qbuf", [128, (GA // 2) * c.QCW], BF16))
        self.expS = en(nc.sbuf_tensor("expS", [128, c.QCW], BF16))
        self.zrow = en(nc.sbuf_tensor("zrow", [128, 512], F32))
        self.zbb = en(nc.sbuf_tensor("zbb", [128, 512], F32))
        self.mvb = en(nc.sbuf_tensor("mvb", [128, 1024], F32))
        self.gq = en(nc.sbuf_tensor("gq", [128, GA], BF16))
        self.gcol = en(nc.sbuf_tensor("gcol", [128, c.DC], BF16))

    def ln_apply(self, mu_row, rs_row, src_sb, dst_sb, dst_dtype_note=None,
                 residual_sb=None):
        c, L = self.cfg, self
        for j in range(c.QC):
            c0 = j * 512
            self.row_bcast(rs_row[0:1, c0:c0 + 512], 512, L.zbb[:, 0:512])
            self.row_bcast(mu_row[0:1, c0:c0 + 512], 512, L.zrow[:, 0:512])

            def ap(v, c0=c0):
                L.wdv(v)
                for dc in range(c.DC):
                    sl = slice(dc * c.S + c0, dc * c.S + c0 + 512)
                    v.tensor_tensor(L.mvb[:, 0:512], src_sb[:, sl],
                                    L.zrow[:, 0:512], ALU.subtract)
                    if residual_sb is None:
                        v.tensor_tensor(dst_sb[:, sl], L.mvb[:, 0:512],
                                        L.zbb[:, 0:512], ALU.mult)
                    else:
                        v.tensor_tensor(L.mvb[:, 512:1024], L.mvb[:, 0:512],
                                        L.zbb[:, 0:512], ALU.mult)
                        v.tensor_tensor(dst_sb[:, sl], residual_sb[:, sl],
                                        L.mvb[:, 512:1024], ALU.add)
            self.dv(ap)

    def spill_fm(self, sb_src, hbm_dst, b, ncols, engine="gp"):
        c, L = self.cfg, self

        def go(g):
            L.wdv(g)
            for dc in range(c.DC):
                g.dma_start(out=hbm_dst[b, dc, :, 0:ncols],
                            in_=sb_src[:, dc * ncols:(dc + 1) * ncols]) \
                 .then_inc((L.s_gd if engine == "gp" else L.s_hd).inc(16), 16)
        if engine == "gp":
            self.gp(go)
        else:
            self.sy(go)

    def _dbg_spill(self, b):
        L = self
        c = self.cfg
        S, DC = c.S, c.DC

        def spdbg(sy, b=b):
            self.wait_all(sy)
            for dc in range(DC):
                sy.dma_start(out=L.x1_out[b, dc],
                             in_=L.x_sb[:, dc * S:(dc + 1) * S]) \
                  .then_inc(L.s_hd.inc(16), 16)
            sy.wait_ge(L.s_hd.h, L.s_hd.v)
        self.sy(spdbg)

    def attn_batch(self, b):
        c, L = self.cfg, self
        S, D, DC, QC, TT, GA = c.S, c.D, c.DC, c.QC, c.TT, c.GA

        def ldx(sy):
            for dc in range(DC):
                sy.dma_start(out=L.x_sb[:, dc * S:(dc + 1) * S],
                             in_=L.xT[b, dc]).then_inc(L.s_hd.inc(16), 16)
        self.sy(ldx)
        if self.upto == "load":
            self._dbg_spill(b)
            return
        self.stats(lambda dc: L.x_sb[:, dc * S:(dc + 1) * S], S, L.mu1, L.rs1,
                   wf=lambda e: L.whd(e))
        self.barrier()
        if self.upto == "stats":
            self._dbg_spill(b)
            return

        self.ln_apply(L.mu1, L.rs1, L.x_sb, L.xb_sb)

        def gl(g):
            for dc in range(DC):
                g.dma_start(out=L.gcol[:, dc:dc + 1],
                            in_=L.gtok[dc].rearrange("(p o) -> p o", o=1)) \
                 .then_inc(L.s_gd.inc(16), 16)
        self.gp(gl)
        self.barrier()

        if self.upto == "ln1":
            def spdbg(sy, b=b):
                self.wait_all(sy)
                for dc in range(DC):
                    sy.dma_start(out=L.x1_out[b, dc],
                                 in_=L.x_sb[:, dc * S:(dc + 1) * S]) \
                      .then_inc(L.s_hd.inc(16), 16)
            self.sy(spdbg)
            return
        for grp in range(c.H // GA):
            h0 = grp * GA
            self.attn_group(b, h0)
        self.barrier()
        if self.upto == "grp":
            def spdbg(sy, b=b):
                self.wait_all(sy)
                for dc in range(DC):
                    sy.dma_start(out=L.x1_out[b, dc],
                                 in_=L.x_sb[:, dc * S:(dc + 1) * S]) \
                      .then_inc(L.s_hd.inc(16), 16)
            self.sy(spdbg)
            return
        self.attn_tail(b)

    def attn_group(self, b, h0):
        c, L = self.cfg, self
        S, D, DC, QC, TT, GA = c.S, c.D, c.DC, c.QC, c.TT, c.GA
        N = c.N
        WS = c.WS
        G64 = GA * 64

        def ldw(g, h0=h0):
            L.wpe(g)
            for dc in range(DC):
                for ti, off in enumerate((h0 * 64, D + h0 * 64,
                                          2 * D + h0 * 64)):
                    g.dma_start(
                        out=L.wg[:, dc * WS + ti * G64:dc * WS + (ti + 1) * G64],
                        in_=L.qkv_wT[dc][:, off:off + G64]) \
                     .then_inc(L.s_gd.inc(16), 16)
        self.gp(ldw)

        def mmg(t, h0=h0):
            L.wgd(t)
            L.wdv(t)
            for p in range(GA // 2):
                for dc in range(DC):
                    t.matmul(L.psC[:, p:p + 1],
                             L.wg[:, dc * WS + G64 + p * 128:dc * WS + G64 + (p + 1) * 128],
                             L.gcol[:, dc:dc + 1],
                             start=(dc == 0), stop=(dc == DC - 1))
                for dc in range(DC):
                    t.matmul(L.psC[:, GA // 2 + p:GA // 2 + p + 1],
                             L.wg[:, dc * WS + 2 * G64 + p * 128:dc * WS + 2 * G64 + (p + 1) * 128],
                             L.gcol[:, dc:dc + 1],
                             start=(dc == 0), stop=(dc == DC - 1))
        self.pe(mmg)

        def cpgq(v):
            L.wpe(v)
            v.tensor_copy(L.gq[:, 0:GA], L.psC[:, 0:GA])
        self.dv(cpgq)

        for p in range(GA // 2):
            hh = h0 + 2 * p
            for j in range(QC):
                def mm(t, p=p, j=j):
                    L.wdv(t)
                    for dc in range(DC):
                        t.matmul(L.psA[:, 0:512],
                                 L.wg[:, dc * WS + G64 + p * 128:dc * WS + G64 + (p + 1) * 128],
                                 L.xb_sb[:, dc * S + j * 512:dc * S + (j + 1) * 512],
                                 start=(dc == 0), stop=(dc == DC - 1))
                self.pe(mm)

                def cp(v, p=p, j=j):
                    L.wpe(v)
                    v.tensor_copy(L.kbuf[:, p * N + j * 512:p * N + (j + 1) * 512],
                                  L.psA[:, 0:512])
                self.dv(cp)
            def cpg(v, p=p):
                v.tensor_copy(L.kbuf[:, p * N + S:p * N + S + 1],
                              L.gq[:, p:p + 1])
            self.dv(cpg)

        for tt in range(TT):
            def mm(t, tt=tt, h0=h0):
                L.wdv(t)
                for dc in range(DC):
                    t.matmul(L.psB[:, 0:GA * 64],
                             L.xb_sb[:, dc * S + tt * 128:dc * S + (tt + 1) * 128],
                             L.wg[:, dc * WS + 2 * G64:dc * WS + 3 * G64],
                             start=(dc == 0), stop=(dc == DC - 1))
            self.pe(mm)

            def cp(v, tt=tt):
                L.wpe(v)
                for i in range(GA):
                    base = (tt * GA + i) * 65
                    v.tensor_copy(L.vbuf[:, base:base + 64],
                                  L.psB[:, i * 64:(i + 1) * 64])
                v.memset(L.vbuf[:, tt * GA * 65:(tt + 1) * GA * 65].rearrange(
                    "p (i c) -> p i c", i=GA)[:, :, 64:65].rearrange("p i c -> p (i c)"), 1.0)
            self.dv(cp)
        for i in range(GA):
            p_, sub = divmod(i, 2)
            oc = GA // 2 + p_
            r0 = sub * 64

            def tp(t, i=i, oc=oc, r0=r0):
                L.wdv(t)
                t.transpose(L.tp_ps[0:1, 0:64], L.gq[r0:r0 + 64, oc:oc + 1],
                            L.idb[r0:r0 + 64, r0:r0 + 64])
            self.pe(tp)

            def cp(v, i=i):
                L.wpe(v)
                base = (TT * GA + i) * 65
                v.tensor_copy(L.vbuf[0:1, base:base + 64], L.tp_ps[0:1, 0:64])
                v.memset(L.vbuf[0:1, base + 64:base + 65], 1.0)
            self.dv(cp)
        self.barrier()

        for qc in range(QC):
            for p in range(GA // 2):
                def mm(t, p=p, qc=qc):
                    L.wdv(t)
                    for dc in range(DC):
                        t.matmul(L.psA[:, 0:512],
                                 L.wg[:, dc * WS + p * 128:dc * WS + (p + 1) * 128],
                                 L.xb_sb[:, dc * S + qc * 512:dc * S + (qc + 1) * 512],
                                 start=(dc == 0), stop=(dc == DC - 1))
                self.pe(mm)

                def cp(v, p=p):
                    L.wpe(v)
                    v.tensor_scalar(L.qbuf[:, p * 512:(p + 1) * 512],
                                    L.psA[:, 0:512], 0.125, None, ALU.mult)
                self.dv(cp)

            ktn = min(TT, 4 * qc + 5)
            for i in range(GA):
                p, sub = divmod(i, 2)
                r0 = sub * 64
                first = [True]
                for kt in range(ktn):
                    qlo = max(0, kt - 4 * qc - 1) * 128

                    def sc_mm(t, p=p, r0=r0, kt=kt, qlo=qlo, qc=qc):
                        L.wdv(t)
                        t.matmul(L.psB[:, qlo:512],
                                 L.kbuf[r0:r0 + 64, p * N + kt * 128:p * N + (kt + 1) * 128],
                                 L.qbuf[r0:r0 + 64, p * 512 + qlo:p * 512 + 512],
                                 start=True, stop=True)
                    self.pe(sc_mm)

                    def ex(e, qlo=qlo):
                        L.wpe(e)
                        e.activation(L.expS[:, qlo:512], L.psB[:, qlo:512], AF.Exp)
                    self.ac(ex)

                    def msk(v, kt=kt, qc=qc, qlo=qlo):
                        L.wac(v)
                        qs0 = kt - 4 * qc
                        qs1 = kt - 4 * qc - 1
                        if 0 <= qs0 < 4:
                            v.tensor_tensor(L.expS[:, qs0 * 128:(qs0 + 1) * 128],
                                            L.expS[:, qs0 * 128:(qs0 + 1) * 128],
                                            L.am0[:, :], ALU.mult)
                        if 0 <= qs1 < 4:
                            v.tensor_tensor(L.expS[:, qs1 * 128:(qs1 + 1) * 128],
                                            L.expS[:, qs1 * 128:(qs1 + 1) * 128],
                                            L.am1[:, :], ALU.mult)
                    self.dv(msk)

                    def av(t, i=i, kt=kt, qlo=qlo, first=first):
                        L.wdv(t)
                        t.matmul(L.psC[0:65, qlo:512],
                                 L.vbuf[:, (kt * GA + i) * 65:(kt * GA + i + 1) * 65],
                                 L.expS[:, qlo:512],
                                 start=first[0], stop=False)
                        first[0] = False
                    self.pe(av, inc=False)

                def sc_g(t, p=p, r0=r0):
                    t.matmul(L.psB[0:1, 0:512],
                             L.kbuf[r0:r0 + 64, p * N + S:p * N + S + 1],
                             L.qbuf[r0:r0 + 64, p * 512:p * 512 + 512],
                             start=True, stop=True)
                self.pe(sc_g)

                def ex_g(e):
                    L.wpe(e)
                    e.activation(L.expS[0:1, 0:512], L.psB[0:1, 0:512], AF.Exp)
                self.ac(ex_g)

                def av_g(t, i=i):
                    L.wac(t)
                    t.matmul(L.psC[0:65, 0:512],
                             L.vbuf[0:1, (TT * GA + i) * 65:(TT * GA + i + 1) * 65],
                             L.expS[0:1, 0:512],
                             start=False, stop=True)
                self.pe(av_g)

                def rz(v):
                    L.wpe(v)
                    v.reciprocal(L.zrow[64:65, 0:512], L.psC[64:65, 0:512])
                self.dv(rz)

                def zb(t):
                    L.wdv(t)
                    t.matmul(L.psD[0:64, 0:512], L.onesrf[64:65, 0:64],
                             L.zrow[64:65, 0:512], start=True, stop=True)
                self.pe(zb)

                def ev(v, i=i, qc=qc, h0=h0, b=b):
                    L.wpe(v)
                    L.wgd(v)
                    v.tensor_copy(L.zbb[0:64, 0:512], L.psD[0:64, 0:512])
                    hh = h0 + i
                    r0o = (hh % 2) * 64
                    v.scalar_tensor_tensor(
                        L.aoj[r0o:r0o + 64, i * 512:(i + 1) * 512],
                        L.psC[0:64, 0:512], 1.0, L.zbb[0:64, 0:512],
                        ALU.mult, ALU.mult)
                self.dv(ev)

                def sp(g, i=i, qc=qc, h0=h0, b=b):
                    L.wdv(g)
                    hh = h0 + i
                    dc = hh // 2
                    r0o = (hh % 2) * 64
                    g.dma_start(
                        out=L.ao_hbm[b, dc, r0o:r0o + 64, qc * 512:(qc + 1) * 512],
                        in_=L.aoj[r0o:r0o + 64, i * 512:(i + 1) * 512]) \
                     .then_inc(L.s_gd.inc(16), 16)
                self.gp(sp)
                self.pe(lambda t: L.wdv(t), inc=False)

    def attn_tail(self, b):
        c, L = self.cfg, self
        S, D, DC, QC = c.S, c.D, c.DC, c.QC
        WS = c.WS

        def ldw(g):
            for dc in range(DC):
                g.dma_start(out=L.wg[:, dc * WS:dc * WS + D],
                            in_=L.proj_wT[dc]).then_inc(L.s_gd.inc(16), 16)
            g.dma_start(out=L.zrow[:, 0:DC], in_=L.proj_bT) \
             .then_inc(L.s_gd.inc(16), 16)
        self.gp(ldw)
        self.barrier()

        for j in range(QC):
            def lda(g, j=j, b=b):
                L.wpe(g)
                for dc in range(DC):
                    g.dma_start(out=L.aoj[:, dc * 512:(dc + 1) * 512],
                                in_=L.ao_hbm[b, dc, :, j * 512:(j + 1) * 512]) \
                     .then_inc(L.s_gd.inc(16), 16)
            self.gp(lda)
            for oc in range(DC):
                def mm(t, j=j, oc=oc):
                    L.wdv(t)
                    L.wgd(t)
                    for dc in range(DC):
                        t.matmul(L.psA[:, 0:512],
                                 L.wg[:, dc * WS + oc * 128:dc * WS + (oc + 1) * 128],
                                 L.aoj[:, dc * 512:(dc + 1) * 512],
                                 start=(dc == 0), stop=(dc == DC - 1))
                self.pe(mm)

                def cp(v, j=j, oc=oc):
                    L.wpe(v)
                    v.tensor_scalar(L.xb_sb[:, oc * S + j * 512:oc * S + (j + 1) * 512],
                                    L.psA[:, 0:512], L.zrow[:, oc:oc + 1], None,
                                    ALU.add)
                self.dv(cp)
        self.barrier()

        self.stats(lambda dc: L.xb_sb[:, dc * S:(dc + 1) * S], S, L.mu3, L.rs3,
                   src_bf16=True)
        self.barrier()
        self.ln_apply(L.mu3, L.rs3, L.xb_sb, L.x_sb, residual_sb=L.x_sb)
        self.barrier()
        def spx(sy, b=b):
            for dc in range(DC):
                sy.dma_start(out=L.x1_out[b, dc],
                             in_=L.x_sb[:, dc * S:(dc + 1) * S]) \
                  .then_inc(L.s_hd.inc(16), 16)
        self.sy(spx)

        self.stats(lambda dc: L.x_sb[:, dc * S:(dc + 1) * S], S, L.mu3, L.rs3)
        self.barrier()

        def sp3(g, b=b):
            g.dma_start(out=L.ln3_out[b, 0:1, :], in_=L.mu3[0:1, :]) \
             .then_inc(L.s_gd.inc(16), 16)
            g.dma_start(out=L.ln3_out[b, 1:2, :], in_=L.rs3[0:1, :]) \
             .then_inc(L.s_gd.inc(16), 16)
        self.gp(sp3)

        self.ln_apply(L.mu3, L.rs3, L.x_sb, L.xb_sb)
        self.barrier()
        self.spill_fm(L.xb_sb, L.e3_hbm, b, S)
        self.barrier()

    def alloc_ffn(self, ph):
        nc, c = self.nc, self.cfg
        en = ph.enter_context
        GF = c.GA
        self.GF = GF
        self.e_sb = en(nc.sbuf_tensor("e_sb", [128, c.DC * c.S], BF16))
        self.wg2 = en(nc.sbuf_tensor("wg2", [128, c.DC * c.WS2], BF16))
        self.kbuf = en(nc.sbuf_tensor("kbuf2", [128, (GF // 2) * c.S], BF16))
        self.vue = en(nc.sbuf_tensor("vue", [128, c.TT * GF * 73], BF16))
        self.vsuf = en(nc.sbuf_tensor("vsuf", [128, GF * c.S], BF16))
        self.tsuf = en(nc.sbuf_tensor("tsuf", [128, 2 * GF], F32))
        self.qbuf = en(nc.sbuf_tensor("qbuf2", [128, (GF // 2) * c.QCW], BF16))
        self.expS = en(nc.sbuf_tensor("expS2", [128, c.QCW], BF16))
        self.av_sb = en(nc.sbuf_tensor("av_sb", [128, c.QCW], BF16))
        self.hid_tm = en(nc.sbuf_tensor("hid_tm", [128, c.TT * c.D], BF16))
        self.lacc = en(nc.sbuf_tensor("lacc", [128, c.TT * c.E], F32))
        self.rzcol = en(nc.sbuf_tensor("rzcol", [128, 2], F32))
        self.bvcol = en(nc.sbuf_tensor("bvcol", [128, 1], BF16))
        self.M2 = en(nc.sbuf_tensor("M2", [128, c.DC * c.H * c.E], BF16))
        self.ubias = en(nc.sbuf_tensor("ubias", [c.E, c.H + 2], F32))
        self.bvb = en(nc.sbuf_tensor("bvb", [128, c.H * 64], F32))
        self.bvr_sb = en(nc.sbuf_tensor("bvr_sb", [1, c.H * 64], F32))
        self.zrow = en(nc.sbuf_tensor("zrow2", [128, 512], F32))
        self.mvb = en(nc.sbuf_tensor("mvb2", [128, 512], F32))
        self.iota8 = en(nc.sbuf_tensor("iota8", [128, c.E], F32))
        self.iota8m = en(nc.sbuf_tensor("iota8m", [128, c.E], F32))
        self.iotap = en(nc.sbuf_tensor("iotap", [128, 1], F32))
        self.probs_sb = en(nc.sbuf_tensor("probs_sb", [128, c.TT], F32))
        self.routes_sb = en(nc.sbuf_tensor("routes_sb", [128, c.TT * 8], U32))
        self.rf32_sb = en(nc.sbuf_tensor("rf32_sb", [128, c.TT], F32))
        self.max8 = en(nc.sbuf_tensor("max8", [128, 8], F32))
        self.mi8 = en(nc.sbuf_tensor("mi8", [128, 8], U32))
        self.rscr = en(nc.sbuf_tensor("rscr", [128, 10], F32))
        self.ohf = en(nc.sbuf_tensor("ohf", [128, c.E], F32))
        self.ohb = en(nc.sbuf_tensor("ohb", [128, c.E], BF16))
        self.ohT = en(nc.sbuf_tensor("ohT", [c.E, c.S], F32))
        self.scan = en(nc.sbuf_tensor("scan", [c.E, c.S], F32))
        self.pos_sb = en(nc.sbuf_tensor("pos_sb", [128, c.TT], F32))
        self.cntc = en(nc.sbuf_tensor("cntc", [c.E, 4], F32))
        self.cnt_u = en(nc.sbuf_tensor("cnt_u", [c.E, 1], U32))
        self.base_r = en(nc.sbuf_tensor("base_r", [1, c.E], F32))
        self.baseb = en(nc.sbuf_tensor("baseb", [128, c.E], F32))
        self.slotu = en(nc.sbuf_tensor("slotu", [128, 1], U32))
        self.slotu2 = en(nc.sbuf_tensor("slotu2", [128, 2], U32))
        self.tokid = en(nc.sbuf_tensor("tokid", [128, 1], U32))
        self.sentc = en(nc.sbuf_tensor("sentc", [128, c.NCH], U32))
        L = self

        def pro(g):
            g.iota(L.iota8[:, :], [[1, c.E]], base=0, channel_multiplier=0,
                   allow_small_or_imprecise_dtypes=True)
            g.iota(L.iota8m[:, :], [[1, c.E]], base=-c.E, channel_multiplier=0,
                   allow_small_or_imprecise_dtypes=True)
            g.iota(L.iotap[:, :], [[0, 1]], base=0, channel_multiplier=1,
                   allow_small_or_imprecise_dtypes=True)
            g.iota(L.iotaNCH[:, :], [[128, c.NCH]], base=0, channel_multiplier=0,
                   allow_small_or_imprecise_dtypes=True)
            g.memset(L.sentc[:, :], c.SENT)
            g.memset(L.lacc[:, :], 0.0)
            g.dma_start(out=L.bvr_sb[:, :], in_=L.bvr).then_inc(L.s_gd.inc(16), 16)
            for r in range(c.H * 64 // 128):
                g.dma_start(out=L.hid_tm[:, r * c.D:(r + 1) * c.D],
                            in_=L.Wv_nat[r * 128:(r + 1) * 128, :]) \
                 .then_inc(L.s_gd.inc(16), 16)
            for dc in range(c.DC):
                g.dma_start(out=L.vsuf[:, dc * c.E:(dc + 1) * c.E],
                            in_=L.WswT[dc * 128:(dc + 1) * 128, :]) \
                 .then_inc(L.s_gd.inc(16), 16)
        self.gp(pro, inc=True)
        self.barrier()

        for j in range(ceil_div(c.H * 64, 512)):
            w = min(512, c.H * 64 - j * 512)
            self.row_bcast(L.bvr_sb[0:1, j * 512:j * 512 + w], w,
                           L.bvb[:, j * 512:j * 512 + w])
        for h in range(c.H):
            r, sub = divmod(h, 2)
            rb = sub * 64
            for dc in range(c.DC):
                def mm(t, h=h, r=r, rb=rb, dc=dc):
                    L.wdv(t)
                    t.matmul(L.psA[:, 0:c.E],
                             L.hid_tm[rb:rb + 64, r * c.D + dc * 128:r * c.D + (dc + 1) * 128],
                             L.vsuf[rb:rb + 64, (dc if False else ((h * 64) // 128) * c.E):((h * 64) // 128) * c.E + c.E]
                             if False else L.vsuf[rb:rb + 64, ((h * 64) // 128) * c.E:((h * 64) // 128 + 1) * c.E],
                             start=True, stop=True)
                self.pe(mm)

                def cp(v, h=h, dc=dc):
                    L.wpe(v)
                    v.tensor_copy(L.M2[:, (dc * c.H + h) * c.E:(dc * c.H + h + 1) * c.E],
                                  L.psA[:, 0:c.E])
                self.dv(cp)
            def cpb(v, h=h):
                v.tensor_copy(L.bvcol[:, 0:1],
                              L.bqkv_sb[:, 2 * (c.H // 2) + h // 2:2 * (c.H // 2) + h // 2 + 1])
            self.dv(cpb)

            def mmu(t, h=h, rb=rb):
                L.wdv(t)
                t.matmul(L.psB[0:c.E, 0:1],
                         L.vsuf[rb:rb + 64, ((h * 64) // 128) * c.E:((h * 64) // 128 + 1) * c.E],
                         L.bvcol[rb:rb + 64, 0:1],
                         start=True, stop=True)
            self.pe(mmu)

            def cpu_(v, h=h):
                L.wpe(v)
                v.tensor_copy(L.ubias[:, h:h + 1], L.psB[0:c.E, 0:1])
            self.dv(cpu_)
        self.barrier()
        def us(v):
            v.reduce_sum(L.ubias[:, c.H:c.H + 1], L.ubias[:, 0:c.H], axis=AX)
        self.dv(us)

        def ut(t):
            L.wdv(t)
            t.transpose(L.psB[0:1, 0:c.E], L.ubias[0:c.E, c.H:c.H + 1],
                        L.idf[0:c.E, 0:c.E])
        self.pe(ut)

        def uc(v):
            L.wpe(v)
            v.tensor_copy(L.rowb[0:1, 0:c.E], L.psB[0:1, 0:c.E])
        self.dv(uc)

        def ub(t):
            L.wdv(t)
            t.matmul(L.psB[:, 0:c.E], L.onesrf[0:1, 0:128], L.rowb[0:1, 0:c.E],
                     start=True, stop=True)
        self.pe(ub)

        def ua(v):
            L.wpe(v)
            v.tensor_tensor(L.bswb[:, :], L.bswb[:, :], L.psB[:, 0:c.E], ALU.add)
        self.dv(ua)
        self.barrier()

    def ffn_batch(self, b):
        c, L = self.cfg, self
        S, D, DC, QC, TT, E = c.S, c.D, c.DC, c.QC, c.TT, c.E
        GF = self.GF
        HP2 = c.H // 2

        def lde(g, b=b):
            for dc in range(DC):
                g.dma_start(out=L.e_sb[:, dc * S:(dc + 1) * S],
                            in_=L.e3_hbm[b, dc]).then_inc(L.s_gd.inc(16), 16)
        self.gp(lde)
        self.dv(lambda v: v.memset(L.lacc[:, :], 0.0))
        self.barrier()
        if self.upto == "e3chk":
            def dump(g, b=b):
                for dc in range(DC):
                    g.dma_start(
                        out=L.hid_out[b].rearrange("s d -> (s d)").rearrange(
                            "(c p f) -> c p f", c=DC, p=128)[dc],
                        in_=L.e_sb[:, dc * S:(dc + 1) * S]) \
                     .then_inc(L.s_gd.inc(16), 16)
            self.gp(dump)
            self.barrier()
            return

        for grp in range(c.H // GF):
            self.ffn_group(b, grp * GF)
        self.barrier()
        if self.upto == "grp2":
            self.ffn_route(b)
            return
        self.ffn_route(b)

    def ffn_group(self, b, h0):
        c, L = self.cfg, self
        S, D, DC, QC, TT, E = c.S, c.D, c.DC, c.QC, c.TT, c.E
        GF = self.GF
        HP2 = c.H // 2
        WS2 = c.WS2
        G64 = GF * 64

        def ldw(g, h0=h0):
            L.wpe(g)
            for dc in range(DC):
                for ti, off in enumerate((h0 * 64, D + h0 * 64,
                                          2 * D + h0 * 64)):
                    g.dma_start(
                        out=L.wg2[:, dc * WS2 + ti * G64:dc * WS2 + (ti + 1) * G64],
                        in_=L.WqkvT[dc][:, off:off + G64]) \
                     .then_inc(L.s_gd.inc(16), 16)
        self.gp(ldw)

        for p in range(GF // 2):
            hh = h0 + 2 * p
            bcol = HP2 + hh // 2
            for j in range(QC):
                def mm(t, p=p, j=j):
                    L.wdv(t)
                    L.wgd(t)
                    for dc in range(DC):
                        t.matmul(L.psA[:, 0:512],
                                 L.wg2[:, dc * WS2 + G64 + p * 128:dc * WS2 + G64 + (p + 1) * 128],
                                 L.e_sb[:, dc * S + j * 512:dc * S + (j + 1) * 512],
                                 start=(dc == 0), stop=(dc == DC - 1))
                self.pe(mm)

                def cp(v, p=p, j=j, bcol=bcol):
                    L.wpe(v)
                    v.tensor_scalar(L.kbuf[:, p * S + j * 512:p * S + (j + 1) * 512],
                                    L.psA[:, 0:512],
                                    L.bqkv_sb[:, bcol:bcol + 1], None, ALU.add)
                self.dv(cp)

        for tt in range(TT):
            def mmv(t, tt=tt, h0=h0):
                L.wdv(t)
                for dc in range(DC):
                    t.matmul(L.psB[:, 0:GF * 64],
                             L.e_sb[:, dc * S + tt * 128:dc * S + (tt + 1) * 128],
                             L.wg2[:, dc * WS2 + 2 * G64:dc * WS2 + 3 * G64],
                             start=(dc == 0), stop=(dc == DC - 1))
                for dc in range(DC):
                    t.matmul(L.psC[:, 0:GF * E],
                             L.e_sb[:, dc * S + tt * 128:dc * S + (tt + 1) * 128],
                             L.M2[:, (dc * c.H + h0) * E:(dc * c.H + h0 + GF) * E],
                             start=(dc == 0), stop=(dc == DC - 1))
            self.pe(mmv)

            def cpv(v, tt=tt, h0=h0):
                L.wpe(v)
                for i in range(GF):
                    base = (tt * GF + i) * 73
                    v.tensor_tensor(L.vue[:, base:base + 64],
                                    L.psB[:, i * 64:(i + 1) * 64],
                                    L.bvb[:, (h0 + i) * 64:(h0 + i + 1) * 64],
                                    ALU.add)
                    v.memset(L.vue[:, base + 64:base + 65], 1.0)
                    v.tensor_copy(L.vue[:, base + 65:base + 73],
                                  L.psC[:, i * E:(i + 1) * E])
            self.dv(cpv)
        self.barrier()

        for i in range(GF):
            def z0(v, i=i):
                v.memset(L.tsuf[:, 2 * i:2 * i + 1], 0.0)
            self.dv(z0)
        for tt in range(TT - 1, -1, -1):
            for i in range(GF):
                def mm(t, tt=tt, i=i):
                    L.wdv(t)
                    t.matmul(L.psD[0:73, 0:129],
                             L.vue[:, (tt * GF + i) * 73:(tt * GF + i + 1) * 73],
                             L.sfo[:, :], start=True, stop=True)
                self.pe(mm)

                def fin(v, tt=tt, i=i):
                    L.wpe(v)
                    v.tensor_scalar(L.vsuf[0:73, i * S + tt * 128:i * S + (tt + 1) * 128],
                                    L.psD[0:73, 0:128],
                                    L.tsuf[0:73, 2 * i:2 * i + 1], None, ALU.add)
                self.dv(fin)

                def fint(v, tt=tt, i=i):
                    L.wdv(v)
                    v.tensor_tensor(L.tsuf[0:73, 2 * i:2 * i + 1],
                                    L.tsuf[0:73, 2 * i:2 * i + 1],
                                    L.psD[0:73, 128:129], ALU.add)
                self.dv(fint)
        self.barrier()

        for qc in range(QC):
            for p in range(GF // 2):
                hh = h0 + 2 * p
                bcol = hh // 2

                def mm(t, p=p, qc=qc):
                    L.wdv(t)
                    for dc in range(DC):
                        t.matmul(L.psA[:, 0:512],
                                 L.wg2[:, dc * WS2 + p * 128:dc * WS2 + (p + 1) * 128],
                                 L.e_sb[:, dc * S + qc * 512:dc * S + (qc + 1) * 512],
                                 start=(dc == 0), stop=(dc == DC - 1))
                self.pe(mm)

                def cp(v, p=p, bcol=bcol):
                    L.wpe(v)
                    v.tensor_scalar(L.qbuf[:, p * 512:(p + 1) * 512],
                                    L.psA[:, 0:512],
                                    L.bqkv_sb[:, bcol:bcol + 1],
                                    1.0 / math.sqrt(D), ALU.add, ALU.mult)
                self.dv(cp)

            ktn = min(TT, 4 * qc + 4)
            for i in range(GF):
                p, sub = divmod(i, 2)
                r0 = sub * 64
                first = [True]
                for kt in range(ktn):
                    qlo = max(0, kt - 4 * qc) * 128

                    def smm(t, p=p, r0=r0, kt=kt, qlo=qlo):
                        L.wdv(t)
                        t.matmul(L.psB[:, qlo:512],
                                 L.kbuf[r0:r0 + 64, p * S + kt * 128:p * S + (kt + 1) * 128],
                                 L.qbuf[r0:r0 + 64, p * 512 + qlo:p * 512 + 512],
                                 start=True, stop=True)
                    self.pe(smm)

                    def ex(e, qlo=qlo):
                        L.wpe(e)
                        e.activation(L.expS[:, qlo:512], L.psB[:, qlo:512], AF.Exp)
                    self.ac(ex)

                    def msk(v, kt=kt, qc=qc):
                        L.wac(v)
                        qs0 = kt - 4 * qc
                        if 0 <= qs0 < 4:
                            v.tensor_tensor(L.expS[:, qs0 * 128:(qs0 + 1) * 128],
                                            L.expS[:, qs0 * 128:(qs0 + 1) * 128],
                                            L.fm0[:, :], ALU.mult)
                    self.dv(msk)

                    def av(t, i=i, kt=kt, qlo=qlo, first=first):
                        L.wdv(t)
                        t.matmul(L.psC[0:73, qlo:512],
                                 L.vue[:, (kt * GF + i) * 73:(kt * GF + i + 1) * 73],
                                 L.expS[:, qlo:512],
                                 start=first[0], stop=(kt == ktn - 1))
                        first[0] = False
                    self.pe(av, inc=(kt == ktn - 1))

                def fin(v, i=i, qc=qc):
                    L.wpe(v)
                    v.tensor_tensor(L.av_sb[0:73, 0:512], L.psC[0:73, 0:512],
                                    L.vsuf[0:73, i * S + qc * 512:i * S + (qc + 1) * 512],
                                    ALU.add)
                self.dv(fin)
                for qs in range(4):
                    def tp(t, qs=qs):
                        L.wdv(t)
                        t.transpose(L.tp_ps[:, 0:73],
                                    L.av_sb[0:73, qs * 128:(qs + 1) * 128],
                                    L.idb[0:73, 0:73])
                    self.pe(tp)

                    def rcp(v, qs=qs):
                        L.wpe(v)
                        v.reciprocal(L.rzcol[:, 0:1], L.tp_ps[:, 64:65])
                    self.dv(rcp)

                    def nrm(v, i=i, qc=qc, qs=qs, h0=h0):
                        L.wdv(v)
                        tile = qc * 4 + qs
                        hh = h0 + i
                        v.tensor_scalar(L.hid_tm[:, tile * D + hh * 64:tile * D + hh * 64 + 64],
                                        L.tp_ps[:, 0:64],
                                        L.rzcol[:, 0:1], None, ALU.mult)
                        v.scalar_tensor_tensor(
                            L.lacc[:, tile * E:(tile + 1) * E],
                            L.tp_ps[:, 65:73], L.rzcol[:, 0:1],
                            L.lacc[:, tile * E:(tile + 1) * E],
                            ALU.mult, ALU.add)
                    self.dv(nrm)
                self.pe(lambda t: L.wdv(t), inc=False)

    def ffn_route(self, b):
        c, L = self.cfg, self
        S, D, DC, TT, E, NCH = c.S, c.D, c.DC, c.TT, c.E, c.NCH

        def sph(g, b=b):
            L.wdv(g)
            for tt in range(TT):
                g.dma_start(out=L.hid_out[b, tt * 128:(tt + 1) * 128, :],
                            in_=L.hid_tm[:, tt * D:(tt + 1) * D]) \
                 .then_inc(L.s_gd.inc(16), 16)
        self.gp(sph)

        for tt in range(TT):
            def fin(v, tt=tt):
                v.tensor_tensor(L.lacc[:, tt * E:(tt + 1) * E],
                                L.lacc[:, tt * E:(tt + 1) * E],
                                L.bswb[:, :], ALU.add)
            self.dv(fin)

            def fin_m(v, tt=tt):
                L.wdv(v)
                v.reduce_max(L.rscr[:, 0:1], L.lacc[:, tt * E:(tt + 1) * E], axis=AX)
            self.dv(fin_m)

            def fin2(v, tt=tt):
                L.wdv(v)
                v.tensor_scalar(L.rscr[:, 2:2 + E], L.lacc[:, tt * E:(tt + 1) * E],
                                L.rscr[:, 0:1], None, ALU.subtract)
            self.dv(fin2)

            def ex(e, tt=tt):
                L.wdv(e)
                e.activation(L.rscr[:, 2:2 + E], L.rscr[:, 2:2 + E], AF.Exp,
                             accum_out=L.rscr[:, 1:2])
            self.ac(ex)

            def rcp2(v, tt=tt):
                L.wac(v)
                v.reciprocal(L.probs_sb[:, tt:tt + 1], L.rscr[:, 1:2])
            self.dv(rcp2)

            def rt_a(v, tt=tt):
                L.wdv(v)
                v.tensor_scalar(L.max8[:, :], L.lacc[:, tt * E:(tt + 1) * E],
                                L.rscr[:, 0:1], None, ALU.is_equal)
            self.dv(rt_a)

            def rt_b(v, tt=tt):
                L.wdv(v)
                v.tensor_tensor(L.max8[:, :], L.max8[:, :], L.iota8m[:, :],
                                ALU.mult)
            self.dv(rt_b)

            def rt_c(v, tt=tt):
                L.wdv(v)
                v.tensor_reduce(L.rscr[:, 3:4], L.max8[:, :], AX, ALU.min)
            self.dv(rt_c)

            def rt_d(v, tt=tt):
                L.wdv(v)
                v.tensor_scalar(L.rf32_sb[:, tt:tt + 1], L.rscr[:, 3:4],
                                float(E), None, ALU.add)
            self.dv(rt_d)

            def rt2(v, tt=tt):
                L.wdv(v)
                v.tensor_scalar(L.ohf[:, :], L.iota8[:, :], L.rf32_sb[:, tt:tt + 1],
                                None, ALU.is_equal)
            self.dv(rt2)

            def rt2b(v, tt=tt):
                L.wdv(v)
                v.tensor_copy(L.ohb[:, :], L.ohf[:, :])
            self.dv(rt2b)

            def tp(t, tt=tt):
                L.wdv(t)
                t.transpose(L.tp_ps[0:E, 0:128], L.ohb[:, :], L.idb[:, :])
            self.pe(tp)

            def cpo(v, tt=tt):
                L.wpe(v)
                v.tensor_copy(L.ohT[:, tt * 128:(tt + 1) * 128], L.tp_ps[0:E, 0:128])
            self.dv(cpo)
        self.barrier()

        def sc0(v):
            v.tensor_tensor_scan(L.scan[:, :], L.ohT[:, :], L.ohT[:, :], 0.0,
                                 ALU.add, ALU.bypass)
        self.dv(sc0)

        steps_sc = [
            lambda v: v.tensor_copy(L.cntc[:, 0:1], L.scan[:, S - 1:S]),
            lambda v: v.tensor_scalar(L.cntc[:, 2:3], L.cntc[:, 0:1], 127.0,
                                      None, ALU.add),
            lambda v: v.tensor_copy(L.cnt_u[:, 0:1], L.cntc[:, 2:3]),
            lambda v: v.tensor_scalar(L.cnt_u[:, 0:1], L.cnt_u[:, 0:1], 7,
                                      None, ALU.logical_shift_right),
            lambda v: v.tensor_scalar(L.cnt_u[:, 0:1], L.cnt_u[:, 0:1], 7,
                                      None, ALU.logical_shift_left),
            lambda v: v.tensor_copy(L.cntc[:, 1:2], L.cnt_u[:, 0:1]),
        ]
        for st_ in steps_sc:
            self.dv(lambda v, st_=st_: (L.wdv(v), st_(v)))

        def spc(g, b=b):
            L.wdv(g)
            g.dma_start(out=L.cnt_out[b], in_=L.cntc[0:c.E, 0:1]) \
             .then_inc(L.s_gd.inc(16), 16)
            g.dma_start(out=L.probs_out[b], in_=L.probs_sb[:, :]) \
             .then_inc(L.s_gd.inc(16), 16)
        self.gp(spc)

        def bmm(t):
            L.wdv(t)
            t.matmul(L.psD[0:1, 0:E], L.cntc[0:E, 1:2], L.lones[:, :],
                     start=True, stop=True)
        self.pe(bmm)

        def bcp(v):
            L.wpe(v)
            v.tensor_copy(L.base_r[0:1, :], L.psD[0:1, 0:E])
        self.dv(bcp)
        self.row_bcast(L.base_r[0:1, 0:E], E, L.baseb[:, 0:E])
        self.barrier()

        def pf(g, b=b):
            g.dma_start(out=L.gtbl_out[b], in_=L.sentc[:, 0:NCH]) \
             .then_inc(L.s_gd.inc(16), 16)
        self.gp(pf)

        for tt in range(TT):
            def prod(v, tt=tt):
                L.wgd(v)
                v.tensor_tensor(L.rscr[:, 0:0 + E] if False else L.ohT[:, tt * 128:(tt + 1) * 128],
                                L.ohT[:, tt * 128:(tt + 1) * 128],
                                L.scan[:, tt * 128:(tt + 1) * 128], ALU.mult)
            self.dv(prod)

            def tpp(t, tt=tt):
                L.wdv(t)
                t.transpose(L.psD[:, 0:E], L.ohT[0:E, tt * 128:(tt + 1) * 128],
                            L.idf[0:E, 0:E])
            self.pe(tpp)

            def mkslot0(v, tt=tt):
                L.wpe(v)
                v.reduce_sum(L.pos_sb[:, tt:tt + 1], L.psD[:, 0:E], axis=AX)
            self.dv(mkslot0)

            steps_mk = [
                lambda v, tt=tt: v.tensor_scalar(L.ohf[:, :], L.iota8[:, :],
                                                 L.rf32_sb[:, tt:tt + 1],
                                                 None, ALU.is_equal),
                lambda v: v.tensor_tensor(L.rscr[:, 2:2 + E], L.ohf[:, :],
                                          L.baseb[:, 0:E], ALU.mult),
                lambda v: v.reduce_sum(L.rscr[:, 1:2], L.rscr[:, 2:2 + E],
                                       axis=AX),
                lambda v, tt=tt: v.tensor_tensor(L.rscr[:, 0:1], L.rscr[:, 1:2],
                                                 L.pos_sb[:, tt:tt + 1], ALU.add),
                lambda v: v.tensor_scalar(L.rscr[:, 0:1], L.rscr[:, 0:1], -1.0,
                                          None, ALU.add),
                lambda v: v.tensor_copy(L.slotu2[:, 0:1], L.rscr[:, 0:1]),
                lambda v: v.tensor_scalar(L.slotu2[:, 1:2], L.slotu2[:, 0:1],
                                          127, None, ALU.bitwise_and),
                lambda v: v.tensor_scalar(L.slotu2[:, 0:1], L.slotu2[:, 0:1],
                                          7, None, ALU.logical_shift_right),
                lambda v: v.tensor_copy(L.rscr[:, 1:2], L.slotu2[:, 1:2]),
                lambda v: v.tensor_copy(L.rscr[:, 2:3], L.slotu2[:, 0:1]),
                lambda v: v.tensor_scalar(L.rscr[:, 1:2], L.rscr[:, 1:2],
                                          float(NCH), None, ALU.mult),
                lambda v: v.tensor_tensor(L.rscr[:, 0:1], L.rscr[:, 1:2],
                                          L.rscr[:, 2:3], ALU.add),
                lambda v: v.tensor_copy(L.slotu[:, 0:1], L.rscr[:, 0:1]),
            ]
            for st_ in steps_mk:
                self.dv(lambda v, st_=st_: (L.wdv(v), st_(v)))

            def sct(g, tt=tt, b=b):
                L.wdv(g)
                L.wgd(g)
                g.iota(L.tokid[:, :], [[0, 1]], base=tt * 128, channel_multiplier=1,
                       allow_small_or_imprecise_dtypes=True)
                g.indirect_dma_start(
                    out=L.gtbl_out.rearrange("b p q -> (b p q)").rearrange(
                        "(n o) -> n o", o=1),
                    out_offset=IOA(L.slotu[:, 0:1], 0),
                    in_=L.tokid[:, 0:1],
                    in_offset=None,
                    element_offset=b * 128 * NCH,
                ).then_inc(L.s_gd.inc(16), 16)
            self.gp(sct)
        self.barrier()

        def rb(g, b=b):
            g.dma_start(out=L.gidx[:, b * NCH:(b + 1) * NCH], in_=L.gtbl_out[b]) \
             .then_inc(L.s_gd.inc(16), 16)
        self.gp(rb)

        def cmp_(v):
            v.tensor_scalar(L.ohT[0:E, 0:NCH], L.iotaNCH[0:E, :],
                            L.base_r[0:1, 0:E].rearrange("o e -> e o") if False
                            else L.cntc[0:E, 3:4], None, ALU.is_ge)
        def bt(t):
            L.wdv(t)
            t.transpose(L.psD[0:E, 0:1], L.base_r[0:1, 0:E], L.idf[0:1, 0:1])
        self.pe(bt)

        def btc(v):
            L.wpe(v)
            v.tensor_copy(L.cntc[:, 3:4], L.psD[0:E, 0:1])
        self.dv(btc)
        self.dv(cmp_)

        def emm(t):
            L.wdv(t)
            t.matmul(L.psD[0:1, 0:NCH], L.onescf[0:E, 0:1], L.ohT[0:E, 0:NCH],
                     start=True, stop=True)
        self.pe(emm)

        def etb(v, b=b):
            L.wpe(v)
            v.tensor_scalar(L.ech_f[0:1, b * NCH:(b + 1) * NCH],
                            L.psD[0:1, 0:NCH], -1.0, None, ALU.add)
        self.dv(etb)

        def etb2(v, b=b):
            L.wdv(v)
            v.tensor_copy(L.echw[0:1, b * NCH:(b + 1) * NCH],
                          L.ech_f[0:1, b * NCH:(b + 1) * NCH])
        self.dv(etb2)
        self.barrier()

    def alloc_moe(self, ph):
        nc, c = self.nc, self.cfg
        en = ph.enter_context
        self.weT_sb = en(nc.sbuf_tensor("weT_sb", [128, c.E * c.DC * c.D], BF16))
        self.be_sb = en(nc.sbuf_tensor("be_sb", [1, c.E * c.D], BF16))
        self.onesr_b = en(nc.sbuf_tensor("onesr_b", [1, 128], BF16))
        self.stg_tm = en(nc.sbuf_tensor("stg_tm", [128, c.D], BF16))
        self.stg_fm = en(nc.sbuf_tensor("stg_fm", [128, c.D], BF16))
        self.eo_sb = en(nc.sbuf_tensor("eo_sb", [128, c.D], BF16))
        self.gcf = en(nc.sbuf_tensor("gcf", [128, c.NCH], F32))
        self.r_e = nc.alloc_register(mybir.EngineType.PE, "moe_re")
        L = self

        def pro(g):
            g.memset(L.onesr_b[:, :], 1.0)
            g.memset(L.stg_tm[:, :], 0.0)
            g.dma_start(out=L.be_sb[:, :], in_=L.ber).then_inc(L.s_gd.inc(16), 16)
            for e in range(c.E):
                for dc in range(c.DC):
                    g.dma_start(
                        out=L.weT_sb[:, (e * c.DC + dc) * c.D:(e * c.DC + dc + 1) * c.D],
                        in_=L.WeT[e, dc]).then_inc(L.s_gd.inc(16), 16)
        self.gp(pro, inc=True)
        self.barrier()

    def moe_batch(self, b):
        c, L = self.cfg, self
        D, DC, E, NCH, S = c.D, c.DC, c.E, c.NCH, c.S
        nhalf = ceil_div(D, 512)

        steps_cl = [
            lambda v: v.tensor_copy(L.gcf[:, :],
                                    L.gidx[:, b * NCH:(b + 1) * NCH]),
            lambda v: v.tensor_scalar(L.gcf[:, :], L.gcf[:, :],
                                      float(S - 1), None, ALU.min),
            lambda v: v.tensor_copy(L.gidx[:, b * NCH:(b + 1) * NCH],
                                    L.gcf[:, :]),
        ]
        for st_ in steps_cl:
            self.dv(lambda v, st_=st_: (L.wdv(v), st_(v)))
        self.barrier()

        for ch in range(NCH):
            def ga(g, ch=ch, b=b):
                L.wdv(g)
                g.indirect_dma_start(
                    out=L.stg_tm[:, :],
                    out_offset=None,
                    in_=L.hid_out.rearrange("b s d -> (b s) d"),
                    in_offset=IOA(L.gidx[:, b * NCH + ch:b * NCH + ch + 1], 0),
                    element_offset=b * S * D,
                ).then_inc(L.s_gd.inc(16), 16)
            self.gp(ga)

            for dc in range(DC):
                def tp(t, dc=dc):
                    L.wgd(t)
                    L.wdv(t)
                    t.transpose(L.tp_ps[:, 0:128],
                                L.stg_tm[:, dc * 128:(dc + 1) * 128],
                                L.idb[:, :])
                self.pe(tp)

                def cp(v, dc=dc):
                    L.wpe(v)
                    v.tensor_copy(L.stg_fm[:, dc * 128:(dc + 1) * 128],
                                  L.tp_ps[:, 0:128])
                self.dv(cp)

            def mm(t, ch=ch, b=b):
                L.wdv(t)
                r_e = L.r_e
                t.reg_load(r_e, L.echw[0:1, b * NCH + ch:b * NCH + ch + 1])
                for e in range(E):
                    with t.If_eq(r_e, e):
                        for half in range(nhalf):
                            w = min(512, D - half * 512)
                            pp = L.psA if half == 0 else L.psB
                            for dc in range(DC):
                                co = (e * DC + dc) * D + half * 512
                                t.matmul(pp[:, 0:w],
                                         L.stg_fm[:, dc * 128:(dc + 1) * 128],
                                         L.weT_sb[:, co:co + w],
                                         start=(dc == 0), stop=False)
                        for half in range(nhalf):
                            w = min(512, D - half * 512)
                            pp = L.psA if half == 0 else L.psB
                            t.matmul(pp[:, 0:w],
                                     L.onesr_b[0:1, 0:128],
                                     L.be_sb[0:1, e * D + half * 512:e * D + half * 512 + w],
                                     start=False, stop=True)
                t.end_ifs()
            self.pe(mm)

            def ev(v, ch=ch, b=b):
                L.wpe(v)
                L.wgd(v)
                for half in range(nhalf):
                    w = min(512, D - half * 512)
                    pp = L.psA if half == 0 else L.psB
                    v.tensor_copy(L.eo_sb[:, half * 512:half * 512 + w], pp[:, 0:w])
            self.dv(ev)

            def st(g, ch=ch, b=b):
                L.wdv(g)
                g.dma_start(out=L.eout_out[b, ch * 128:(ch + 1) * 128, :],
                            in_=L.eo_sb[:, :]).then_inc(L.s_gd.inc(16), 16)
            self.gp(st)
        self.barrier()


def build_launch1(cfg, upto="all"):
    return L1(cfg, upto).build()



class L2:
    def __init__(self, cfg: CFG):
        self.cfg = cfg
        nc = bass.Bass("TRN2", target_bir_lowering=False)
        nc.detect_race_conditions = False
        self.nc = nc
        c = cfg
        S, D, E, NB, DC, TT, NCH, NSLOT = (c.S, c.D, c.E, c.NB, c.DC, c.TT,
                                           c.NCH, c.NSLOT)
        NBG = c.n_cores * NB
        self.NBG = NBG
        dt = nc.dram_tensor
        self.x1_in = dt("x1_in", [NB, DC, 128, S], F32, kind="ExternalInput").ap()
        self.hid_in = dt("hid_in", [NB, S, D], BF16, kind="ExternalInput").ap()
        self.eout_in = dt("eout_in", [NB, NSLOT, D], BF16, kind="ExternalInput").ap()
        self.gtbl_in = dt("gtbl_in", [NB, 128, NCH], U32, kind="ExternalInput").ap()
        self.probs_in = dt("probs_in", [NB, 128, TT], F32, kind="ExternalInput").ap()
        self.ln3_in = dt("ln3_in", [NB, 2, S], F32, kind="ExternalInput").ap()
        self.allcnt = dt("allcnt", [NBG, E], F32, kind="ExternalInput").ap()
        self.offmask = dt("offmask", [NBG, 2 * NB], F32, kind="ExternalInput").ap()
        self.identf = dt("identf2", [128, 128], F32, kind="ExternalInput").ap()
        self.lonesr = dt("lonesr", [E, E], F32, kind="ExternalInput").ap()
        self.gammaT = dt("gammaT", [128, DC], F32, kind="ExternalInput").ap()
        self.betaT = dt("betaT", [128, DC], F32, kind="ExternalInput").ap()
        self.out = dt("out", [NB, DC, 128, S], F32, kind="ExternalOutput").ap()
        self.y_hbm = dt("y_hbm", [NB * S + 128, D], BF16).ap()

        es = contextlib.ExitStack()
        self.es = es

        def sbuf(name, p, cc, dty):
            return es.enter_context(nc.sbuf_tensor(name, [p, cc], dty))
        self.sbuf = sbuf
        self.onesc = sbuf("onesc", 128, 1, BF16)
        self.onesrf = sbuf("onesrf", 128, 128, F32)
        self.lones = sbuf("lones", E, E, F32)
        self.rowb = sbuf("rowb", 1, 2048, F32)
        self.mu4 = sbuf("mu4", 1, S, F32)
        self.rs4 = sbuf("rs4", 1, S, F32)
        self.mu3 = sbuf("mu3", 1, S, F32)
        self.rs3 = sbuf("rs3", 1, S, F32)
        self.acnt = sbuf("acnt", NBG, E, F32)
        self.mycc = sbuf("mycc", E, 4, F32)
        self.mycu = sbuf("mycu", E, 1, U32)
        self.offc = sbuf("offc", E, 2, F32)
        self.maskc = sbuf("maskc", NBG, 2 * NB, F32)
        self.idb = sbuf("idb", 128, 128, BF16)
        self.iotaN = sbuf("iotaN", E, NCH, F32)
        self.iotaP = sbuf("iotaP", 128, 1, F32)
        self.base_r = sbuf("base_r", 1, E, F32)
        self.rows = sbuf("rows", 1, 6 * NCH, F32)
        self.cmpA = sbuf("cmpA", E, NCH, F32)
        self.cmpB = sbuf("cmpB", E, NCH, F32)
        self.thrb = sbuf("thrb", 128, NCH, F32)
        self.keptm = sbuf("keptm", 128, NCH, F32)
        self.gidxf = sbuf("gidxf", 128, NCH, F32)
        self.sidx = sbuf("sidx", 128, NCH, U32)
        self.gidx = sbuf("gidx", 128, NCH, U32)
        self.eo_sb = sbuf("eo_sb", 128, D, BF16)
        self.yt_sb = sbuf("yt_sb", 128, D, BF16)
        self.gam = sbuf("gam", 128, DC, F32)
        self.bet = sbuf("bet", 128, DC, F32)
        self.zbb = sbuf("zbb", 128, 512, F32)
        self.zrow = sbuf("zrow", 128, 512, F32)
        self.mvb = sbuf("mvb", 128, 1024, F32)
        self.x1c = sbuf("x1c", 128, DC * 512, F32)
        self.x2_sb = sbuf("x2_sb", 128, DC * S, F32)
        self.xb_sb = sbuf("xb_sb", 128, DC * S, BF16)
        self.stg_b = sbuf("stg_b", 128, DC * 512, BF16)
        self.sqc = sbuf("sqc", 128, DC * 512, BF16)

        self.psA = es.enter_context(nc.psum_tensor("psA", [128, 512], F32))
        self.psD = es.enter_context(nc.psum_tensor("psD", [128, 512], F32))
        self.tp_ps = es.enter_context(nc.psum_tensor("tp_ps", [128, 512], BF16))
        self.st_ps = es.enter_context(nc.psum_tensor("st_ps", [1, 1024], F32))

        self.s_hd = Sem(es, nc, "s_hd")
        self.s_gd = Sem(es, nc, "s_gd")
        self.s_pe = Sem(es, nc, "s_pe")
        self.s_dv = Sem(es, nc, "s_dv")
        self.s_ac = Sem(es, nc, "s_ac")
        self.s_gp = Sem(es, nc, "s_gp")
        self.sems = [self.s_hd, self.s_gd, self.s_pe, self.s_dv, self.s_ac,
                     self.s_gp]
        self.block = es.enter_context(nc.Block())

    wait_all = L1.wait_all
    pe = L1.pe
    dv = L1.dv
    ac = L1.ac
    gp = L1.gp
    sy = L1.sy
    barrier = L1.barrier
    wpe = L1.wpe
    wdv = L1.wdv
    wac = L1.wac
    wgd = L1.wgd
    whd = L1.whd
    wgp = L1.wgp
    row_bcast = L1.row_bcast
    stats = L1.stats
    ln_apply = L1.ln_apply

    def build(self):
        c, L = self.cfg, self
        S, D, E, NB, DC, TT, NCH = c.S, c.D, c.E, c.NB, c.DC, c.TT, c.NCH
        NBG = self.NBG

        def pro(g):
            g.memset(L.onesc[:, :], 1.0)
            g.memset(L.onesrf[:, :], 1.0)
            g.iota(L.iotaN[:, :], [[128, NCH]], base=0, channel_multiplier=0,
                   allow_small_or_imprecise_dtypes=True)
            g.iota(L.iotaP[:, :], [[0, 1]], base=0, channel_multiplier=1,
                   allow_small_or_imprecise_dtypes=True)
            g.dma_start(out=L.lones[:, :], in_=L.lonesr).then_inc(L.s_gd.inc(16), 16)
            g.dma_start(out=L.acnt[:, :], in_=L.allcnt).then_inc(L.s_gd.inc(16), 16)
            g.dma_start(out=L.maskc[:, :], in_=L.offmask) \
             .then_inc(L.s_gd.inc(16), 16)
            g.dma_start(out=L.gam[:, :], in_=L.gammaT).then_inc(L.s_gd.inc(16), 16)
            g.dma_start(out=L.bet[:, :], in_=L.betaT).then_inc(L.s_gd.inc(16), 16)
            g.dma_start(out=L.idb[:, :], in_=L.identf).then_inc(L.s_gd.inc(16), 16)
        self.gp(pro, inc=True)
        self.barrier()

        for b in range(NB):
            self.batch(b)
        self.sy(lambda e: self.wait_all(e))
        self.es.close()
        return self.nc

    def batch(self, b):
        c, L = self.cfg, self
        S, D, E, NB, DC, TT, NCH = c.S, c.D, c.E, c.NB, c.DC, c.TT, c.NCH
        NBG = self.NBG
        myrow = None

        def t0(v, b=b):
            pass
        def mm0(t, b=b):
            self.wait_all(t)
            t.matmul(L.psA[0:E, 0:2], L.acnt[:, :],
                     L.maskc[:, 2 * b:2 * b + 2], start=True, stop=True)
        self.pe(mm0)

        steps_c0 = [
            lambda v: v.tensor_copy(L.offc[:, 0:1], L.psA[0:E, 0:1]),
            lambda v: v.tensor_copy(L.mycc[:, 0:1], L.psA[0:E, 1:2]),
            lambda v: v.tensor_scalar(L.mycc[:, 2:3], L.mycc[:, 0:1], 127.0,
                                      None, ALU.add),
            lambda v: v.tensor_copy(L.mycu[:, 0:1], L.mycc[:, 2:3]),
            lambda v: v.tensor_scalar(L.mycu[:, 0:1], L.mycu[:, 0:1], 7, None,
                                      ALU.logical_shift_right),
            lambda v: v.tensor_scalar(L.mycu[:, 0:1], L.mycu[:, 0:1], 7, None,
                                      ALU.logical_shift_left),
            lambda v: v.tensor_copy(L.mycc[:, 1:2], L.mycu[:, 0:1]),
        ]
        first_c0 = [True]
        for st_ in steps_c0:
            def item(v, st_=st_, first_c0=first_c0):
                if first_c0[0]:
                    L.wpe(v)
                    first_c0[0] = False
                else:
                    L.wdv(v)
                st_(v)
            self.dv(item)

        def mm1(t):
            L.wdv(t)
            t.matmul(L.psD[0:1, 0:E], L.mycc[0:E, 1:2], L.lones[:, :],
                     start=True, stop=True)
        self.pe(mm1)

        def cp1(v):
            L.wpe(v)
            v.tensor_copy(L.base_r[0:1, :], L.psD[0:1, 0:E])
        self.dv(cp1)

        def tp1(t):
            L.wdv(t)
            t.transpose(L.psD[0:E, 0:1], L.base_r[0:1, 0:E], L.onesrf[0:1, 0:1])
        self.pe(tp1)

        def cp2a(v):
            L.wpe(v)
            v.tensor_copy(L.mycc[:, 3:4], L.psD[0:E, 0:1])
        self.dv(cp2a)

        def cp2a2(v):
            L.wdv(v)
            v.tensor_tensor(L.offc[:, 1:2], L.mycc[:, 3:4], L.mycc[:, 1:2],
                            ALU.add)
        self.dv(cp2a2)

        def cp2(v):
            L.wdv(v)
            v.tensor_scalar(L.cmpA[:, :], L.iotaN[:, :], L.mycc[:, 3:4], None,
                            ALU.is_ge)
            v.tensor_scalar(L.cmpB[:, :], L.iotaN[:, :], L.offc[:, 1:2], None,
                            ALU.is_ge)
        self.dv(cp2)

        def cp2b(v):
            L.wdv(v)
            v.tensor_tensor(L.cmpB[:, :], L.cmpA[:, :], L.cmpB[:, :],
                            ALU.subtract)
        self.dv(cp2b)

        def mm2(t):
            L.wdv(t)
            t.matmul(L.psD[0:1, 0:NCH], L.offc[0:E, 0:1], L.cmpB[:, :],
                     start=True, stop=True)
        self.pe(mm2)

        def cp3(v):
            L.wpe(v)
            v.tensor_copy(L.rows[0:1, NCH:2 * NCH], L.psD[0:1, 0:NCH])
            v.tensor_tensor(L.cmpA[:, :], L.cmpA[:, :], L.cmpB[:, :],
                            ALU.subtract)
        self.dv(cp3)

        def mm3(t):
            L.wdv(t)
            t.matmul(L.psD[0:1, 0:NCH], L.mycc[0:E, 1:2], L.cmpA[:, :],
                     start=True, stop=True)
        self.pe(mm3)

        steps_c4 = [
            lambda v: v.tensor_copy(L.rows[0:1, 0:NCH], L.psD[0:1, 0:NCH]),
            lambda v: v.tensor_tensor(L.rows[0:1, 2 * NCH:3 * NCH],
                                      L.rows[0:1, 0:NCH],
                                      L.rows[0:1, NCH:2 * NCH], ALU.subtract),
            lambda v: v.tensor_scalar(L.rows[0:1, 2 * NCH:3 * NCH],
                                      L.rows[0:1, 2 * NCH:3 * NCH],
                                      float(c.capacity), None, ALU.add),
            lambda v: v.tensor_tensor(L.rows[0:1, 2 * NCH:3 * NCH],
                                      L.rows[0:1, 2 * NCH:3 * NCH],
                                      L.iotaN[0:1, 0:NCH], ALU.subtract),
        ]
        first_c4 = [True]
        for st_ in steps_c4:
            def item(v, st_=st_, first_c4=first_c4):
                if first_c4[0]:
                    L.wpe(v)
                    first_c4[0] = False
                else:
                    L.wdv(v)
                st_(v)
            self.dv(item)
        self.row_bcast(L.rows[0:1, 2 * NCH:3 * NCH], NCH, L.thrb[:, :])

        def ld_gt(g, b=b):
            g.dma_start(out=L.gidx[:, :], in_=L.gtbl_in[b]) \
             .then_inc(L.s_gd.inc(16), 16)
        self.gp(ld_gt)
        self.barrier()

        trash = float((NB - b) * S)
        steps_kp = [
            lambda v: v.tensor_scalar(L.keptm[:, :], L.thrb[:, :],
                                      L.iotaP[:, 0:1], None, ALU.is_gt),
            lambda v: v.tensor_copy(L.gidxf[:, :], L.gidx[:, :]),
            lambda v: v.tensor_scalar(L.gidxf[:, :], L.gidxf[:, :],
                                      -float(c.SENT), None, ALU.add),
            lambda v: v.tensor_tensor(L.gidxf[:, :], L.keptm[:, :],
                                      L.gidxf[:, :], ALU.mult),
            lambda v: v.tensor_scalar(L.gidxf[:, :], L.gidxf[:, :],
                                      float(c.SENT), None, ALU.add),
            lambda v: v.tensor_scalar(L.gidxf[:, :], L.gidxf[:, :],
                                      trash, None, ALU.min),
            lambda v: v.tensor_copy(L.sidx[:, :], L.gidxf[:, :]),
        ]
        for st_ in steps_kp:
            self.dv(lambda v, st_=st_: (L.wdv(v), st_(v)))

        for tt in range(TT):
            def cpy(g, tt=tt, b=b):
                L.wgd(g)
                g.dma_start(out=L.yt_sb[:, :],
                            in_=L.hid_in[b, tt * 128:(tt + 1) * 128, :]) \
                 .then_inc(L.s_gd.inc(16), 16)
                g.wait_ge(L.s_gd.h, L.s_gd.v)
                g.dma_start(out=L.y_hbm[b * S + tt * 128:b * S + (tt + 1) * 128, :],
                            in_=L.yt_sb[:, :]).then_inc(L.s_gd.inc(16), 16)
            self.gp(cpy)
        self.barrier()

        for ch in range(NCH):
            def sc(g, ch=ch, b=b):
                L.wgd(g)
                g.dma_start(out=L.eo_sb[:, :],
                            in_=L.eout_in[b, ch * 128:(ch + 1) * 128, :]) \
                 .then_inc(L.s_gd.inc(16), 16)
                g.wait_ge(L.s_gd.h, L.s_gd.v)
                g.indirect_dma_start(
                    out=L.y_hbm,
                    out_offset=IOA(L.sidx[:, ch:ch + 1], 0),
                    in_=L.eo_sb[:, :],
                    in_offset=None,
                    element_offset=b * S * D,
                ).then_inc(L.s_gd.inc(16), 16)
            self.gp(sc)
        self.barrier()

        def ld3(g, b=b):
            g.dma_start(out=L.mu3[0:1, :], in_=L.ln3_in[b, 0:1, :]) \
             .then_inc(L.s_gd.inc(16), 16)
            g.dma_start(out=L.rs3[0:1, :], in_=L.ln3_in[b, 1:2, :]) \
             .then_inc(L.s_gd.inc(16), 16)
            g.dma_start(out=L.thrb[:, 0:TT], in_=L.probs_in[b]) \
             .then_inc(L.s_gd.inc(16), 16)
        self.gp(ld3)
        self.barrier()

        for tt in range(TT):
            def ldy(g, tt=tt, b=b):
                L.wpe(g)
                g.dma_start(out=L.yt_sb[:, :],
                            in_=L.y_hbm[b * S + tt * 128:b * S + (tt + 1) * 128, :]) \
                 .then_inc(L.s_gd.inc(16), 16)
            self.gp(ldy)

            def sc_y(v, tt=tt):
                L.wgd(v)
                v.tensor_scalar(L.yt_sb[:, :], L.yt_sb[:, :],
                                L.thrb[:, tt:tt + 1], None, ALU.mult)
            self.dv(sc_y)

            for dc in range(DC):
                def tp(t, tt=tt, dc=dc):
                    L.wdv(t)
                    t.transpose(L.tp_ps[:, 0:128],
                                L.yt_sb[:, dc * 128:(dc + 1) * 128],
                                L.idb[:, :])
                self.pe(tp)

                def cp(v, tt=tt, dc=dc):
                    L.wpe(v)
                    v.tensor_copy(L.xb_sb[:, dc * S + tt * 128:dc * S + (tt + 1) * 128],
                                  L.tp_ps[:, 0:128])
                self.dv(cp)
        self.barrier()
        for j in range(c.QC):
            c0 = j * 512

            def ldx1(sy_, j=j, c0=c0, b=b):
                L.wdv(sy_)
                for dc in range(DC):
                    sy_.dma_start(out=L.x1c[:, dc * 512:(dc + 1) * 512],
                                  in_=L.x1_in[b, dc, :, c0:c0 + 512]) \
                       .then_inc(L.s_hd.inc(16), 16)
            self.sy(ldx1)
            self.row_bcast(L.rs3[0:1, c0:c0 + 512], 512, L.zbb[:, 0:512])
            self.row_bcast(L.mu3[0:1, c0:c0 + 512], 512, L.zrow[:, 0:512])

            def x2c(v, j=j, c0=c0):
                L.wdv(v)
                L.whd(v)
                for dc in range(DC):
                    sl = slice(dc * S + c0, dc * S + c0 + 512)
                    v.tensor_tensor(L.mvb[:, 0:512], L.x1c[:, dc * 512:(dc + 1) * 512],
                                    L.zrow[:, 0:512], ALU.subtract)
                    v.tensor_tensor(L.mvb[:, 512:1024], L.mvb[:, 0:512],
                                    L.zbb[:, 0:512], ALU.mult)
                    v.tensor_tensor(L.x2_sb[:, sl], L.mvb[:, 512:1024],
                                    L.xb_sb[:, sl], ALU.add)
            self.dv(x2c)
        self.barrier()
        self.stats(lambda dc: L.x2_sb[:, dc * S:(dc + 1) * S], S, L.mu4, L.rs4)
        self.barrier()
        for j in range(c.QC):
            c0 = j * 512
            self.row_bcast(L.rs4[0:1, c0:c0 + 512], 512, L.zbb[:, 0:512])
            self.row_bcast(L.mu4[0:1, c0:c0 + 512], 512, L.zrow[:, 0:512])

            def f4(v, j=j, c0=c0):
                L.wdv(v)
                for dc in range(DC):
                    sl = slice(dc * S + c0, dc * S + c0 + 512)
                    v.tensor_tensor(L.mvb[:, 0:512], L.x2_sb[:, sl],
                                    L.zrow[:, 0:512], ALU.subtract)
                    v.tensor_tensor(L.mvb[:, 512:1024], L.mvb[:, 0:512],
                                    L.zbb[:, 0:512], ALU.mult)
                    v.tensor_scalar(L.xb_sb[:, sl], L.mvb[:, 512:1024],
                                    L.gam[:, dc:dc + 1], L.bet[:, dc:dc + 1],
                                    ALU.mult, ALU.add)
            self.dv(f4)
        self.barrier()

        self.stats(lambda dc: L.xb_sb[:, dc * S:(dc + 1) * S], S, L.mu4, L.rs4,
                   src_bf16=True)
        self.barrier()
        for j in range(c.QC):
            c0 = j * 512

            def ldx1(sy_, j=j, c0=c0, b=b):
                L.whd(sy_)
                L.wdv(sy_)
                for dc in range(DC):
                    sy_.dma_start(out=L.x1c[:, dc * 512:(dc + 1) * 512],
                                  in_=L.x1_in[b, dc, :, c0:c0 + 512]) \
                       .then_inc(L.s_hd.inc(16), 16)
            self.sy(ldx1)
            self.row_bcast(L.rs4[0:1, c0:c0 + 512], 512, L.zbb[:, 0:512])
            self.row_bcast(L.mu4[0:1, c0:c0 + 512], 512, L.zrow[:, 0:512])

            def oc_(v, j=j, c0=c0):
                L.wdv(v)
                L.whd(v)
                for dc in range(DC):
                    sl = slice(dc * S + c0, dc * S + c0 + 512)
                    v.tensor_tensor(L.mvb[:, 0:512], L.xb_sb[:, sl],
                                    L.zrow[:, 0:512], ALU.subtract)
                    v.tensor_tensor(L.mvb[:, 512:1024], L.mvb[:, 0:512],
                                    L.zbb[:, 0:512], ALU.mult)
                    v.tensor_tensor(L.x1c[:, dc * 512:(dc + 1) * 512],
                                    L.x1c[:, dc * 512:(dc + 1) * 512],
                                    L.mvb[:, 512:1024], ALU.add)
            self.dv(oc_)

            def spo(sy_, j=j, c0=c0, b=b):
                L.wdv(sy_)
                for dc in range(DC):
                    sy_.dma_start(out=L.out[b, dc, :, c0:c0 + 512],
                                  in_=L.x1c[:, dc * 512:(dc + 1) * 512]) \
                       .then_inc(L.s_hd.inc(16), 16)
            self.sy(spo)
        self.sy(lambda e: e.wait_ge(L.s_hd.h, L.s_hd.v))
        self.barrier()


def build_launch2(cfg):
    return L2(cfg).build()



def prep_inputs(cfg, inputs):
    E, D, H, NB, S, DC = cfg.E, cfg.D, cfg.H, cfg.NB, cfg.S, cfg.DC
    HP = H // 2
    x = np.ascontiguousarray(inputs["L0_attn_x"])
    qkv_w = np.asarray(inputs["L0_attn_qkv_w"])
    common = {}
    common["gtok"] = np.ascontiguousarray(
        np.asarray(inputs["L0_attn_global_token"]).reshape(D).reshape(DC, 128))
    common["qkv_wT"] = np.ascontiguousarray(qkv_w.T).reshape(DC, 128, 3 * D)
    common["proj_wT"] = np.ascontiguousarray(
        np.asarray(inputs["L0_attn_proj_w"]).T).reshape(DC, 128, D)
    common["proj_bT"] = np.ascontiguousarray(
        np.asarray(inputs["L0_attn_proj_b"]).reshape(DC, 128).T)
    Wq = np.asarray(inputs["L0_ffn_Wq"]).reshape(H * 64, D)
    Wk = np.asarray(inputs["L0_ffn_Wk"]).reshape(H * 64, D)
    Wv = np.asarray(inputs["L0_ffn_Wv"]).reshape(H * 64, D)
    Wqkv = np.concatenate([Wq, Wk, Wv], 0)
    common["WqkvT"] = np.ascontiguousarray(Wqkv.T).reshape(DC, 128, 3 * D)
    common["Wv_nat"] = np.ascontiguousarray(Wv)
    common["WswT"] = np.ascontiguousarray(np.asarray(inputs["L0_ffn_Wsw"]).T)
    common["bswr"] = np.ascontiguousarray(
        np.asarray(inputs["L0_ffn_bsw"]).reshape(1, E))
    bq = np.asarray(inputs["L0_ffn_bq"]).reshape(HP, 128).T
    bk = np.asarray(inputs["L0_ffn_bk"]).reshape(HP, 128).T
    bv = np.asarray(inputs["L0_ffn_bv"]).reshape(HP, 128).T
    common["bqkvT"] = np.ascontiguousarray(np.concatenate([bq, bk, bv], 1))
    common["bvr"] = np.ascontiguousarray(np.asarray(inputs["L0_ffn_bv"]).reshape(1, H * 64))
    common["WeT"] = np.ascontiguousarray(
        np.transpose(np.asarray(inputs["L0_ffn_We"]), (0, 2, 1))).reshape(
            E, DC, 128, D)
    common["ber"] = np.ascontiguousarray(
        np.asarray(inputs["L0_ffn_be"]).reshape(1, E * D))
    common["identf"] = np.eye(128, dtype=np.float32)
    ki = np.arange(128)[:, None]
    qi = np.arange(128)[None, :]
    common["amask0"] = (((ki + 1) // 64) <= ((qi + 1) // 64)).astype(np.float32)
    common["amask1"] = (((ki + 129) // 64) <= ((qi + 1) // 64)).astype(np.float32)
    common["fmask0"] = (ki <= qi).astype(np.float32)
    suf = np.zeros((128, 129), np.float32)
    suf[:, :128] = (np.arange(128)[:, None] > np.arange(128)[None, :])
    suf[:, 128] = 1.0
    common["sufones"] = suf
    common["lonesr"] = (np.arange(E)[:, None] < np.arange(E)[None, :]).astype(
        np.float32)

    maps = []
    for ci in range(cfg.n_cores):
        m = dict(common)
        xs = np.asarray(x[ci * NB:(ci + 1) * NB])
        m["xT"] = np.ascontiguousarray(xs.transpose(0, 2, 1)).reshape(
            NB, DC, 128, S)
        maps.append(m)
    return maps


def kernel(**inputs):
    cfg = FULL
    return _run(cfg, inputs)


def _run(cfg, inputs):
    maps1 = prep_inputs(cfg, inputs)
    nc1 = build_launch1(cfg, upto="all")
    res1 = run_bass_kernel_spmd(nc1, maps1, core_ids=list(range(cfg.n_cores)))

    maps2 = prep_launch2(cfg, inputs, res1.results)
    nc2 = build_launch2(cfg)
    res2 = run_bass_kernel_spmd(nc2, maps2, core_ids=list(range(cfg.n_cores)))

    outs = []
    for ci in range(cfg.n_cores):
        o = res2.results[ci]["out"].reshape(cfg.NB, cfg.D, cfg.S)
        outs.append(np.ascontiguousarray(o.transpose(0, 2, 1)))
    return np.concatenate(outs, 0).astype(np.float32)


def prep_launch2(cfg, inputs, res1):
    NB, E, D, DC = cfg.NB, cfg.E, cfg.D, cfg.DC
    NBG = cfg.n_cores * NB
    allcnt = np.concatenate(
        [np.asarray(r["cnt_out"]).reshape(NB, E) for r in res1], 0).astype(
            np.float32)
    lones = (np.arange(E)[:, None] < np.arange(E)[None, :]).astype(np.float32)
    gammaT = np.ascontiguousarray(
        np.asarray(inputs["L0_ffn_gamma"]).reshape(DC, 128).T)
    betaT = np.ascontiguousarray(
        np.asarray(inputs["L0_ffn_beta"]).reshape(DC, 128).T)
    identf = np.eye(128, dtype=np.float32)
    maps = []
    for ci in range(cfg.n_cores):
        m = {}
        r = res1[ci]
        m["x1_in"] = np.asarray(r["x1_out"])
        m["hid_in"] = np.asarray(r["hid_out"])
        m["eout_in"] = np.asarray(r["eout_out"])
        m["gtbl_in"] = np.asarray(r["gtbl_out"])
        m["probs_in"] = np.asarray(r["probs_out"])
        m["ln3_in"] = np.asarray(r["ln3_out"])
        m["allcnt"] = allcnt
        om = np.zeros((NBG, 2 * NB), np.float32)
        for b in range(NB):
            g = ci * NB + b
            om[:g, 2 * b] = 1.0
            om[g, 2 * b + 1] = 1.0
        m["offmask"] = om
        m["lonesr"] = lones
        m["gammaT"] = gammaT
        m["betaT"] = betaT
        m["identf2"] = identf
        maps.append(m)
    return maps


# version 02
# speedup vs baseline: 1.2819x; 1.2819x over previous
import contextlib
import math
import numpy as np

import concourse.bass as bass
import concourse.mybir as mybir
from concourse.bass import IndirectOffsetOnAxis as IOA
from concourse.bass_utils import run_bass_kernel_spmd
from concourse.bass import create_sync_update

F32 = mybir.dt.float32
BF16 = mybir.dt.bfloat16
U32 = mybir.dt.uint32
AF = mybir.ActivationFunctionType
ALU = mybir.AluOpType
AX = mybir.AxisListType.X


class CFG:
    def __init__(self, S=2048, D=1024, H=16, E=8, NB=2, n_cores=8,
                 cap_factor=1.2):
        self.S = S
        self.D = D
        self.H = H
        self.DH = 64
        self.E = E
        self.NB = NB
        self.n_cores = n_cores
        self.DC = D // 128
        self.TT = S // 128
        self.QCW = 512
        self.QC = S // self.QCW
        self.N = S + 1
        self.capacity = int(cap_factor * n_cores * NB * S / E)
        self.NCH = (S + E * 127 + 127) // 128
        self.NSLOT = self.NCH * 128
        self.SENT = 1 << 20
        self.GA = 4 if H >= 4 else 2
        self.WS = max(3 * self.GA * 64, D)
        self.WS2 = 3 * self.GA * 64


FULL = CFG()


class Sem:
    def __init__(self, es, nc, name):
        self.h = es.enter_context(nc.semaphore(name))
        self.v = 0

    def inc(self, n=1):
        self.v += n
        return self.h


def ceil_div(a, b):
    return (a + b - 1) // b



class L1:
    def __init__(self, cfg: CFG, upto="all"):
        self.cfg = cfg
        self.upto = upto
        nc = bass.Bass("TRN2", target_bir_lowering=False)
        nc.detect_race_conditions = False
        self.nc = nc
        c = cfg
        S, D, H, E, NB = c.S, c.D, c.H, c.E, c.NB
        DC, NCH, NSLOT = c.DC, c.NCH, c.NSLOT
        HP = H // 2

        dt = nc.dram_tensor
        self.xT = dt("xT", [NB, DC, 128, S], F32, kind="ExternalInput").ap()
        self.gtok = dt("gtok", [DC, 128], F32, kind="ExternalInput").ap()
        self.qkv_wT = dt("qkv_wT", [DC, 128, 3 * D], F32, kind="ExternalInput").ap()
        self.proj_wT = dt("proj_wT", [DC, 128, D], F32, kind="ExternalInput").ap()
        self.proj_bT = dt("proj_bT", [128, DC], F32, kind="ExternalInput").ap()
        self.WqkvT = dt("WqkvT", [DC, 128, 3 * D], F32, kind="ExternalInput").ap()
        self.Wv_nat = dt("Wv_nat", [H * 64, D], F32, kind="ExternalInput").ap()
        self.WswT = dt("WswT", [D, E], F32, kind="ExternalInput").ap()
        self.bswr = dt("bswr", [1, E], F32, kind="ExternalInput").ap()
        self.bqkvT = dt("bqkvT", [128, 3 * HP], F32, kind="ExternalInput").ap()
        self.bvr = dt("bvr", [1, H * 64], F32, kind="ExternalInput").ap()
        self.WeT = dt("WeT", [E, DC, 128, D], F32, kind="ExternalInput").ap()
        self.ber = dt("ber", [1, E * D], F32, kind="ExternalInput").ap()
        self.identf = dt("identf", [128, 128], F32, kind="ExternalInput").ap()
        self.amask0 = dt("amask0", [128, 128], F32, kind="ExternalInput").ap()
        self.amask1 = dt("amask1", [128, 128], F32, kind="ExternalInput").ap()
        self.fmask0 = dt("fmask0", [128, 128], F32, kind="ExternalInput").ap()
        self.sufones = dt("sufones", [128, 129], F32, kind="ExternalInput").ap()
        self.lonesr = dt("lonesr", [E, E], F32, kind="ExternalInput").ap()

        self.x1_out = dt("x1_out", [NB, DC, 128, S], F32, kind="ExternalOutput").ap()
        self.hid_out = dt("hid_out", [NB, S, D], BF16, kind="ExternalOutput").ap()
        self.eout_out = dt("eout_out", [NB, NSLOT, D], BF16, kind="ExternalOutput").ap()
        self.gtbl_out = dt("gtbl_out", [NB, 128, NCH], U32, kind="ExternalOutput").ap()
        self.cnt_out = dt("cnt_out", [NB, E, 1], F32, kind="ExternalOutput").ap()
        self.probs_out = dt("probs_out", [NB, 128, c.TT], F32, kind="ExternalOutput").ap()
        self.ln3_out = dt("ln3_out", [NB, 2, S], F32, kind="ExternalOutput").ap()

        self.ao_hbm = dt("ao_hbm", [NB, DC, 128, S], BF16).ap()
        self.e3_hbm = dt("e3_hbm", [NB, DC, 128, S], BF16).ap()

        es = contextlib.ExitStack()
        self.es = es

        def sbuf(name, p, cc, dty):
            return es.enter_context(nc.sbuf_tensor(name, [p, cc], dty))
        self.sbuf = sbuf

        self.idf = sbuf("idf", 128, 128, F32)
        self.idb = sbuf("idb", 128, 128, BF16)
        self.am0 = sbuf("am0", 128, 128, BF16)
        self.am1 = sbuf("am1", 128, 128, BF16)
        self.fm0 = sbuf("fm0", 128, 128, BF16)
        self.sfo = sbuf("sfo", 128, 129, BF16)
        self.lones = sbuf("lones", E, E, F32)
        self.onesc = sbuf("onesc", 128, 1, BF16)
        self.rowb = sbuf("rowb", 1, 1024, F32)
        self.mu1 = sbuf("mu1", 1, S, F32)
        self.rs1 = sbuf("rs1", 1, S, F32)
        self.mu3 = sbuf("mu3", 1, S, F32)
        self.rs3 = sbuf("rs3", 1, S, F32)
        self.bqkv_sb = sbuf("bqkv_sb", 128, 3 * HP, F32)
        self.bsw_sb = sbuf("bsw_sb", 1, E, F32)
        self.bswb = sbuf("bswb", 128, E, F32)
        self.gidx = sbuf("gidx", 128, NB * NCH, U32)
        self.echw = sbuf("echw", 1, NB * NCH, U32)
        self.echb = sbuf("echb", 1, NB * NCH, U32)
        self.ech_f = sbuf("ech_f", 1, NB * NCH, F32)
        self.iotaNCH = sbuf("iotaNCH", E, NCH, F32)

        self.psA = es.enter_context(nc.psum_tensor("psA", [128, 512], F32))
        self.psB = es.enter_context(nc.psum_tensor("psB", [128, 512], F32))
        self.psC = es.enter_context(nc.psum_tensor("psC", [128, 512], F32))
        self.psD = es.enter_context(nc.psum_tensor("psD", [128, 512], F32))
        self.tp_ps = es.enter_context(nc.psum_tensor("tp_ps", [128, 512], BF16))
        self.st_ps = es.enter_context(nc.psum_tensor("st_ps", [1, 1024], F32))

        self.s_hd = Sem(es, nc, "s_hd")
        self.s_gd = Sem(es, nc, "s_gd")
        self.s_pe = Sem(es, nc, "s_pe")
        self.s_dv = Sem(es, nc, "s_dv")
        self.s_ac = Sem(es, nc, "s_ac")
        self.s_gp = Sem(es, nc, "s_gp")
        self.sems = [self.s_hd, self.s_gd, self.s_pe, self.s_dv, self.s_ac,
                     self.s_gp]

        self.block = es.enter_context(nc.Block())
        self.items = {"tensor": [], "vector": [], "scalar": []}
        self._cursnap = None

    def _curv(self, s):
        if self._cursnap is not None:
            return self._cursnap[id(s)]
        return s.v

    def wait_all(self, e):
        for s in self.sems:
            e.wait_ge(s.h, self._curv(s))

    def _defer(self, eng, f, inc):
        snap = {id(s): s.v for s in self.sems}
        s = {"tensor": self.s_pe, "vector": self.s_dv,
             "scalar": self.s_ac}[eng]
        if inc:
            s.v += 1
        self.items[eng].append((f, snap, inc, s))

    def flush_items(self):
        for eng, emit in (("tensor", self.block.tensor),
                          ("vector", self.block.vector),
                          ("scalar", self.block.scalar)):
            lst = self.items[eng]
            if not lst:
                continue
            self.items[eng] = []

            def run(e, lst=lst):
                for f, snap, inc, s in lst:
                    self._cursnap = snap
                    f(e)
                    if inc:
                        e.maybe_drain_then_inc((s.h, 1))
                self._cursnap = None
            emit(run)

    def pe(self, f, inc=True):
        self._defer("tensor", f, inc)

    def dv(self, f, inc=True):
        self._defer("vector", f, inc)

    def ac(self, f, inc=True):
        self._defer("scalar", f, inc)

    def gp(self, f, inc=False):
        s = self.s_gp

        def g(e):
            f(e)
            if inc:
                e.maybe_drain_then_inc((s.h, 1))
                s.inc()
        self.block.gpsimd(g)

    def sy(self, f):
        self.block.sync(f)

    def barrier(self):
        self.pe(lambda t: None, inc=True)
        self.dv(lambda v: None, inc=True)
        self.ac(lambda e: None, inc=True)
        self.gp(lambda e: None, inc=True)
        for emit in (self.pe, self.dv, self.ac, self.gp):
            emit(lambda e: self.wait_all(e), inc=False)
        self.sy(lambda e: self.wait_all(e))

    def wpe(self, e):
        e.wait_ge(self.s_pe.h, self._curv(self.s_pe))

    def wdv(self, e):
        e.wait_ge(self.s_dv.h, self._curv(self.s_dv))

    def wac(self, e):
        e.wait_ge(self.s_ac.h, self._curv(self.s_ac))

    def wgd(self, e):
        e.wait_ge(self.s_gd.h, self._curv(self.s_gd))

    def whd(self, e):
        e.wait_ge(self.s_hd.h, self._curv(self.s_hd))

    def wgp(self, e):
        e.wait_ge(self.s_gp.h, self._curv(self.s_gp))

    def row_bcast(self, row_ap, w, out_ap):
        def mm(t):
            self.wdv(t)
            t.matmul(self.psD[:, 0:w], self.onesrf[0:1, 0:128], row_ap,
                     start=True, stop=True)
        self.pe(mm)

        def cp(v):
            self.wpe(v)
            v.tensor_copy(out_ap, self.psD[:, 0:w])
        self.dv(cp)

    def stats(self, src_fn, n_tok, mu_row, rs_row, src_bf16=False, wf=None):
        c = self.cfg
        for j in range(ceil_div(n_tok, 512)):
            w = min(512, n_tok - j * 512)
            c0 = j * 512

            if not src_bf16:
                def cpb(v, w=w, c0=c0):
                    self.wpe(v)
                    if wf is not None:
                        wf(v)
                    for dc in range(c.DC):
                        v.tensor_copy(self.stg_b[:, dc * 512:dc * 512 + w],
                                      src_fn(dc)[:, c0:c0 + w])
                self.dv(cpb)

            def sqx(e, w=w, c0=c0):
                self.wpe(e)
                self.wdv(e)
                if wf is not None:
                    wf(e)
                for dc in range(c.DC):
                    e.activation(self.sqc[:, dc * 512:dc * 512 + w],
                                 src_fn(dc)[:, c0:c0 + w], AF.Square)
            self.ac(sqx)

            def mm(t, w=w, c0=c0):
                self.wdv(t)
                self.wac(t)
                for dc in range(c.DC):
                    sap = (src_fn(dc)[:, c0:c0 + w] if src_bf16
                           else self.stg_b[:, dc * 512:dc * 512 + w])
                    t.matmul(self.st_ps[0:1, 0:w], self.onesc[:, 0:1],
                             sap, start=(dc == 0), stop=(dc == c.DC - 1))
                for dc in range(c.DC):
                    t.matmul(self.st_ps[0:1, 512:512 + w], self.onesc[:, 0:1],
                             self.sqc[:, dc * 512:dc * 512 + w],
                             start=(dc == 0), stop=(dc == c.DC - 1))
            self.pe(mm)

            def fin(v, w=w, c0=c0):
                self.wpe(v)
                v.tensor_scalar(mu_row[0:1, c0:c0 + w], self.st_ps[0:1, 0:w],
                                1.0 / c.D, None, ALU.mult)
                v.tensor_scalar(self.rowb[0:1, 0:w], self.st_ps[0:1, 512:512 + w],
                                1.0 / c.D, None, ALU.mult)
            self.dv(fin)

            def fin_b(v, w=w, c0=c0):
                self.wdv(v)
                v.tensor_tensor(self.rowb[0:1, 512:512 + w],
                                mu_row[0:1, c0:c0 + w], mu_row[0:1, c0:c0 + w],
                                ALU.mult)
            self.dv(fin_b)

            def fin_c(v, w=w, c0=c0):
                self.wdv(v)
                v.tensor_tensor(self.rowb[0:1, 0:w], self.rowb[0:1, 0:w],
                                self.rowb[0:1, 512:512 + w], ALU.subtract)
            self.dv(fin_c)

            def fin_d(v, w=w, c0=c0):
                self.wdv(v)
                v.tensor_scalar(rs_row[0:1, c0:c0 + w], self.rowb[0:1, 0:w],
                                1e-5, None, ALU.add)
            self.dv(fin_d)
        def sq(e, n=n_tok):
            self.wdv(e)
            e.activation(rs_row[0:1, 0:n], rs_row[0:1, 0:n], AF.Sqrt)
        self.ac(sq)

        def rec(v, n=n_tok):
            self.wac(v)
            v.reciprocal(rs_row[0:1, 0:n], rs_row[0:1, 0:n])
        self.dv(rec)

    def build(self):
        self.prologue()
        with contextlib.ExitStack() as ph:
            self.alloc_attn(ph)
            for b in range(self.cfg.NB):
                self.attn_batch(b)
                if self.upto in ("ln1", "grp", "load", "stats"):
                    break
            self.flush_items()
        if self.upto in ("attn", "ln1", "grp", "load", "stats"):
            self.sy(lambda e: self.wait_all(e))
            self.es.close()
            return self.nc
        with contextlib.ExitStack() as ph:
            self.alloc_ffn(ph)
            for b in range(self.cfg.NB):
                self.ffn_batch(b)
                if self.upto in ("e3chk", "grp2"):
                    break
            self.flush_items()
        if self.upto in ("ffn", "e3chk", "grp2"):
            self.sy(lambda e: self.wait_all(e))
            self.es.close()
            return self.nc
        with contextlib.ExitStack() as ph:
            self.alloc_moe(ph)
            for b in range(self.cfg.NB):
                self.moe_batch(b)
            self.flush_items()
        self.sy(lambda e: self.wait_all(e))
        self.es.close()
        return self.nc

    def prologue(self):
        L = self

        def loads(g):
            g.dma_start(out=L.idf[:, :], in_=L.identf).then_inc(L.s_gd.inc(16), 16)
            g.dma_start(out=L.idb[:, :], in_=L.identf).then_inc(L.s_gd.inc(16), 16)
            g.dma_start(out=L.am0[:, :], in_=L.amask0).then_inc(L.s_gd.inc(16), 16)
            g.dma_start(out=L.am1[:, :], in_=L.amask1).then_inc(L.s_gd.inc(16), 16)
            g.dma_start(out=L.fm0[:, :], in_=L.fmask0).then_inc(L.s_gd.inc(16), 16)
            g.dma_start(out=L.sfo[:, :], in_=L.sufones).then_inc(L.s_gd.inc(16), 16)
            g.dma_start(out=L.lones[:, :], in_=L.lonesr).then_inc(L.s_gd.inc(16), 16)
            g.dma_start(out=L.bqkv_sb[:, :], in_=L.bqkvT).then_inc(L.s_gd.inc(16), 16)
            g.dma_start(out=L.bsw_sb[:, :], in_=L.bswr).then_inc(L.s_gd.inc(16), 16)
            g.memset(L.onesc[:, :], 1.0)
        self.gp(loads, inc=True)
        self.onescf = self.sbuf("onescf", 128, 1, F32)
        self.onesrf = self.sbuf("onesrf", 128, 128, F32)
        self.gp(lambda g: g.memset(L.onescf[:, :], 1.0), inc=True)
        self.gp(lambda g: g.memset(L.onesrf[:, :], 1.0), inc=True)
        self.barrier()

        def mm(t):
            t.matmul(L.psA[:, 0:L.cfg.E], L.onesrf[0:1, 0:128], L.bsw_sb[0:1, :],
                     start=True, stop=True)
        self.pe(mm)

        def cp(v):
            L.wpe(v)
            v.tensor_copy(L.bswb[:, :], L.psA[:, 0:L.cfg.E])
        self.dv(cp)
        self.barrier()

    def alloc_attn(self, ph):
        nc, c = self.nc, self.cfg
        en = ph.enter_context
        self.x_sb = en(nc.sbuf_tensor("x_sb", [128, c.DC * c.S], F32))
        self.xb_sb = en(nc.sbuf_tensor("xb_sb", [128, c.DC * c.S], BF16))
        self.wg = en(nc.sbuf_tensor("wg", [128, c.DC * c.WS], BF16))
        self.stg_b = en(nc.sbuf_tensor("stg_b", [128, c.DC * 512], BF16))
        self.sqc = en(nc.sbuf_tensor("sqc", [128, c.DC * 512], BF16))
        self.aoj = en(nc.sbuf_tensor("aoj", [128, c.DC * 512], BF16))
        GA = c.GA
        self.kbuf = en(nc.sbuf_tensor("kbuf", [128, (GA // 2) * c.N + 2], BF16))
        self.vbuf = en(nc.sbuf_tensor("vbuf", [128, (c.TT + 1) * GA * 65], BF16))
        self.qbuf = en(nc.sbuf_tensor("qbuf", [128, (GA // 2) * c.QCW], BF16))
        self.expS = en(nc.sbuf_tensor("expS", [128, c.QCW], BF16))
        self.zrow = en(nc.sbuf_tensor("zrow", [128, 512], F32))
        self.zbb = en(nc.sbuf_tensor("zbb", [128, 512], F32))
        self.mvb = en(nc.sbuf_tensor("mvb", [128, 1024], F32))
        self.gq = en(nc.sbuf_tensor("gq", [128, GA], BF16))
        self.gcol = en(nc.sbuf_tensor("gcol", [128, c.DC], BF16))

    def ln_apply(self, mu_row, rs_row, src_sb, dst_sb, dst_dtype_note=None,
                 residual_sb=None):
        c, L = self.cfg, self
        for j in range(c.QC):
            c0 = j * 512
            self.row_bcast(rs_row[0:1, c0:c0 + 512], 512, L.zbb[:, 0:512])
            self.row_bcast(mu_row[0:1, c0:c0 + 512], 512, L.zrow[:, 0:512])

            def ap(v, c0=c0):
                L.wdv(v)
                for dc in range(c.DC):
                    sl = slice(dc * c.S + c0, dc * c.S + c0 + 512)
                    v.tensor_tensor(L.mvb[:, 0:512], src_sb[:, sl],
                                    L.zrow[:, 0:512], ALU.subtract)
                    if residual_sb is None:
                        v.tensor_tensor(dst_sb[:, sl], L.mvb[:, 0:512],
                                        L.zbb[:, 0:512], ALU.mult)
                    else:
                        v.tensor_tensor(L.mvb[:, 512:1024], L.mvb[:, 0:512],
                                        L.zbb[:, 0:512], ALU.mult)
                        v.tensor_tensor(dst_sb[:, sl], residual_sb[:, sl],
                                        L.mvb[:, 512:1024], ALU.add)
            self.dv(ap)

    def spill_fm(self, sb_src, hbm_dst, b, ncols, engine="gp"):
        c, L = self.cfg, self

        def go(g):
            L.wdv(g)
            for dc in range(c.DC):
                g.dma_start(out=hbm_dst[b, dc, :, 0:ncols],
                            in_=sb_src[:, dc * ncols:(dc + 1) * ncols]) \
                 .then_inc((L.s_gd if engine == "gp" else L.s_hd).inc(16), 16)
        if engine == "gp":
            self.gp(go)
        else:
            self.sy(go)

    def _dbg_spill(self, b):
        L = self
        c = self.cfg
        S, DC = c.S, c.DC

        def spdbg(sy, b=b):
            self.wait_all(sy)
            for dc in range(DC):
                sy.dma_start(out=L.x1_out[b, dc],
                             in_=L.x_sb[:, dc * S:(dc + 1) * S]) \
                  .then_inc(L.s_hd.inc(16), 16)
            sy.wait_ge(L.s_hd.h, L.s_hd.v)
        self.sy(spdbg)

    def attn_batch(self, b):
        c, L = self.cfg, self
        S, D, DC, QC, TT, GA = c.S, c.D, c.DC, c.QC, c.TT, c.GA

        def ldx(sy):
            for dc in range(DC):
                sy.dma_start(out=L.x_sb[:, dc * S:(dc + 1) * S],
                             in_=L.xT[b, dc]).then_inc(L.s_hd.inc(16), 16)
        self.sy(ldx)
        if self.upto == "load":
            self._dbg_spill(b)
            return
        self.stats(lambda dc: L.x_sb[:, dc * S:(dc + 1) * S], S, L.mu1, L.rs1,
                   wf=lambda e: L.whd(e))
        self.barrier()
        if self.upto == "stats":
            self._dbg_spill(b)
            return

        self.ln_apply(L.mu1, L.rs1, L.x_sb, L.xb_sb)

        def gl(g):
            for dc in range(DC):
                g.dma_start(out=L.gcol[:, dc:dc + 1],
                            in_=L.gtok[dc].rearrange("(p o) -> p o", o=1)) \
                 .then_inc(L.s_gd.inc(16), 16)
        self.gp(gl)
        self.barrier()

        if self.upto == "ln1":
            def spdbg(sy, b=b):
                self.wait_all(sy)
                for dc in range(DC):
                    sy.dma_start(out=L.x1_out[b, dc],
                                 in_=L.x_sb[:, dc * S:(dc + 1) * S]) \
                      .then_inc(L.s_hd.inc(16), 16)
            self.sy(spdbg)
            return
        for grp in range(c.H // GA):
            h0 = grp * GA
            self.attn_group(b, h0)
        self.barrier()
        if self.upto == "grp":
            def spdbg(sy, b=b):
                self.wait_all(sy)
                for dc in range(DC):
                    sy.dma_start(out=L.x1_out[b, dc],
                                 in_=L.x_sb[:, dc * S:(dc + 1) * S]) \
                      .then_inc(L.s_hd.inc(16), 16)
            self.sy(spdbg)
            return
        self.attn_tail(b)

    def attn_group(self, b, h0):
        c, L = self.cfg, self
        S, D, DC, QC, TT, GA = c.S, c.D, c.DC, c.QC, c.TT, c.GA
        N = c.N
        WS = c.WS
        G64 = GA * 64

        def ldw(g, h0=h0):
            L.wpe(g)
            for dc in range(DC):
                for ti, off in enumerate((h0 * 64, D + h0 * 64,
                                          2 * D + h0 * 64)):
                    g.dma_start(
                        out=L.wg[:, dc * WS + ti * G64:dc * WS + (ti + 1) * G64],
                        in_=L.qkv_wT[dc][:, off:off + G64]) \
                     .then_inc(L.s_gd.inc(16), 16)
        self.gp(ldw)

        def mmg(t, h0=h0):
            L.wgd(t)
            L.wdv(t)
            for p in range(GA // 2):
                for dc in range(DC):
                    t.matmul(L.psC[:, p:p + 1],
                             L.wg[:, dc * WS + G64 + p * 128:dc * WS + G64 + (p + 1) * 128],
                             L.gcol[:, dc:dc + 1],
                             start=(dc == 0), stop=(dc == DC - 1))
                for dc in range(DC):
                    t.matmul(L.psC[:, GA // 2 + p:GA // 2 + p + 1],
                             L.wg[:, dc * WS + 2 * G64 + p * 128:dc * WS + 2 * G64 + (p + 1) * 128],
                             L.gcol[:, dc:dc + 1],
                             start=(dc == 0), stop=(dc == DC - 1))
        self.pe(mmg)

        def cpgq(v):
            L.wpe(v)
            v.tensor_copy(L.gq[:, 0:GA], L.psC[:, 0:GA])
        self.dv(cpgq)

        for p in range(GA // 2):
            hh = h0 + 2 * p
            for j in range(QC):
                def mm(t, p=p, j=j):
                    L.wdv(t)
                    for dc in range(DC):
                        t.matmul(L.psA[:, 0:512],
                                 L.wg[:, dc * WS + G64 + p * 128:dc * WS + G64 + (p + 1) * 128],
                                 L.xb_sb[:, dc * S + j * 512:dc * S + (j + 1) * 512],
                                 start=(dc == 0), stop=(dc == DC - 1))
                self.pe(mm)

                def cp(v, p=p, j=j):
                    L.wpe(v)
                    v.tensor_copy(L.kbuf[:, p * N + j * 512:p * N + (j + 1) * 512],
                                  L.psA[:, 0:512])
                self.dv(cp)
            def cpg(v, p=p):
                v.tensor_copy(L.kbuf[:, p * N + S:p * N + S + 1],
                              L.gq[:, p:p + 1])
            self.dv(cpg)

        for tt in range(TT):
            def mm(t, tt=tt, h0=h0):
                L.wdv(t)
                for dc in range(DC):
                    t.matmul(L.psB[:, 0:GA * 64],
                             L.xb_sb[:, dc * S + tt * 128:dc * S + (tt + 1) * 128],
                             L.wg[:, dc * WS + 2 * G64:dc * WS + 3 * G64],
                             start=(dc == 0), stop=(dc == DC - 1))
            self.pe(mm)

            def cp(v, tt=tt):
                L.wpe(v)
                for i in range(GA):
                    base = (tt * GA + i) * 65
                    v.tensor_copy(L.vbuf[:, base:base + 64],
                                  L.psB[:, i * 64:(i + 1) * 64])
                v.memset(L.vbuf[:, tt * GA * 65:(tt + 1) * GA * 65].rearrange(
                    "p (i c) -> p i c", i=GA)[:, :, 64:65].rearrange("p i c -> p (i c)"), 1.0)
            self.dv(cp)
        for i in range(GA):
            p_, sub = divmod(i, 2)
            oc = GA // 2 + p_
            r0 = sub * 64

            def tp(t, i=i, oc=oc, r0=r0):
                L.wdv(t)
                t.transpose(L.tp_ps[0:1, 0:64], L.gq[r0:r0 + 64, oc:oc + 1],
                            L.idb[r0:r0 + 64, r0:r0 + 64])
            self.pe(tp)

            def cp(v, i=i):
                L.wpe(v)
                base = (TT * GA + i) * 65
                v.tensor_copy(L.vbuf[0:1, base:base + 64], L.tp_ps[0:1, 0:64])
                v.memset(L.vbuf[0:1, base + 64:base + 65], 1.0)
            self.dv(cp)
        self.barrier()

        for qc in range(QC):
            for p in range(GA // 2):
                def mm(t, p=p, qc=qc):
                    L.wdv(t)
                    for dc in range(DC):
                        t.matmul(L.psA[:, 0:512],
                                 L.wg[:, dc * WS + p * 128:dc * WS + (p + 1) * 128],
                                 L.xb_sb[:, dc * S + qc * 512:dc * S + (qc + 1) * 512],
                                 start=(dc == 0), stop=(dc == DC - 1))
                self.pe(mm)

                def cp(v, p=p):
                    L.wpe(v)
                    v.tensor_scalar(L.qbuf[:, p * 512:(p + 1) * 512],
                                    L.psA[:, 0:512], 0.125, None, ALU.mult)
                self.dv(cp)

            ktn = min(TT, 4 * qc + 5)
            for i in range(GA):
                p, sub = divmod(i, 2)
                r0 = sub * 64
                first = [True]
                for kt in range(ktn):
                    qlo = max(0, kt - 4 * qc - 1) * 128

                    def sc_mm(t, p=p, r0=r0, kt=kt, qlo=qlo, qc=qc):
                        L.wdv(t)
                        t.matmul(L.psB[:, qlo:512],
                                 L.kbuf[r0:r0 + 64, p * N + kt * 128:p * N + (kt + 1) * 128],
                                 L.qbuf[r0:r0 + 64, p * 512 + qlo:p * 512 + 512],
                                 start=True, stop=True)
                    self.pe(sc_mm)

                    def ex(e, qlo=qlo):
                        L.wpe(e)
                        e.activation(L.expS[:, qlo:512], L.psB[:, qlo:512], AF.Exp)
                    self.ac(ex)

                    def msk(v, kt=kt, qc=qc, qlo=qlo):
                        L.wac(v)
                        qs0 = kt - 4 * qc
                        qs1 = kt - 4 * qc - 1
                        if 0 <= qs0 < 4:
                            v.tensor_tensor(L.expS[:, qs0 * 128:(qs0 + 1) * 128],
                                            L.expS[:, qs0 * 128:(qs0 + 1) * 128],
                                            L.am0[:, :], ALU.mult)
                        if 0 <= qs1 < 4:
                            v.tensor_tensor(L.expS[:, qs1 * 128:(qs1 + 1) * 128],
                                            L.expS[:, qs1 * 128:(qs1 + 1) * 128],
                                            L.am1[:, :], ALU.mult)
                    self.dv(msk)

                    def av(t, i=i, kt=kt, qlo=qlo, first=first):
                        L.wdv(t)
                        t.matmul(L.psC[0:65, qlo:512],
                                 L.vbuf[:, (kt * GA + i) * 65:(kt * GA + i + 1) * 65],
                                 L.expS[:, qlo:512],
                                 start=first[0], stop=False)
                        first[0] = False
                    self.pe(av, inc=False)

                def sc_g(t, p=p, r0=r0):
                    t.matmul(L.psB[0:1, 0:512],
                             L.kbuf[r0:r0 + 64, p * N + S:p * N + S + 1],
                             L.qbuf[r0:r0 + 64, p * 512:p * 512 + 512],
                             start=True, stop=True)
                self.pe(sc_g)

                def ex_g(e):
                    L.wpe(e)
                    e.activation(L.expS[0:1, 0:512], L.psB[0:1, 0:512], AF.Exp)
                self.ac(ex_g)

                def av_g(t, i=i):
                    L.wac(t)
                    t.matmul(L.psC[0:65, 0:512],
                             L.vbuf[0:1, (TT * GA + i) * 65:(TT * GA + i + 1) * 65],
                             L.expS[0:1, 0:512],
                             start=False, stop=True)
                self.pe(av_g)

                def rz(v):
                    L.wpe(v)
                    v.reciprocal(L.zrow[64:65, 0:512], L.psC[64:65, 0:512])
                self.dv(rz)

                def zb(t):
                    L.wdv(t)
                    t.matmul(L.psD[0:64, 0:512], L.onesrf[64:65, 0:64],
                             L.zrow[64:65, 0:512], start=True, stop=True)
                self.pe(zb)

                def ev(v, i=i, qc=qc, h0=h0, b=b):
                    L.wpe(v)
                    L.wgd(v)
                    v.tensor_copy(L.zbb[0:64, 0:512], L.psD[0:64, 0:512])
                    hh = h0 + i
                    r0o = (hh % 2) * 64
                    v.scalar_tensor_tensor(
                        L.aoj[r0o:r0o + 64, i * 512:(i + 1) * 512],
                        L.psC[0:64, 0:512], 1.0, L.zbb[0:64, 0:512],
                        ALU.mult, ALU.mult)
                self.dv(ev)

                def sp(g, i=i, qc=qc, h0=h0, b=b):
                    L.wdv(g)
                    hh = h0 + i
                    dc = hh // 2
                    r0o = (hh % 2) * 64
                    g.dma_start(
                        out=L.ao_hbm[b, dc, r0o:r0o + 64, qc * 512:(qc + 1) * 512],
                        in_=L.aoj[r0o:r0o + 64, i * 512:(i + 1) * 512]) \
                     .then_inc(L.s_gd.inc(16), 16)
                self.gp(sp)
                self.pe(lambda t: L.wdv(t), inc=False)

    def attn_tail(self, b):
        c, L = self.cfg, self
        S, D, DC, QC = c.S, c.D, c.DC, c.QC
        WS = c.WS

        def ldw(g):
            for dc in range(DC):
                g.dma_start(out=L.wg[:, dc * WS:dc * WS + D],
                            in_=L.proj_wT[dc]).then_inc(L.s_gd.inc(16), 16)
            g.dma_start(out=L.zrow[:, 0:DC], in_=L.proj_bT) \
             .then_inc(L.s_gd.inc(16), 16)
        self.gp(ldw)
        self.barrier()

        for j in range(QC):
            def lda(g, j=j, b=b):
                L.wpe(g)
                for dc in range(DC):
                    g.dma_start(out=L.aoj[:, dc * 512:(dc + 1) * 512],
                                in_=L.ao_hbm[b, dc, :, j * 512:(j + 1) * 512]) \
                     .then_inc(L.s_gd.inc(16), 16)
            self.gp(lda)
            for oc in range(DC):
                def mm(t, j=j, oc=oc):
                    L.wdv(t)
                    L.wgd(t)
                    for dc in range(DC):
                        t.matmul(L.psA[:, 0:512],
                                 L.wg[:, dc * WS + oc * 128:dc * WS + (oc + 1) * 128],
                                 L.aoj[:, dc * 512:(dc + 1) * 512],
                                 start=(dc == 0), stop=(dc == DC - 1))
                self.pe(mm)

                def cp(v, j=j, oc=oc):
                    L.wpe(v)
                    v.tensor_scalar(L.xb_sb[:, oc * S + j * 512:oc * S + (j + 1) * 512],
                                    L.psA[:, 0:512], L.zrow[:, oc:oc + 1], None,
                                    ALU.add)
                self.dv(cp)
        self.barrier()

        self.stats(lambda dc: L.xb_sb[:, dc * S:(dc + 1) * S], S, L.mu3, L.rs3,
                   src_bf16=True)
        self.barrier()
        self.ln_apply(L.mu3, L.rs3, L.xb_sb, L.x_sb, residual_sb=L.x_sb)
        self.barrier()
        def spx(sy, b=b):
            for dc in range(DC):
                sy.dma_start(out=L.x1_out[b, dc],
                             in_=L.x_sb[:, dc * S:(dc + 1) * S]) \
                  .then_inc(L.s_hd.inc(16), 16)
        self.sy(spx)

        self.stats(lambda dc: L.x_sb[:, dc * S:(dc + 1) * S], S, L.mu3, L.rs3)
        self.barrier()

        def sp3(g, b=b):
            g.dma_start(out=L.ln3_out[b, 0:1, :], in_=L.mu3[0:1, :]) \
             .then_inc(L.s_gd.inc(16), 16)
            g.dma_start(out=L.ln3_out[b, 1:2, :], in_=L.rs3[0:1, :]) \
             .then_inc(L.s_gd.inc(16), 16)
        self.gp(sp3)

        self.ln_apply(L.mu3, L.rs3, L.x_sb, L.xb_sb)
        self.barrier()
        self.spill_fm(L.xb_sb, L.e3_hbm, b, S)
        self.barrier()

    def alloc_ffn(self, ph):
        nc, c = self.nc, self.cfg
        en = ph.enter_context
        GF = c.GA
        self.GF = GF
        self.e_sb = en(nc.sbuf_tensor("e_sb", [128, c.DC * c.S], BF16))
        self.wg2 = en(nc.sbuf_tensor("wg2", [128, c.DC * c.WS2], BF16))
        self.kbuf = en(nc.sbuf_tensor("kbuf2", [128, (GF // 2) * c.S], BF16))
        self.vue = en(nc.sbuf_tensor("vue", [128, c.TT * GF * 73], BF16))
        self.vsuf = en(nc.sbuf_tensor("vsuf", [128, GF * c.S], BF16))
        self.tsuf = en(nc.sbuf_tensor("tsuf", [128, 2 * GF], F32))
        self.qbuf = en(nc.sbuf_tensor("qbuf2", [128, (GF // 2) * c.QCW], BF16))
        self.expS = en(nc.sbuf_tensor("expS2", [128, c.QCW], BF16))
        self.av_sb = en(nc.sbuf_tensor("av_sb", [128, c.QCW], BF16))
        self.hid_tm = en(nc.sbuf_tensor("hid_tm", [128, c.TT * c.D], BF16))
        self.lacc = en(nc.sbuf_tensor("lacc", [128, c.TT * c.E], F32))
        self.rzcol = en(nc.sbuf_tensor("rzcol", [128, 2], F32))
        self.bvcol = en(nc.sbuf_tensor("bvcol", [128, 1], BF16))
        self.M2 = en(nc.sbuf_tensor("M2", [128, c.DC * c.H * c.E], BF16))
        self.ubias = en(nc.sbuf_tensor("ubias", [c.E, c.H + 2], F32))
        self.bvb = en(nc.sbuf_tensor("bvb", [128, c.H * 64], F32))
        self.bvr_sb = en(nc.sbuf_tensor("bvr_sb", [1, c.H * 64], F32))
        self.zrow = en(nc.sbuf_tensor("zrow2", [128, 512], F32))
        self.mvb = en(nc.sbuf_tensor("mvb2", [128, 512], F32))
        self.iota8 = en(nc.sbuf_tensor("iota8", [128, c.E], F32))
        self.iota8m = en(nc.sbuf_tensor("iota8m", [128, c.E], F32))
        self.iotap = en(nc.sbuf_tensor("iotap", [128, 1], F32))
        self.probs_sb = en(nc.sbuf_tensor("probs_sb", [128, c.TT], F32))
        self.routes_sb = en(nc.sbuf_tensor("routes_sb", [128, c.TT * 8], U32))
        self.rf32_sb = en(nc.sbuf_tensor("rf32_sb", [128, c.TT], F32))
        self.max8 = en(nc.sbuf_tensor("max8", [128, 8], F32))
        self.mi8 = en(nc.sbuf_tensor("mi8", [128, 8], U32))
        self.rscr = en(nc.sbuf_tensor("rscr", [128, 10], F32))
        self.ohf = en(nc.sbuf_tensor("ohf", [128, c.E], F32))
        self.ohb = en(nc.sbuf_tensor("ohb", [128, c.E], BF16))
        self.ohT = en(nc.sbuf_tensor("ohT", [c.E, c.S], F32))
        self.scan = en(nc.sbuf_tensor("scan", [c.E, c.S], F32))
        self.pos_sb = en(nc.sbuf_tensor("pos_sb", [128, c.TT], F32))
        self.cntc = en(nc.sbuf_tensor("cntc", [c.E, 4], F32))
        self.cnt_u = en(nc.sbuf_tensor("cnt_u", [c.E, 1], U32))
        self.base_r = en(nc.sbuf_tensor("base_r", [1, c.E], F32))
        self.baseb = en(nc.sbuf_tensor("baseb", [128, c.E], F32))
        self.slotu = en(nc.sbuf_tensor("slotu", [128, 1], U32))
        self.slotu2 = en(nc.sbuf_tensor("slotu2", [128, 2], U32))
        self.tokid = en(nc.sbuf_tensor("tokid", [128, 1], U32))
        self.sentc = en(nc.sbuf_tensor("sentc", [128, c.NCH], U32))
        L = self

        def pro(g):
            g.iota(L.iota8[:, :], [[1, c.E]], base=0, channel_multiplier=0,
                   allow_small_or_imprecise_dtypes=True)
            g.iota(L.iota8m[:, :], [[1, c.E]], base=-c.E, channel_multiplier=0,
                   allow_small_or_imprecise_dtypes=True)
            g.iota(L.iotap[:, :], [[0, 1]], base=0, channel_multiplier=1,
                   allow_small_or_imprecise_dtypes=True)
            g.iota(L.iotaNCH[:, :], [[128, c.NCH]], base=0, channel_multiplier=0,
                   allow_small_or_imprecise_dtypes=True)
            g.memset(L.sentc[:, :], c.SENT)
            g.memset(L.lacc[:, :], 0.0)
            g.dma_start(out=L.bvr_sb[:, :], in_=L.bvr).then_inc(L.s_gd.inc(16), 16)
            for r in range(c.H * 64 // 128):
                g.dma_start(out=L.hid_tm[:, r * c.D:(r + 1) * c.D],
                            in_=L.Wv_nat[r * 128:(r + 1) * 128, :]) \
                 .then_inc(L.s_gd.inc(16), 16)
            for dc in range(c.DC):
                g.dma_start(out=L.vsuf[:, dc * c.E:(dc + 1) * c.E],
                            in_=L.WswT[dc * 128:(dc + 1) * 128, :]) \
                 .then_inc(L.s_gd.inc(16), 16)
        self.gp(pro, inc=True)
        self.barrier()

        for j in range(ceil_div(c.H * 64, 512)):
            w = min(512, c.H * 64 - j * 512)
            self.row_bcast(L.bvr_sb[0:1, j * 512:j * 512 + w], w,
                           L.bvb[:, j * 512:j * 512 + w])
        for h in range(c.H):
            r, sub = divmod(h, 2)
            rb = sub * 64
            for dc in range(c.DC):
                def mm(t, h=h, r=r, rb=rb, dc=dc):
                    L.wdv(t)
                    t.matmul(L.psA[:, 0:c.E],
                             L.hid_tm[rb:rb + 64, r * c.D + dc * 128:r * c.D + (dc + 1) * 128],
                             L.vsuf[rb:rb + 64, (dc if False else ((h * 64) // 128) * c.E):((h * 64) // 128) * c.E + c.E]
                             if False else L.vsuf[rb:rb + 64, ((h * 64) // 128) * c.E:((h * 64) // 128 + 1) * c.E],
                             start=True, stop=True)
                self.pe(mm)

                def cp(v, h=h, dc=dc):
                    L.wpe(v)
                    v.tensor_copy(L.M2[:, (dc * c.H + h) * c.E:(dc * c.H + h + 1) * c.E],
                                  L.psA[:, 0:c.E])
                self.dv(cp)
            def cpb(v, h=h):
                v.tensor_copy(L.bvcol[:, 0:1],
                              L.bqkv_sb[:, 2 * (c.H // 2) + h // 2:2 * (c.H // 2) + h // 2 + 1])
            self.dv(cpb)

            def mmu(t, h=h, rb=rb):
                L.wdv(t)
                t.matmul(L.psB[0:c.E, 0:1],
                         L.vsuf[rb:rb + 64, ((h * 64) // 128) * c.E:((h * 64) // 128 + 1) * c.E],
                         L.bvcol[rb:rb + 64, 0:1],
                         start=True, stop=True)
            self.pe(mmu)

            def cpu_(v, h=h):
                L.wpe(v)
                v.tensor_copy(L.ubias[:, h:h + 1], L.psB[0:c.E, 0:1])
            self.dv(cpu_)
        self.barrier()
        def us(v):
            v.reduce_sum(L.ubias[:, c.H:c.H + 1], L.ubias[:, 0:c.H], axis=AX)
        self.dv(us)

        def ut(t):
            L.wdv(t)
            t.transpose(L.psB[0:1, 0:c.E], L.ubias[0:c.E, c.H:c.H + 1],
                        L.idf[0:c.E, 0:c.E])
        self.pe(ut)

        def uc(v):
            L.wpe(v)
            v.tensor_copy(L.rowb[0:1, 0:c.E], L.psB[0:1, 0:c.E])
        self.dv(uc)

        def ub(t):
            L.wdv(t)
            t.matmul(L.psB[:, 0:c.E], L.onesrf[0:1, 0:128], L.rowb[0:1, 0:c.E],
                     start=True, stop=True)
        self.pe(ub)

        def ua(v):
            L.wpe(v)
            v.tensor_tensor(L.bswb[:, :], L.bswb[:, :], L.psB[:, 0:c.E], ALU.add)
        self.dv(ua)
        self.barrier()

    def ffn_batch(self, b):
        c, L = self.cfg, self
        S, D, DC, QC, TT, E = c.S, c.D, c.DC, c.QC, c.TT, c.E
        GF = self.GF
        HP2 = c.H // 2

        def lde(g, b=b):
            for dc in range(DC):
                g.dma_start(out=L.e_sb[:, dc * S:(dc + 1) * S],
                            in_=L.e3_hbm[b, dc]).then_inc(L.s_gd.inc(16), 16)
        self.gp(lde)
        self.dv(lambda v: v.memset(L.lacc[:, :], 0.0))
        self.barrier()
        if self.upto == "e3chk":
            def dump(g, b=b):
                for dc in range(DC):
                    g.dma_start(
                        out=L.hid_out[b].rearrange("s d -> (s d)").rearrange(
                            "(c p f) -> c p f", c=DC, p=128)[dc],
                        in_=L.e_sb[:, dc * S:(dc + 1) * S]) \
                     .then_inc(L.s_gd.inc(16), 16)
            self.gp(dump)
            self.barrier()
            return

        for grp in range(c.H // GF):
            self.ffn_group(b, grp * GF)
        self.barrier()
        if self.upto == "grp2":
            self.ffn_route(b)
            return
        self.ffn_route(b)

    def ffn_group(self, b, h0):
        c, L = self.cfg, self
        S, D, DC, QC, TT, E = c.S, c.D, c.DC, c.QC, c.TT, c.E
        GF = self.GF
        HP2 = c.H // 2
        WS2 = c.WS2
        G64 = GF * 64

        def ldw(g, h0=h0):
            L.wpe(g)
            for dc in range(DC):
                for ti, off in enumerate((h0 * 64, D + h0 * 64,
                                          2 * D + h0 * 64)):
                    g.dma_start(
                        out=L.wg2[:, dc * WS2 + ti * G64:dc * WS2 + (ti + 1) * G64],
                        in_=L.WqkvT[dc][:, off:off + G64]) \
                     .then_inc(L.s_gd.inc(16), 16)
        self.gp(ldw)

        for p in range(GF // 2):
            hh = h0 + 2 * p
            bcol = HP2 + hh // 2
            for j in range(QC):
                def mm(t, p=p, j=j):
                    L.wdv(t)
                    L.wgd(t)
                    for dc in range(DC):
                        t.matmul(L.psA[:, 0:512],
                                 L.wg2[:, dc * WS2 + G64 + p * 128:dc * WS2 + G64 + (p + 1) * 128],
                                 L.e_sb[:, dc * S + j * 512:dc * S + (j + 1) * 512],
                                 start=(dc == 0), stop=(dc == DC - 1))
                self.pe(mm)

                def cp(v, p=p, j=j, bcol=bcol):
                    L.wpe(v)
                    v.tensor_scalar(L.kbuf[:, p * S + j * 512:p * S + (j + 1) * 512],
                                    L.psA[:, 0:512],
                                    L.bqkv_sb[:, bcol:bcol + 1], None, ALU.add)
                self.dv(cp)

        for tt in range(TT):
            def mmv(t, tt=tt, h0=h0):
                L.wdv(t)
                for dc in range(DC):
                    t.matmul(L.psB[:, 0:GF * 64],
                             L.e_sb[:, dc * S + tt * 128:dc * S + (tt + 1) * 128],
                             L.wg2[:, dc * WS2 + 2 * G64:dc * WS2 + 3 * G64],
                             start=(dc == 0), stop=(dc == DC - 1))
                for dc in range(DC):
                    t.matmul(L.psC[:, 0:GF * E],
                             L.e_sb[:, dc * S + tt * 128:dc * S + (tt + 1) * 128],
                             L.M2[:, (dc * c.H + h0) * E:(dc * c.H + h0 + GF) * E],
                             start=(dc == 0), stop=(dc == DC - 1))
            self.pe(mmv)

            def cpv(v, tt=tt, h0=h0):
                L.wpe(v)
                for i in range(GF):
                    base = (tt * GF + i) * 73
                    v.tensor_tensor(L.vue[:, base:base + 64],
                                    L.psB[:, i * 64:(i + 1) * 64],
                                    L.bvb[:, (h0 + i) * 64:(h0 + i + 1) * 64],
                                    ALU.add)
                    v.memset(L.vue[:, base + 64:base + 65], 1.0)
                    v.tensor_copy(L.vue[:, base + 65:base + 73],
                                  L.psC[:, i * E:(i + 1) * E])
            self.dv(cpv)
        self.barrier()

        for i in range(GF):
            def z0(v, i=i):
                v.memset(L.tsuf[:, 2 * i:2 * i + 1], 0.0)
            self.dv(z0)
        for tt in range(TT - 1, -1, -1):
            for i in range(GF):
                def mm(t, tt=tt, i=i):
                    L.wdv(t)
                    t.matmul(L.psD[0:73, 0:129],
                             L.vue[:, (tt * GF + i) * 73:(tt * GF + i + 1) * 73],
                             L.sfo[:, :], start=True, stop=True)
                self.pe(mm)

                def fin(v, tt=tt, i=i):
                    L.wpe(v)
                    v.tensor_scalar(L.vsuf[0:73, i * S + tt * 128:i * S + (tt + 1) * 128],
                                    L.psD[0:73, 0:128],
                                    L.tsuf[0:73, 2 * i:2 * i + 1], None, ALU.add)
                self.dv(fin)

                def fint(v, tt=tt, i=i):
                    L.wdv(v)
                    v.tensor_tensor(L.tsuf[0:73, 2 * i:2 * i + 1],
                                    L.tsuf[0:73, 2 * i:2 * i + 1],
                                    L.psD[0:73, 128:129], ALU.add)
                self.dv(fint)
        self.barrier()

        for qc in range(QC):
            for p in range(GF // 2):
                hh = h0 + 2 * p
                bcol = hh // 2

                def mm(t, p=p, qc=qc):
                    L.wdv(t)
                    for dc in range(DC):
                        t.matmul(L.psA[:, 0:512],
                                 L.wg2[:, dc * WS2 + p * 128:dc * WS2 + (p + 1) * 128],
                                 L.e_sb[:, dc * S + qc * 512:dc * S + (qc + 1) * 512],
                                 start=(dc == 0), stop=(dc == DC - 1))
                self.pe(mm)

                def cp(v, p=p, bcol=bcol):
                    L.wpe(v)
                    v.tensor_scalar(L.qbuf[:, p * 512:(p + 1) * 512],
                                    L.psA[:, 0:512],
                                    L.bqkv_sb[:, bcol:bcol + 1],
                                    1.0 / math.sqrt(D), ALU.add, ALU.mult)
                self.dv(cp)

            ktn = min(TT, 4 * qc + 4)
            for i in range(GF):
                p, sub = divmod(i, 2)
                r0 = sub * 64
                first = [True]
                for kt in range(ktn):
                    qlo = max(0, kt - 4 * qc) * 128

                    def smm(t, p=p, r0=r0, kt=kt, qlo=qlo):
                        L.wdv(t)
                        t.matmul(L.psB[:, qlo:512],
                                 L.kbuf[r0:r0 + 64, p * S + kt * 128:p * S + (kt + 1) * 128],
                                 L.qbuf[r0:r0 + 64, p * 512 + qlo:p * 512 + 512],
                                 start=True, stop=True)
                    self.pe(smm)

                    def ex(e, qlo=qlo):
                        L.wpe(e)
                        e.activation(L.expS[:, qlo:512], L.psB[:, qlo:512], AF.Exp)
                    self.ac(ex)

                    def msk(v, kt=kt, qc=qc):
                        L.wac(v)
                        qs0 = kt - 4 * qc
                        if 0 <= qs0 < 4:
                            v.tensor_tensor(L.expS[:, qs0 * 128:(qs0 + 1) * 128],
                                            L.expS[:, qs0 * 128:(qs0 + 1) * 128],
                                            L.fm0[:, :], ALU.mult)
                    self.dv(msk)

                    def av(t, i=i, kt=kt, qlo=qlo, first=first):
                        L.wdv(t)
                        t.matmul(L.psC[0:73, qlo:512],
                                 L.vue[:, (kt * GF + i) * 73:(kt * GF + i + 1) * 73],
                                 L.expS[:, qlo:512],
                                 start=first[0], stop=(kt == ktn - 1))
                        first[0] = False
                    self.pe(av, inc=(kt == ktn - 1))

                def fin(v, i=i, qc=qc):
                    L.wpe(v)
                    v.tensor_tensor(L.av_sb[0:73, 0:512], L.psC[0:73, 0:512],
                                    L.vsuf[0:73, i * S + qc * 512:i * S + (qc + 1) * 512],
                                    ALU.add)
                self.dv(fin)
                for qs in range(4):
                    def tp(t, qs=qs):
                        L.wdv(t)
                        t.transpose(L.tp_ps[:, 0:73],
                                    L.av_sb[0:73, qs * 128:(qs + 1) * 128],
                                    L.idb[0:73, 0:73])
                    self.pe(tp)

                    def rcp(v, qs=qs):
                        L.wpe(v)
                        v.reciprocal(L.rzcol[:, 0:1], L.tp_ps[:, 64:65])
                    self.dv(rcp)

                    def nrm(v, i=i, qc=qc, qs=qs, h0=h0):
                        L.wdv(v)
                        tile = qc * 4 + qs
                        hh = h0 + i
                        v.tensor_scalar(L.hid_tm[:, tile * D + hh * 64:tile * D + hh * 64 + 64],
                                        L.tp_ps[:, 0:64],
                                        L.rzcol[:, 0:1], None, ALU.mult)
                        v.scalar_tensor_tensor(
                            L.lacc[:, tile * E:(tile + 1) * E],
                            L.tp_ps[:, 65:73], L.rzcol[:, 0:1],
                            L.lacc[:, tile * E:(tile + 1) * E],
                            ALU.mult, ALU.add)
                    self.dv(nrm)
                self.pe(lambda t: L.wdv(t), inc=False)

    def ffn_route(self, b):
        c, L = self.cfg, self
        S, D, DC, TT, E, NCH = c.S, c.D, c.DC, c.TT, c.E, c.NCH

        def sph(g, b=b):
            L.wdv(g)
            for tt in range(TT):
                g.dma_start(out=L.hid_out[b, tt * 128:(tt + 1) * 128, :],
                            in_=L.hid_tm[:, tt * D:(tt + 1) * D]) \
                 .then_inc(L.s_gd.inc(16), 16)
        self.gp(sph)

        for tt in range(TT):
            def fin(v, tt=tt):
                v.tensor_tensor(L.lacc[:, tt * E:(tt + 1) * E],
                                L.lacc[:, tt * E:(tt + 1) * E],
                                L.bswb[:, :], ALU.add)
            self.dv(fin)

            def fin_m(v, tt=tt):
                L.wdv(v)
                v.reduce_max(L.rscr[:, 0:1], L.lacc[:, tt * E:(tt + 1) * E], axis=AX)
            self.dv(fin_m)

            def fin2(v, tt=tt):
                L.wdv(v)
                v.tensor_scalar(L.rscr[:, 2:2 + E], L.lacc[:, tt * E:(tt + 1) * E],
                                L.rscr[:, 0:1], None, ALU.subtract)
            self.dv(fin2)

            def ex(e, tt=tt):
                L.wdv(e)
                e.activation(L.rscr[:, 2:2 + E], L.rscr[:, 2:2 + E], AF.Exp,
                             accum_out=L.rscr[:, 1:2])
            self.ac(ex)

            def rcp2(v, tt=tt):
                L.wac(v)
                v.reciprocal(L.probs_sb[:, tt:tt + 1], L.rscr[:, 1:2])
            self.dv(rcp2)

            def rt_a(v, tt=tt):
                L.wdv(v)
                v.tensor_scalar(L.max8[:, :], L.lacc[:, tt * E:(tt + 1) * E],
                                L.rscr[:, 0:1], None, ALU.is_equal)
            self.dv(rt_a)

            def rt_b(v, tt=tt):
                L.wdv(v)
                v.tensor_tensor(L.max8[:, :], L.max8[:, :], L.iota8m[:, :],
                                ALU.mult)
            self.dv(rt_b)

            def rt_c(v, tt=tt):
                L.wdv(v)
                v.tensor_reduce(L.rscr[:, 3:4], L.max8[:, :], AX, ALU.min)
            self.dv(rt_c)

            def rt_d(v, tt=tt):
                L.wdv(v)
                v.tensor_scalar(L.rf32_sb[:, tt:tt + 1], L.rscr[:, 3:4],
                                float(E), None, ALU.add)
            self.dv(rt_d)

            def rt2(v, tt=tt):
                L.wdv(v)
                v.tensor_scalar(L.ohf[:, :], L.iota8[:, :], L.rf32_sb[:, tt:tt + 1],
                                None, ALU.is_equal)
            self.dv(rt2)

            def rt2b(v, tt=tt):
                L.wdv(v)
                v.tensor_copy(L.ohb[:, :], L.ohf[:, :])
            self.dv(rt2b)

            def tp(t, tt=tt):
                L.wdv(t)
                t.transpose(L.tp_ps[0:E, 0:128], L.ohb[:, :], L.idb[:, :])
            self.pe(tp)

            def cpo(v, tt=tt):
                L.wpe(v)
                v.tensor_copy(L.ohT[:, tt * 128:(tt + 1) * 128], L.tp_ps[0:E, 0:128])
            self.dv(cpo)
        self.barrier()

        def sc0(v):
            v.tensor_tensor_scan(L.scan[:, :], L.ohT[:, :], L.ohT[:, :], 0.0,
                                 ALU.add, ALU.bypass)
        self.dv(sc0)

        steps_sc = [
            lambda v: v.tensor_copy(L.cntc[:, 0:1], L.scan[:, S - 1:S]),
            lambda v: v.tensor_scalar(L.cntc[:, 2:3], L.cntc[:, 0:1], 127.0,
                                      None, ALU.add),
            lambda v: v.tensor_copy(L.cnt_u[:, 0:1], L.cntc[:, 2:3]),
            lambda v: v.tensor_scalar(L.cnt_u[:, 0:1], L.cnt_u[:, 0:1], 7,
                                      None, ALU.logical_shift_right),
            lambda v: v.tensor_scalar(L.cnt_u[:, 0:1], L.cnt_u[:, 0:1], 7,
                                      None, ALU.logical_shift_left),
            lambda v: v.tensor_copy(L.cntc[:, 1:2], L.cnt_u[:, 0:1]),
        ]
        for st_ in steps_sc:
            self.dv(lambda v, st_=st_: (L.wdv(v), st_(v)))

        def spc(g, b=b):
            L.wdv(g)
            g.dma_start(out=L.cnt_out[b], in_=L.cntc[0:c.E, 0:1]) \
             .then_inc(L.s_gd.inc(16), 16)
            g.dma_start(out=L.probs_out[b], in_=L.probs_sb[:, :]) \
             .then_inc(L.s_gd.inc(16), 16)
        self.gp(spc)

        def bmm(t):
            L.wdv(t)
            t.matmul(L.psD[0:1, 0:E], L.cntc[0:E, 1:2], L.lones[:, :],
                     start=True, stop=True)
        self.pe(bmm)

        def bcp(v):
            L.wpe(v)
            v.tensor_copy(L.base_r[0:1, :], L.psD[0:1, 0:E])
        self.dv(bcp)
        self.row_bcast(L.base_r[0:1, 0:E], E, L.baseb[:, 0:E])
        self.barrier()

        def pf(g, b=b):
            g.dma_start(out=L.gtbl_out[b], in_=L.sentc[:, 0:NCH]) \
             .then_inc(L.s_gd.inc(16), 16)
        self.gp(pf)

        for tt in range(TT):
            def prod(v, tt=tt):
                L.wgd(v)
                v.tensor_tensor(L.rscr[:, 0:0 + E] if False else L.ohT[:, tt * 128:(tt + 1) * 128],
                                L.ohT[:, tt * 128:(tt + 1) * 128],
                                L.scan[:, tt * 128:(tt + 1) * 128], ALU.mult)
            self.dv(prod)

            def tpp(t, tt=tt):
                L.wdv(t)
                t.transpose(L.psD[:, 0:E], L.ohT[0:E, tt * 128:(tt + 1) * 128],
                            L.idf[0:E, 0:E])
            self.pe(tpp)

            def mkslot0(v, tt=tt):
                L.wpe(v)
                v.reduce_sum(L.pos_sb[:, tt:tt + 1], L.psD[:, 0:E], axis=AX)
            self.dv(mkslot0)

            steps_mk = [
                lambda v, tt=tt: v.tensor_scalar(L.ohf[:, :], L.iota8[:, :],
                                                 L.rf32_sb[:, tt:tt + 1],
                                                 None, ALU.is_equal),
                lambda v: v.tensor_tensor(L.rscr[:, 2:2 + E], L.ohf[:, :],
                                          L.baseb[:, 0:E], ALU.mult),
                lambda v: v.reduce_sum(L.rscr[:, 1:2], L.rscr[:, 2:2 + E],
                                       axis=AX),
                lambda v, tt=tt: v.tensor_tensor(L.rscr[:, 0:1], L.rscr[:, 1:2],
                                                 L.pos_sb[:, tt:tt + 1], ALU.add),
                lambda v: v.tensor_scalar(L.rscr[:, 0:1], L.rscr[:, 0:1], -1.0,
                                          None, ALU.add),
                lambda v: v.tensor_copy(L.slotu2[:, 0:1], L.rscr[:, 0:1]),
                lambda v: v.tensor_scalar(L.slotu2[:, 1:2], L.slotu2[:, 0:1],
                                          127, None, ALU.bitwise_and),
                lambda v: v.tensor_scalar(L.slotu2[:, 0:1], L.slotu2[:, 0:1],
                                          7, None, ALU.logical_shift_right),
                lambda v: v.tensor_copy(L.rscr[:, 1:2], L.slotu2[:, 1:2]),
                lambda v: v.tensor_copy(L.rscr[:, 2:3], L.slotu2[:, 0:1]),
                lambda v: v.tensor_scalar(L.rscr[:, 1:2], L.rscr[:, 1:2],
                                          float(NCH), None, ALU.mult),
                lambda v: v.tensor_tensor(L.rscr[:, 0:1], L.rscr[:, 1:2],
                                          L.rscr[:, 2:3], ALU.add),
                lambda v: v.tensor_copy(L.slotu[:, 0:1], L.rscr[:, 0:1]),
            ]
            for st_ in steps_mk:
                self.dv(lambda v, st_=st_: (L.wdv(v), st_(v)))

            def sct(g, tt=tt, b=b):
                L.wdv(g)
                L.wgd(g)
                g.iota(L.tokid[:, :], [[0, 1]], base=tt * 128, channel_multiplier=1,
                       allow_small_or_imprecise_dtypes=True)
                g.indirect_dma_start(
                    out=L.gtbl_out.rearrange("b p q -> (b p q)").rearrange(
                        "(n o) -> n o", o=1),
                    out_offset=IOA(L.slotu[:, 0:1], 0),
                    in_=L.tokid[:, 0:1],
                    in_offset=None,
                    element_offset=b * 128 * NCH,
                ).then_inc(L.s_gd.inc(16), 16)
            self.gp(sct)
        self.barrier()

        def rb(g, b=b):
            g.dma_start(out=L.gidx[:, b * NCH:(b + 1) * NCH], in_=L.gtbl_out[b]) \
             .then_inc(L.s_gd.inc(16), 16)
        self.gp(rb)

        def cmp_(v):
            v.tensor_scalar(L.ohT[0:E, 0:NCH], L.iotaNCH[0:E, :],
                            L.base_r[0:1, 0:E].rearrange("o e -> e o") if False
                            else L.cntc[0:E, 3:4], None, ALU.is_ge)
        def bt(t):
            L.wdv(t)
            t.transpose(L.psD[0:E, 0:1], L.base_r[0:1, 0:E], L.idf[0:1, 0:1])
        self.pe(bt)

        def btc(v):
            L.wpe(v)
            v.tensor_copy(L.cntc[:, 3:4], L.psD[0:E, 0:1])
        self.dv(btc)
        self.dv(cmp_)

        def emm(t):
            L.wdv(t)
            t.matmul(L.psD[0:1, 0:NCH], L.onescf[0:E, 0:1], L.ohT[0:E, 0:NCH],
                     start=True, stop=True)
        self.pe(emm)

        def etb(v, b=b):
            L.wpe(v)
            v.tensor_scalar(L.ech_f[0:1, b * NCH:(b + 1) * NCH],
                            L.psD[0:1, 0:NCH], -1.0, None, ALU.add)
        self.dv(etb)

        def etb2(v, b=b):
            L.wdv(v)
            v.tensor_copy(L.echw[0:1, b * NCH:(b + 1) * NCH],
                          L.ech_f[0:1, b * NCH:(b + 1) * NCH])
        self.dv(etb2)
        self.barrier()

    def alloc_moe(self, ph):
        nc, c = self.nc, self.cfg
        en = ph.enter_context
        self.weT_sb = en(nc.sbuf_tensor("weT_sb", [128, c.E * c.DC * c.D], BF16))
        self.be_sb = en(nc.sbuf_tensor("be_sb", [1, c.E * c.D], BF16))
        self.onesr_b = en(nc.sbuf_tensor("onesr_b", [1, 128], BF16))
        self.stg_tm = en(nc.sbuf_tensor("stg_tm", [128, c.D], BF16))
        self.stg_fm = en(nc.sbuf_tensor("stg_fm", [128, c.D], BF16))
        self.eo_sb = en(nc.sbuf_tensor("eo_sb", [128, c.D], BF16))
        self.gcf = en(nc.sbuf_tensor("gcf", [128, c.NCH], F32))
        self.r_e = nc.alloc_register(mybir.EngineType.PE, "moe_re")
        L = self

        def pro(g):
            g.memset(L.onesr_b[:, :], 1.0)
            g.memset(L.stg_tm[:, :], 0.0)
            g.dma_start(out=L.be_sb[:, :], in_=L.ber).then_inc(L.s_gd.inc(16), 16)
            for e in range(c.E):
                for dc in range(c.DC):
                    g.dma_start(
                        out=L.weT_sb[:, (e * c.DC + dc) * c.D:(e * c.DC + dc + 1) * c.D],
                        in_=L.WeT[e, dc]).then_inc(L.s_gd.inc(16), 16)
        self.gp(pro, inc=True)
        self.barrier()

    def moe_batch(self, b):
        c, L = self.cfg, self
        D, DC, E, NCH, S = c.D, c.DC, c.E, c.NCH, c.S
        nhalf = ceil_div(D, 512)

        steps_cl = [
            lambda v: v.tensor_copy(L.gcf[:, :],
                                    L.gidx[:, b * NCH:(b + 1) * NCH]),
            lambda v: v.tensor_scalar(L.gcf[:, :], L.gcf[:, :],
                                      float(S - 1), None, ALU.min),
            lambda v: v.tensor_copy(L.gidx[:, b * NCH:(b + 1) * NCH],
                                    L.gcf[:, :]),
        ]
        for st_ in steps_cl:
            self.dv(lambda v, st_=st_: (L.wdv(v), st_(v)))
        self.barrier()

        for ch in range(NCH):
            def ga(g, ch=ch, b=b):
                L.wdv(g)
                g.indirect_dma_start(
                    out=L.stg_tm[:, :],
                    out_offset=None,
                    in_=L.hid_out.rearrange("b s d -> (b s) d"),
                    in_offset=IOA(L.gidx[:, b * NCH + ch:b * NCH + ch + 1], 0),
                    element_offset=b * S * D,
                ).then_inc(L.s_gd.inc(16), 16)
            self.gp(ga)

            for dc in range(DC):
                def tp(t, dc=dc):
                    L.wgd(t)
                    L.wdv(t)
                    t.transpose(L.tp_ps[:, 0:128],
                                L.stg_tm[:, dc * 128:(dc + 1) * 128],
                                L.idb[:, :])
                self.pe(tp)

                def cp(v, dc=dc):
                    L.wpe(v)
                    v.tensor_copy(L.stg_fm[:, dc * 128:(dc + 1) * 128],
                                  L.tp_ps[:, 0:128])
                self.dv(cp)

            def mm(t, ch=ch, b=b):
                L.wdv(t)
                r_e = L.r_e
                t.reg_load(r_e, L.echw[0:1, b * NCH + ch:b * NCH + ch + 1])
                for e in range(E):
                    with t.If_eq(r_e, e):
                        for half in range(nhalf):
                            w = min(512, D - half * 512)
                            pp = L.psA if half == 0 else L.psB
                            for dc in range(DC):
                                co = (e * DC + dc) * D + half * 512
                                t.matmul(pp[:, 0:w],
                                         L.stg_fm[:, dc * 128:(dc + 1) * 128],
                                         L.weT_sb[:, co:co + w],
                                         start=(dc == 0), stop=False)
                        for half in range(nhalf):
                            w = min(512, D - half * 512)
                            pp = L.psA if half == 0 else L.psB
                            t.matmul(pp[:, 0:w],
                                     L.onesr_b[0:1, 0:128],
                                     L.be_sb[0:1, e * D + half * 512:e * D + half * 512 + w],
                                     start=False, stop=True)
                t.end_ifs()
            self.pe(mm)

            def ev(v, ch=ch, b=b):
                L.wpe(v)
                L.wgd(v)
                for half in range(nhalf):
                    w = min(512, D - half * 512)
                    pp = L.psA if half == 0 else L.psB
                    v.tensor_copy(L.eo_sb[:, half * 512:half * 512 + w], pp[:, 0:w])
            self.dv(ev)

            def st(g, ch=ch, b=b):
                L.wdv(g)
                g.dma_start(out=L.eout_out[b, ch * 128:(ch + 1) * 128, :],
                            in_=L.eo_sb[:, :]).then_inc(L.s_gd.inc(16), 16)
            self.gp(st)
        self.barrier()


def build_launch1(cfg, upto="all"):
    return L1(cfg, upto).build()



class L2:
    def __init__(self, cfg: CFG):
        self.cfg = cfg
        nc = bass.Bass("TRN2", target_bir_lowering=False)
        nc.detect_race_conditions = False
        self.nc = nc
        c = cfg
        S, D, E, NB, DC, TT, NCH, NSLOT = (c.S, c.D, c.E, c.NB, c.DC, c.TT,
                                           c.NCH, c.NSLOT)
        NBG = c.n_cores * NB
        self.NBG = NBG
        dt = nc.dram_tensor
        self.x1_in = dt("x1_in", [NB, DC, 128, S], F32, kind="ExternalInput").ap()
        self.hid_in = dt("hid_in", [NB, S, D], BF16, kind="ExternalInput").ap()
        self.eout_in = dt("eout_in", [NB, NSLOT, D], BF16, kind="ExternalInput").ap()
        self.gtbl_in = dt("gtbl_in", [NB, 128, NCH], U32, kind="ExternalInput").ap()
        self.probs_in = dt("probs_in", [NB, 128, TT], F32, kind="ExternalInput").ap()
        self.ln3_in = dt("ln3_in", [NB, 2, S], F32, kind="ExternalInput").ap()
        self.allcnt = dt("allcnt", [NBG, E], F32, kind="ExternalInput").ap()
        self.offmask = dt("offmask", [NBG, 2 * NB], F32, kind="ExternalInput").ap()
        self.identf = dt("identf2", [128, 128], F32, kind="ExternalInput").ap()
        self.lonesr = dt("lonesr", [E, E], F32, kind="ExternalInput").ap()
        self.gammaT = dt("gammaT", [128, DC], F32, kind="ExternalInput").ap()
        self.betaT = dt("betaT", [128, DC], F32, kind="ExternalInput").ap()
        self.out = dt("out", [NB, DC, 128, S], F32, kind="ExternalOutput").ap()
        self.y_hbm = dt("y_hbm", [NB * S + 128, D], BF16).ap()

        es = contextlib.ExitStack()
        self.es = es

        def sbuf(name, p, cc, dty):
            return es.enter_context(nc.sbuf_tensor(name, [p, cc], dty))
        self.sbuf = sbuf
        self.onesc = sbuf("onesc", 128, 1, BF16)
        self.onesrf = sbuf("onesrf", 128, 128, F32)
        self.lones = sbuf("lones", E, E, F32)
        self.rowb = sbuf("rowb", 1, 2048, F32)
        self.mu4 = sbuf("mu4", 1, S, F32)
        self.rs4 = sbuf("rs4", 1, S, F32)
        self.mu3 = sbuf("mu3", 1, S, F32)
        self.rs3 = sbuf("rs3", 1, S, F32)
        self.acnt = sbuf("acnt", NBG, E, F32)
        self.mycc = sbuf("mycc", E, 4, F32)
        self.mycu = sbuf("mycu", E, 1, U32)
        self.offc = sbuf("offc", E, 2, F32)
        self.maskc = sbuf("maskc", NBG, 2 * NB, F32)
        self.idb = sbuf("idb", 128, 128, BF16)
        self.iotaN = sbuf("iotaN", E, NCH, F32)
        self.iotaP = sbuf("iotaP", 128, 1, F32)
        self.base_r = sbuf("base_r", 1, E, F32)
        self.rows = sbuf("rows", 1, 6 * NCH, F32)
        self.cmpA = sbuf("cmpA", E, NCH, F32)
        self.cmpB = sbuf("cmpB", E, NCH, F32)
        self.thrb = sbuf("thrb", 128, NCH, F32)
        self.keptm = sbuf("keptm", 128, NCH, F32)
        self.gidxf = sbuf("gidxf", 128, NCH, F32)
        self.sidx = sbuf("sidx", 128, NCH, U32)
        self.gidx = sbuf("gidx", 128, NCH, U32)
        self.eo_sb = sbuf("eo_sb", 128, D, BF16)
        self.yt_sb = sbuf("yt_sb", 128, D, BF16)
        self.gam = sbuf("gam", 128, DC, F32)
        self.bet = sbuf("bet", 128, DC, F32)
        self.zbb = sbuf("zbb", 128, 512, F32)
        self.zrow = sbuf("zrow", 128, 512, F32)
        self.mvb = sbuf("mvb", 128, 1024, F32)
        self.x1c = sbuf("x1c", 128, DC * 512, F32)
        self.x2_sb = sbuf("x2_sb", 128, DC * S, F32)
        self.xb_sb = sbuf("xb_sb", 128, DC * S, BF16)
        self.stg_b = sbuf("stg_b", 128, DC * 512, BF16)
        self.sqc = sbuf("sqc", 128, DC * 512, BF16)

        self.psA = es.enter_context(nc.psum_tensor("psA", [128, 512], F32))
        self.psD = es.enter_context(nc.psum_tensor("psD", [128, 512], F32))
        self.tp_ps = es.enter_context(nc.psum_tensor("tp_ps", [128, 512], BF16))
        self.st_ps = es.enter_context(nc.psum_tensor("st_ps", [1, 1024], F32))

        self.s_hd = Sem(es, nc, "s_hd")
        self.s_gd = Sem(es, nc, "s_gd")
        self.s_pe = Sem(es, nc, "s_pe")
        self.s_dv = Sem(es, nc, "s_dv")
        self.s_ac = Sem(es, nc, "s_ac")
        self.s_gp = Sem(es, nc, "s_gp")
        self.sems = [self.s_hd, self.s_gd, self.s_pe, self.s_dv, self.s_ac,
                     self.s_gp]
        self.block = es.enter_context(nc.Block())
        self.items = {"tensor": [], "vector": [], "scalar": []}
        self._cursnap = None

    wait_all = L1.wait_all
    _curv = L1._curv
    _defer = L1._defer
    flush_items = L1.flush_items
    pe = L1.pe
    dv = L1.dv
    ac = L1.ac
    gp = L1.gp
    sy = L1.sy
    barrier = L1.barrier
    wpe = L1.wpe
    wdv = L1.wdv
    wac = L1.wac
    wgd = L1.wgd
    whd = L1.whd
    wgp = L1.wgp
    row_bcast = L1.row_bcast
    stats = L1.stats
    ln_apply = L1.ln_apply

    def build(self):
        c, L = self.cfg, self
        S, D, E, NB, DC, TT, NCH = c.S, c.D, c.E, c.NB, c.DC, c.TT, c.NCH
        NBG = self.NBG

        def pro(g):
            g.memset(L.onesc[:, :], 1.0)
            g.memset(L.onesrf[:, :], 1.0)
            g.iota(L.iotaN[:, :], [[128, NCH]], base=0, channel_multiplier=0,
                   allow_small_or_imprecise_dtypes=True)
            g.iota(L.iotaP[:, :], [[0, 1]], base=0, channel_multiplier=1,
                   allow_small_or_imprecise_dtypes=True)
            g.dma_start(out=L.lones[:, :], in_=L.lonesr).then_inc(L.s_gd.inc(16), 16)
            g.dma_start(out=L.acnt[:, :], in_=L.allcnt).then_inc(L.s_gd.inc(16), 16)
            g.dma_start(out=L.maskc[:, :], in_=L.offmask) \
             .then_inc(L.s_gd.inc(16), 16)
            g.dma_start(out=L.gam[:, :], in_=L.gammaT).then_inc(L.s_gd.inc(16), 16)
            g.dma_start(out=L.bet[:, :], in_=L.betaT).then_inc(L.s_gd.inc(16), 16)
            g.dma_start(out=L.idb[:, :], in_=L.identf).then_inc(L.s_gd.inc(16), 16)
        self.gp(pro, inc=True)
        self.barrier()

        for b in range(NB):
            self.batch(b)
        self.flush_items()
        self.sy(lambda e: self.wait_all(e))
        self.es.close()
        return self.nc

    def batch(self, b):
        c, L = self.cfg, self
        S, D, E, NB, DC, TT, NCH = c.S, c.D, c.E, c.NB, c.DC, c.TT, c.NCH
        NBG = self.NBG
        myrow = None

        def t0(v, b=b):
            pass
        def mm0(t, b=b):
            self.wait_all(t)
            t.matmul(L.psA[0:E, 0:2], L.acnt[:, :],
                     L.maskc[:, 2 * b:2 * b + 2], start=True, stop=True)
        self.pe(mm0)

        steps_c0 = [
            lambda v: v.tensor_copy(L.offc[:, 0:1], L.psA[0:E, 0:1]),
            lambda v: v.tensor_copy(L.mycc[:, 0:1], L.psA[0:E, 1:2]),
            lambda v: v.tensor_scalar(L.mycc[:, 2:3], L.mycc[:, 0:1], 127.0,
                                      None, ALU.add),
            lambda v: v.tensor_copy(L.mycu[:, 0:1], L.mycc[:, 2:3]),
            lambda v: v.tensor_scalar(L.mycu[:, 0:1], L.mycu[:, 0:1], 7, None,
                                      ALU.logical_shift_right),
            lambda v: v.tensor_scalar(L.mycu[:, 0:1], L.mycu[:, 0:1], 7, None,
                                      ALU.logical_shift_left),
            lambda v: v.tensor_copy(L.mycc[:, 1:2], L.mycu[:, 0:1]),
        ]
        first_c0 = [True]
        for st_ in steps_c0:
            def item(v, st_=st_, first_c0=first_c0):
                if first_c0[0]:
                    L.wpe(v)
                    first_c0[0] = False
                else:
                    L.wdv(v)
                st_(v)
            self.dv(item)

        def mm1(t):
            L.wdv(t)
            t.matmul(L.psD[0:1, 0:E], L.mycc[0:E, 1:2], L.lones[:, :],
                     start=True, stop=True)
        self.pe(mm1)

        def cp1(v):
            L.wpe(v)
            v.tensor_copy(L.base_r[0:1, :], L.psD[0:1, 0:E])
        self.dv(cp1)

        def tp1(t):
            L.wdv(t)
            t.transpose(L.psD[0:E, 0:1], L.base_r[0:1, 0:E], L.onesrf[0:1, 0:1])
        self.pe(tp1)

        def cp2a(v):
            L.wpe(v)
            v.tensor_copy(L.mycc[:, 3:4], L.psD[0:E, 0:1])
        self.dv(cp2a)

        def cp2a2(v):
            L.wdv(v)
            v.tensor_tensor(L.offc[:, 1:2], L.mycc[:, 3:4], L.mycc[:, 1:2],
                            ALU.add)
        self.dv(cp2a2)

        def cp2(v):
            L.wdv(v)
            v.tensor_scalar(L.cmpA[:, :], L.iotaN[:, :], L.mycc[:, 3:4], None,
                            ALU.is_ge)
            v.tensor_scalar(L.cmpB[:, :], L.iotaN[:, :], L.offc[:, 1:2], None,
                            ALU.is_ge)
        self.dv(cp2)

        def cp2b(v):
            L.wdv(v)
            v.tensor_tensor(L.cmpB[:, :], L.cmpA[:, :], L.cmpB[:, :],
                            ALU.subtract)
        self.dv(cp2b)

        def mm2(t):
            L.wdv(t)
            t.matmul(L.psD[0:1, 0:NCH], L.offc[0:E, 0:1], L.cmpB[:, :],
                     start=True, stop=True)
        self.pe(mm2)

        def cp3(v):
            L.wpe(v)
            v.tensor_copy(L.rows[0:1, NCH:2 * NCH], L.psD[0:1, 0:NCH])
            v.tensor_tensor(L.cmpA[:, :], L.cmpA[:, :], L.cmpB[:, :],
                            ALU.subtract)
        self.dv(cp3)

        def mm3(t):
            L.wdv(t)
            t.matmul(L.psD[0:1, 0:NCH], L.mycc[0:E, 1:2], L.cmpA[:, :],
                     start=True, stop=True)
        self.pe(mm3)

        steps_c4 = [
            lambda v: v.tensor_copy(L.rows[0:1, 0:NCH], L.psD[0:1, 0:NCH]),
            lambda v: v.tensor_tensor(L.rows[0:1, 2 * NCH:3 * NCH],
                                      L.rows[0:1, 0:NCH],
                                      L.rows[0:1, NCH:2 * NCH], ALU.subtract),
            lambda v: v.tensor_scalar(L.rows[0:1, 2 * NCH:3 * NCH],
                                      L.rows[0:1, 2 * NCH:3 * NCH],
                                      float(c.capacity), None, ALU.add),
            lambda v: v.tensor_tensor(L.rows[0:1, 2 * NCH:3 * NCH],
                                      L.rows[0:1, 2 * NCH:3 * NCH],
                                      L.iotaN[0:1, 0:NCH], ALU.subtract),
        ]
        first_c4 = [True]
        for st_ in steps_c4:
            def item(v, st_=st_, first_c4=first_c4):
                if first_c4[0]:
                    L.wpe(v)
                    first_c4[0] = False
                else:
                    L.wdv(v)
                st_(v)
            self.dv(item)
        self.row_bcast(L.rows[0:1, 2 * NCH:3 * NCH], NCH, L.thrb[:, :])

        def ld_gt(g, b=b):
            g.dma_start(out=L.gidx[:, :], in_=L.gtbl_in[b]) \
             .then_inc(L.s_gd.inc(16), 16)
        self.gp(ld_gt)
        self.barrier()

        trash = float((NB - b) * S)
        steps_kp = [
            lambda v: v.tensor_scalar(L.keptm[:, :], L.thrb[:, :],
                                      L.iotaP[:, 0:1], None, ALU.is_gt),
            lambda v: v.tensor_copy(L.gidxf[:, :], L.gidx[:, :]),
            lambda v: v.tensor_scalar(L.gidxf[:, :], L.gidxf[:, :],
                                      -float(c.SENT), None, ALU.add),
            lambda v: v.tensor_tensor(L.gidxf[:, :], L.keptm[:, :],
                                      L.gidxf[:, :], ALU.mult),
            lambda v: v.tensor_scalar(L.gidxf[:, :], L.gidxf[:, :],
                                      float(c.SENT), None, ALU.add),
            lambda v: v.tensor_scalar(L.gidxf[:, :], L.gidxf[:, :],
                                      trash, None, ALU.min),
            lambda v: v.tensor_copy(L.sidx[:, :], L.gidxf[:, :]),
        ]
        for st_ in steps_kp:
            self.dv(lambda v, st_=st_: (L.wdv(v), st_(v)))

        for tt in range(TT):
            def cpy(g, tt=tt, b=b):
                L.wgd(g)
                g.dma_start(out=L.yt_sb[:, :],
                            in_=L.hid_in[b, tt * 128:(tt + 1) * 128, :]) \
                 .then_inc(L.s_gd.inc(16), 16)
                g.wait_ge(L.s_gd.h, L.s_gd.v)
                g.dma_start(out=L.y_hbm[b * S + tt * 128:b * S + (tt + 1) * 128, :],
                            in_=L.yt_sb[:, :]).then_inc(L.s_gd.inc(16), 16)
            self.gp(cpy)
        self.barrier()

        for ch in range(NCH):
            def sc(g, ch=ch, b=b):
                L.wgd(g)
                g.dma_start(out=L.eo_sb[:, :],
                            in_=L.eout_in[b, ch * 128:(ch + 1) * 128, :]) \
                 .then_inc(L.s_gd.inc(16), 16)
                g.wait_ge(L.s_gd.h, L.s_gd.v)
                g.indirect_dma_start(
                    out=L.y_hbm,
                    out_offset=IOA(L.sidx[:, ch:ch + 1], 0),
                    in_=L.eo_sb[:, :],
                    in_offset=None,
                    element_offset=b * S * D,
                ).then_inc(L.s_gd.inc(16), 16)
            self.gp(sc)
        self.barrier()

        def ld3(g, b=b):
            g.dma_start(out=L.mu3[0:1, :], in_=L.ln3_in[b, 0:1, :]) \
             .then_inc(L.s_gd.inc(16), 16)
            g.dma_start(out=L.rs3[0:1, :], in_=L.ln3_in[b, 1:2, :]) \
             .then_inc(L.s_gd.inc(16), 16)
            g.dma_start(out=L.thrb[:, 0:TT], in_=L.probs_in[b]) \
             .then_inc(L.s_gd.inc(16), 16)
        self.gp(ld3)
        self.barrier()

        for tt in range(TT):
            def ldy(g, tt=tt, b=b):
                L.wpe(g)
                g.dma_start(out=L.yt_sb[:, :],
                            in_=L.y_hbm[b * S + tt * 128:b * S + (tt + 1) * 128, :]) \
                 .then_inc(L.s_gd.inc(16), 16)
            self.gp(ldy)

            def sc_y(v, tt=tt):
                L.wgd(v)
                v.tensor_scalar(L.yt_sb[:, :], L.yt_sb[:, :],
                                L.thrb[:, tt:tt + 1], None, ALU.mult)
            self.dv(sc_y)

            for dc in range(DC):
                def tp(t, tt=tt, dc=dc):
                    L.wdv(t)
                    t.transpose(L.tp_ps[:, 0:128],
                                L.yt_sb[:, dc * 128:(dc + 1) * 128],
                                L.idb[:, :])
                self.pe(tp)

                def cp(v, tt=tt, dc=dc):
                    L.wpe(v)
                    v.tensor_copy(L.xb_sb[:, dc * S + tt * 128:dc * S + (tt + 1) * 128],
                                  L.tp_ps[:, 0:128])
                self.dv(cp)
        self.barrier()
        for j in range(c.QC):
            c0 = j * 512

            def ldx1(sy_, j=j, c0=c0, b=b):
                L.wdv(sy_)
                for dc in range(DC):
                    sy_.dma_start(out=L.x1c[:, dc * 512:(dc + 1) * 512],
                                  in_=L.x1_in[b, dc, :, c0:c0 + 512]) \
                       .then_inc(L.s_hd.inc(16), 16)
            self.sy(ldx1)
            self.row_bcast(L.rs3[0:1, c0:c0 + 512], 512, L.zbb[:, 0:512])
            self.row_bcast(L.mu3[0:1, c0:c0 + 512], 512, L.zrow[:, 0:512])

            def x2c(v, j=j, c0=c0):
                L.wdv(v)
                L.whd(v)
                for dc in range(DC):
                    sl = slice(dc * S + c0, dc * S + c0 + 512)
                    v.tensor_tensor(L.mvb[:, 0:512], L.x1c[:, dc * 512:(dc + 1) * 512],
                                    L.zrow[:, 0:512], ALU.subtract)
                    v.tensor_tensor(L.mvb[:, 512:1024], L.mvb[:, 0:512],
                                    L.zbb[:, 0:512], ALU.mult)
                    v.tensor_tensor(L.x2_sb[:, sl], L.mvb[:, 512:1024],
                                    L.xb_sb[:, sl], ALU.add)
            self.dv(x2c)
        self.barrier()
        self.stats(lambda dc: L.x2_sb[:, dc * S:(dc + 1) * S], S, L.mu4, L.rs4)
        self.barrier()
        for j in range(c.QC):
            c0 = j * 512
            self.row_bcast(L.rs4[0:1, c0:c0 + 512], 512, L.zbb[:, 0:512])
            self.row_bcast(L.mu4[0:1, c0:c0 + 512], 512, L.zrow[:, 0:512])

            def f4(v, j=j, c0=c0):
                L.wdv(v)
                for dc in range(DC):
                    sl = slice(dc * S + c0, dc * S + c0 + 512)
                    v.tensor_tensor(L.mvb[:, 0:512], L.x2_sb[:, sl],
                                    L.zrow[:, 0:512], ALU.subtract)
                    v.tensor_tensor(L.mvb[:, 512:1024], L.mvb[:, 0:512],
                                    L.zbb[:, 0:512], ALU.mult)
                    v.tensor_scalar(L.xb_sb[:, sl], L.mvb[:, 512:1024],
                                    L.gam[:, dc:dc + 1], L.bet[:, dc:dc + 1],
                                    ALU.mult, ALU.add)
            self.dv(f4)
        self.barrier()

        self.stats(lambda dc: L.xb_sb[:, dc * S:(dc + 1) * S], S, L.mu4, L.rs4,
                   src_bf16=True)
        self.barrier()
        for j in range(c.QC):
            c0 = j * 512

            def ldx1(sy_, j=j, c0=c0, b=b):
                L.whd(sy_)
                L.wdv(sy_)
                for dc in range(DC):
                    sy_.dma_start(out=L.x1c[:, dc * 512:(dc + 1) * 512],
                                  in_=L.x1_in[b, dc, :, c0:c0 + 512]) \
                       .then_inc(L.s_hd.inc(16), 16)
            self.sy(ldx1)
            self.row_bcast(L.rs4[0:1, c0:c0 + 512], 512, L.zbb[:, 0:512])
            self.row_bcast(L.mu4[0:1, c0:c0 + 512], 512, L.zrow[:, 0:512])

            def oc_(v, j=j, c0=c0):
                L.wdv(v)
                L.whd(v)
                for dc in range(DC):
                    sl = slice(dc * S + c0, dc * S + c0 + 512)
                    v.tensor_tensor(L.mvb[:, 0:512], L.xb_sb[:, sl],
                                    L.zrow[:, 0:512], ALU.subtract)
                    v.tensor_tensor(L.mvb[:, 512:1024], L.mvb[:, 0:512],
                                    L.zbb[:, 0:512], ALU.mult)
                    v.tensor_tensor(L.x1c[:, dc * 512:(dc + 1) * 512],
                                    L.x1c[:, dc * 512:(dc + 1) * 512],
                                    L.mvb[:, 512:1024], ALU.add)
            self.dv(oc_)

            def spo(sy_, j=j, c0=c0, b=b):
                L.wdv(sy_)
                for dc in range(DC):
                    sy_.dma_start(out=L.out[b, dc, :, c0:c0 + 512],
                                  in_=L.x1c[:, dc * 512:(dc + 1) * 512]) \
                       .then_inc(L.s_hd.inc(16), 16)
            self.sy(spo)
        self.sy(lambda e: e.wait_ge(L.s_hd.h, L.s_hd.v))
        self.barrier()


def build_launch2(cfg):
    return L2(cfg).build()



def prep_inputs(cfg, inputs):
    E, D, H, NB, S, DC = cfg.E, cfg.D, cfg.H, cfg.NB, cfg.S, cfg.DC
    HP = H // 2
    x = np.ascontiguousarray(inputs["L0_attn_x"])
    qkv_w = np.asarray(inputs["L0_attn_qkv_w"])
    common = {}
    common["gtok"] = np.ascontiguousarray(
        np.asarray(inputs["L0_attn_global_token"]).reshape(D).reshape(DC, 128))
    common["qkv_wT"] = np.ascontiguousarray(qkv_w.T).reshape(DC, 128, 3 * D)
    common["proj_wT"] = np.ascontiguousarray(
        np.asarray(inputs["L0_attn_proj_w"]).T).reshape(DC, 128, D)
    common["proj_bT"] = np.ascontiguousarray(
        np.asarray(inputs["L0_attn_proj_b"]).reshape(DC, 128).T)
    Wq = np.asarray(inputs["L0_ffn_Wq"]).reshape(H * 64, D)
    Wk = np.asarray(inputs["L0_ffn_Wk"]).reshape(H * 64, D)
    Wv = np.asarray(inputs["L0_ffn_Wv"]).reshape(H * 64, D)
    Wqkv = np.concatenate([Wq, Wk, Wv], 0)
    common["WqkvT"] = np.ascontiguousarray(Wqkv.T).reshape(DC, 128, 3 * D)
    common["Wv_nat"] = np.ascontiguousarray(Wv)
    common["WswT"] = np.ascontiguousarray(np.asarray(inputs["L0_ffn_Wsw"]).T)
    common["bswr"] = np.ascontiguousarray(
        np.asarray(inputs["L0_ffn_bsw"]).reshape(1, E))
    bq = np.asarray(inputs["L0_ffn_bq"]).reshape(HP, 128).T
    bk = np.asarray(inputs["L0_ffn_bk"]).reshape(HP, 128).T
    bv = np.asarray(inputs["L0_ffn_bv"]).reshape(HP, 128).T
    common["bqkvT"] = np.ascontiguousarray(np.concatenate([bq, bk, bv], 1))
    common["bvr"] = np.ascontiguousarray(np.asarray(inputs["L0_ffn_bv"]).reshape(1, H * 64))
    common["WeT"] = np.ascontiguousarray(
        np.transpose(np.asarray(inputs["L0_ffn_We"]), (0, 2, 1))).reshape(
            E, DC, 128, D)
    common["ber"] = np.ascontiguousarray(
        np.asarray(inputs["L0_ffn_be"]).reshape(1, E * D))
    common["identf"] = np.eye(128, dtype=np.float32)
    ki = np.arange(128)[:, None]
    qi = np.arange(128)[None, :]
    common["amask0"] = (((ki + 1) // 64) <= ((qi + 1) // 64)).astype(np.float32)
    common["amask1"] = (((ki + 129) // 64) <= ((qi + 1) // 64)).astype(np.float32)
    common["fmask0"] = (ki <= qi).astype(np.float32)
    suf = np.zeros((128, 129), np.float32)
    suf[:, :128] = (np.arange(128)[:, None] > np.arange(128)[None, :])
    suf[:, 128] = 1.0
    common["sufones"] = suf
    common["lonesr"] = (np.arange(E)[:, None] < np.arange(E)[None, :]).astype(
        np.float32)

    maps = []
    for ci in range(cfg.n_cores):
        m = dict(common)
        xs = np.asarray(x[ci * NB:(ci + 1) * NB])
        m["xT"] = np.ascontiguousarray(xs.transpose(0, 2, 1)).reshape(
            NB, DC, 128, S)
        maps.append(m)
    return maps


def kernel(**inputs):
    cfg = FULL
    return _run(cfg, inputs)


def _run(cfg, inputs):
    maps1 = prep_inputs(cfg, inputs)
    nc1 = build_launch1(cfg, upto="all")
    res1 = run_bass_kernel_spmd(nc1, maps1, core_ids=list(range(cfg.n_cores)))

    maps2 = prep_launch2(cfg, inputs, res1.results)
    nc2 = build_launch2(cfg)
    res2 = run_bass_kernel_spmd(nc2, maps2, core_ids=list(range(cfg.n_cores)))

    outs = []
    for ci in range(cfg.n_cores):
        o = res2.results[ci]["out"].reshape(cfg.NB, cfg.D, cfg.S)
        outs.append(np.ascontiguousarray(o.transpose(0, 2, 1)))
    return np.concatenate(outs, 0).astype(np.float32)


def prep_launch2(cfg, inputs, res1):
    NB, E, D, DC = cfg.NB, cfg.E, cfg.D, cfg.DC
    NBG = cfg.n_cores * NB
    allcnt = np.concatenate(
        [np.asarray(r["cnt_out"]).reshape(NB, E) for r in res1], 0).astype(
            np.float32)
    lones = (np.arange(E)[:, None] < np.arange(E)[None, :]).astype(np.float32)
    gammaT = np.ascontiguousarray(
        np.asarray(inputs["L0_ffn_gamma"]).reshape(DC, 128).T)
    betaT = np.ascontiguousarray(
        np.asarray(inputs["L0_ffn_beta"]).reshape(DC, 128).T)
    identf = np.eye(128, dtype=np.float32)
    maps = []
    for ci in range(cfg.n_cores):
        m = {}
        r = res1[ci]
        m["x1_in"] = np.asarray(r["x1_out"])
        m["hid_in"] = np.asarray(r["hid_out"])
        m["eout_in"] = np.asarray(r["eout_out"])
        m["gtbl_in"] = np.asarray(r["gtbl_out"])
        m["probs_in"] = np.asarray(r["probs_out"])
        m["ln3_in"] = np.asarray(r["ln3_out"])
        m["allcnt"] = allcnt
        om = np.zeros((NBG, 2 * NB), np.float32)
        for b in range(NB):
            g = ci * NB + b
            om[:g, 2 * b] = 1.0
            om[g, 2 * b + 1] = 1.0
        m["offmask"] = om
        m["lonesr"] = lones
        m["gammaT"] = gammaT
        m["betaT"] = betaT
        m["identf2"] = identf
        maps.append(m)
    return maps


# version 03
# speedup vs baseline: 1.4444x; 1.1268x over previous
import contextlib
import math
import numpy as np

import concourse.bass as bass
import concourse.mybir as mybir
from concourse.bass import IndirectOffsetOnAxis as IOA
from concourse.bass_utils import run_bass_kernel_spmd
from concourse.bass import create_sync_update

F32 = mybir.dt.float32
BF16 = mybir.dt.bfloat16
U32 = mybir.dt.uint32
AF = mybir.ActivationFunctionType
ALU = mybir.AluOpType
AX = mybir.AxisListType.X


class CFG:
    def __init__(self, S=2048, D=1024, H=16, E=8, NB=2, n_cores=8,
                 cap_factor=1.2):
        self.S = S
        self.D = D
        self.H = H
        self.DH = 64
        self.E = E
        self.NB = NB
        self.n_cores = n_cores
        self.DC = D // 128
        self.TT = S // 128
        self.QCW = 512
        self.QC = S // self.QCW
        self.N = S + 1
        self.capacity = int(cap_factor * n_cores * NB * S / E)
        self.NCH = (S + E * 127 + 127) // 128
        self.NSLOT = self.NCH * 128
        self.SENT = 1 << 20
        self.GA = 4 if H >= 4 else 2
        self.WS = max(3 * self.GA * 64, D)
        self.WS2 = 3 * self.GA * 64


FULL = CFG()


class Sem:
    def __init__(self, es, nc, name):
        self.h = es.enter_context(nc.semaphore(name))
        self.v = 0

    def inc(self, n=1):
        self.v += n
        return self.h


def ceil_div(a, b):
    return (a + b - 1) // b



class L1:
    def __init__(self, cfg: CFG, upto="all"):
        self.cfg = cfg
        self.upto = upto
        nc = bass.Bass("TRN2", target_bir_lowering=False)
        nc.detect_race_conditions = False
        self.nc = nc
        c = cfg
        S, D, H, E, NB = c.S, c.D, c.H, c.E, c.NB
        DC, NCH, NSLOT = c.DC, c.NCH, c.NSLOT
        HP = H // 2

        dt = nc.dram_tensor
        self.xT = dt("xT", [NB, DC, 128, S], F32, kind="ExternalInput").ap()
        self.gtok = dt("gtok", [DC, 128], F32, kind="ExternalInput").ap()
        self.qkv_wT = dt("qkv_wT", [DC, 128, 3 * D], F32, kind="ExternalInput").ap()
        self.proj_wT = dt("proj_wT", [DC, 128, D], F32, kind="ExternalInput").ap()
        self.proj_bT = dt("proj_bT", [128, DC], F32, kind="ExternalInput").ap()
        self.WqkvT = dt("WqkvT", [DC, 128, 3 * D], F32, kind="ExternalInput").ap()
        self.Wv_nat = dt("Wv_nat", [H * 64, D], F32, kind="ExternalInput").ap()
        self.WswT = dt("WswT", [D, E], F32, kind="ExternalInput").ap()
        self.bswr = dt("bswr", [1, E], F32, kind="ExternalInput").ap()
        self.bqkvT = dt("bqkvT", [128, 3 * HP], F32, kind="ExternalInput").ap()
        self.bvr = dt("bvr", [1, H * 64], F32, kind="ExternalInput").ap()
        self.WeT = dt("WeT", [E, DC, 128, D], F32, kind="ExternalInput").ap()
        self.ber = dt("ber", [1, E * D], F32, kind="ExternalInput").ap()
        self.identf = dt("identf", [128, 128], F32, kind="ExternalInput").ap()
        self.amask0 = dt("amask0", [128, 128], F32, kind="ExternalInput").ap()
        self.amask1 = dt("amask1", [128, 128], F32, kind="ExternalInput").ap()
        self.fmask0 = dt("fmask0", [128, 128], F32, kind="ExternalInput").ap()
        self.sufones = dt("sufones", [128, 129], F32, kind="ExternalInput").ap()
        self.lonesr = dt("lonesr", [E, E], F32, kind="ExternalInput").ap()

        self.x1_out = dt("x1_out", [NB, DC, 128, S], F32, kind="ExternalOutput").ap()
        self.hid_out = dt("hid_out", [NB, S, D], BF16, kind="ExternalOutput").ap()
        self.eout_out = dt("eout_out", [NB, NSLOT, D], BF16, kind="ExternalOutput").ap()
        self.gtbl_out = dt("gtbl_out", [NB, 128, NCH], U32, kind="ExternalOutput").ap()
        self.cnt_out = dt("cnt_out", [NB, E, 1], F32, kind="ExternalOutput").ap()
        self.probs_out = dt("probs_out", [NB, 128, c.TT], F32, kind="ExternalOutput").ap()
        self.ln3_out = dt("ln3_out", [NB, 2, S], F32, kind="ExternalOutput").ap()

        self.ao_hbm = dt("ao_hbm", [NB, DC, 128, S], BF16).ap()
        self.e3_hbm = dt("e3_hbm", [NB, DC, 128, S], BF16).ap()

        es = contextlib.ExitStack()
        self.es = es

        def sbuf(name, p, cc, dty):
            return es.enter_context(nc.sbuf_tensor(name, [p, cc], dty))
        self.sbuf = sbuf

        self.idf = sbuf("idf", 128, 128, F32)
        self.idb = sbuf("idb", 128, 128, BF16)
        self.am0 = sbuf("am0", 128, 128, BF16)
        self.am1 = sbuf("am1", 128, 128, BF16)
        self.fm0 = sbuf("fm0", 128, 128, BF16)
        self.sfo = sbuf("sfo", 128, 129, BF16)
        self.lones = sbuf("lones", E, E, F32)
        self.onesc = sbuf("onesc", 128, 1, BF16)
        self.rowb = sbuf("rowb", 1, 1024, F32)
        self.mu1 = sbuf("mu1", 1, S, F32)
        self.rs1 = sbuf("rs1", 1, S, F32)
        self.mu3 = sbuf("mu3", 1, S, F32)
        self.rs3 = sbuf("rs3", 1, S, F32)
        self.bqkv_sb = sbuf("bqkv_sb", 128, 3 * HP, F32)
        self.bsw_sb = sbuf("bsw_sb", 1, E, F32)
        self.bswb = sbuf("bswb", 128, E, F32)
        self.gidx = sbuf("gidx", 128, NB * NCH, U32)
        self.echw = sbuf("echw", 1, NB * NCH, U32)
        self.echb = sbuf("echb", 1, NB * NCH, U32)
        self.ech_f = sbuf("ech_f", 1, NB * NCH, F32)
        self.iotaNCH = sbuf("iotaNCH", E, NCH, F32)

        self.psA = es.enter_context(nc.psum_tensor("psA", [128, 512], F32))
        self.psB = es.enter_context(nc.psum_tensor("psB", [128, 512], F32))
        self.psC = es.enter_context(nc.psum_tensor("psC", [128, 512], F32))
        self.psD = es.enter_context(nc.psum_tensor("psD", [128, 512], F32))
        self.tp_ps = es.enter_context(nc.psum_tensor("tp_ps", [128, 512], BF16))
        self.st_ps = es.enter_context(nc.psum_tensor("st_ps", [1, 1024], F32))

        self.s_hd = Sem(es, nc, "s_hd")
        self.s_gd = Sem(es, nc, "s_gd")
        self.s_pe = Sem(es, nc, "s_pe")
        self.s_dv = Sem(es, nc, "s_dv")
        self.s_ac = Sem(es, nc, "s_ac")
        self.s_gp = Sem(es, nc, "s_gp")
        self.sems = [self.s_hd, self.s_gd, self.s_pe, self.s_dv, self.s_ac,
                     self.s_gp]

        self.block = es.enter_context(nc.Block())
        self.items = {"tensor": [], "vector": [], "scalar": []}
        self._cursnap = None

    def _curv(self, s):
        if self._cursnap is not None:
            return self._cursnap[id(s)]
        return s.v

    def wait_all(self, e):
        for s in self.sems:
            e.wait_ge(s.h, self._curv(s))

    def _defer(self, eng, f, inc):
        snap = {id(s): s.v for s in self.sems}
        s = {"tensor": self.s_pe, "vector": self.s_dv,
             "scalar": self.s_ac}[eng]
        if inc:
            s.v += 1
        self.items[eng].append((f, snap, inc, s))

    def flush_items(self):
        for eng, emit in (("tensor", self.block.tensor),
                          ("vector", self.block.vector),
                          ("scalar", self.block.scalar)):
            lst = self.items[eng]
            if not lst:
                continue
            self.items[eng] = []

            def run(e, lst=lst):
                for f, snap, inc, s in lst:
                    self._cursnap = snap
                    f(e)
                    if inc:
                        e.maybe_drain_then_inc((s.h, 1))
                self._cursnap = None
            emit(run)

    def pe(self, f, inc=True):
        self._defer("tensor", f, inc)

    def dv(self, f, inc=True):
        self._defer("vector", f, inc)

    def ac(self, f, inc=True):
        self._defer("scalar", f, inc)

    def gp(self, f, inc=False):
        s = self.s_gp

        def g(e):
            f(e)
            if inc:
                e.maybe_drain_then_inc((s.h, 1))
                s.inc()
        self.block.gpsimd(g)

    def sy(self, f):
        self.block.sync(f)

    def barrier(self):
        self.pe(lambda t: None, inc=True)
        self.dv(lambda v: None, inc=True)
        self.ac(lambda e: None, inc=True)
        self.gp(lambda e: None, inc=True)
        for emit in (self.pe, self.dv, self.ac, self.gp):
            emit(lambda e: self.wait_all(e), inc=False)
        self.sy(lambda e: self.wait_all(e))

    def wpe(self, e):
        e.wait_ge(self.s_pe.h, self._curv(self.s_pe))

    def wdv(self, e):
        e.wait_ge(self.s_dv.h, self._curv(self.s_dv))

    def wac(self, e):
        e.wait_ge(self.s_ac.h, self._curv(self.s_ac))

    def wgd(self, e):
        e.wait_ge(self.s_gd.h, self._curv(self.s_gd))

    def whd(self, e):
        e.wait_ge(self.s_hd.h, self._curv(self.s_hd))

    def wgp(self, e):
        e.wait_ge(self.s_gp.h, self._curv(self.s_gp))

    def row_bcast(self, row_ap, w, out_ap):
        def mm(t):
            self.wdv(t)
            t.matmul(self.psD[:, 0:w], self.onesrf[0:1, 0:128], row_ap,
                     start=True, stop=True)
        self.pe(mm)

        def cp(v):
            self.wpe(v)
            v.tensor_copy(out_ap, self.psD[:, 0:w])
        self.dv(cp)

    def stats(self, src_fn, n_tok, mu_row, rs_row, src_bf16=False, wf=None):
        c = self.cfg
        for j in range(ceil_div(n_tok, 512)):
            w = min(512, n_tok - j * 512)
            c0 = j * 512

            if not src_bf16:
                def cpb(v, w=w, c0=c0):
                    self.wpe(v)
                    if wf is not None:
                        wf(v)
                    for dc in range(c.DC):
                        v.tensor_copy(self.stg_b[:, dc * 512:dc * 512 + w],
                                      src_fn(dc)[:, c0:c0 + w])
                self.dv(cpb)

            def sqx(e, w=w, c0=c0):
                self.wpe(e)
                self.wdv(e)
                if wf is not None:
                    wf(e)
                for dc in range(c.DC):
                    e.activation(self.sqc[:, dc * 512:dc * 512 + w],
                                 src_fn(dc)[:, c0:c0 + w], AF.Square)
            self.ac(sqx)

            def mm(t, w=w, c0=c0):
                self.wdv(t)
                self.wac(t)
                for dc in range(c.DC):
                    sap = (src_fn(dc)[:, c0:c0 + w] if src_bf16
                           else self.stg_b[:, dc * 512:dc * 512 + w])
                    t.matmul(self.st_ps[0:1, 0:w], self.onesc[:, 0:1],
                             sap, start=(dc == 0), stop=(dc == c.DC - 1))
                for dc in range(c.DC):
                    t.matmul(self.st_ps[0:1, 512:512 + w], self.onesc[:, 0:1],
                             self.sqc[:, dc * 512:dc * 512 + w],
                             start=(dc == 0), stop=(dc == c.DC - 1))
            self.pe(mm)

            def fin(v, w=w, c0=c0):
                self.wpe(v)
                v.tensor_scalar(mu_row[0:1, c0:c0 + w], self.st_ps[0:1, 0:w],
                                1.0 / c.D, None, ALU.mult)
                v.tensor_scalar(self.rowb[0:1, 0:w], self.st_ps[0:1, 512:512 + w],
                                1.0 / c.D, None, ALU.mult)
            self.dv(fin)

            def fin_b(v, w=w, c0=c0):
                self.wdv(v)
                v.tensor_tensor(self.rowb[0:1, 512:512 + w],
                                mu_row[0:1, c0:c0 + w], mu_row[0:1, c0:c0 + w],
                                ALU.mult)
            self.dv(fin_b)

            def fin_c(v, w=w, c0=c0):
                self.wdv(v)
                v.tensor_tensor(self.rowb[0:1, 0:w], self.rowb[0:1, 0:w],
                                self.rowb[0:1, 512:512 + w], ALU.subtract)
            self.dv(fin_c)

            def fin_d(v, w=w, c0=c0):
                self.wdv(v)
                v.tensor_scalar(rs_row[0:1, c0:c0 + w], self.rowb[0:1, 0:w],
                                1e-5, None, ALU.add)
            self.dv(fin_d)
        def sq(e, n=n_tok):
            self.wdv(e)
            e.activation(rs_row[0:1, 0:n], rs_row[0:1, 0:n], AF.Sqrt)
        self.ac(sq)

        def rec(v, n=n_tok):
            self.wac(v)
            v.reciprocal(rs_row[0:1, 0:n], rs_row[0:1, 0:n])
        self.dv(rec)

    def build(self):
        self.prologue()
        with contextlib.ExitStack() as ph:
            self.alloc_attn(ph)
            for b in range(self.cfg.NB):
                self.attn_batch(b)
                if self.upto in ("ln1", "grp", "load", "stats"):
                    break
            self.flush_items()
        if self.upto in ("attn", "ln1", "grp", "load", "stats"):
            self.sy(lambda e: self.wait_all(e))
            self.es.close()
            return self.nc
        with contextlib.ExitStack() as ph:
            self.alloc_ffn(ph)
            for b in range(self.cfg.NB):
                self.ffn_batch(b)
                if self.upto in ("e3chk", "grp2"):
                    break
            self.flush_items()
        if self.upto in ("ffn", "e3chk", "grp2"):
            self.sy(lambda e: self.wait_all(e))
            self.es.close()
            return self.nc
        with contextlib.ExitStack() as ph:
            self.alloc_moe(ph)
            for b in range(self.cfg.NB):
                self.moe_batch(b)
            self.flush_items()
        self.sy(lambda e: self.wait_all(e))
        self.es.close()
        return self.nc

    def prologue(self):
        L = self

        def loads(g):
            g.dma_start(out=L.idf[:, :], in_=L.identf).then_inc(L.s_gd.inc(16), 16)
            g.dma_start(out=L.idb[:, :], in_=L.identf).then_inc(L.s_gd.inc(16), 16)
            g.dma_start(out=L.am0[:, :], in_=L.amask0).then_inc(L.s_gd.inc(16), 16)
            g.dma_start(out=L.am1[:, :], in_=L.amask1).then_inc(L.s_gd.inc(16), 16)
            g.dma_start(out=L.fm0[:, :], in_=L.fmask0).then_inc(L.s_gd.inc(16), 16)
            g.dma_start(out=L.sfo[:, :], in_=L.sufones).then_inc(L.s_gd.inc(16), 16)
            g.dma_start(out=L.lones[:, :], in_=L.lonesr).then_inc(L.s_gd.inc(16), 16)
            g.dma_start(out=L.bqkv_sb[:, :], in_=L.bqkvT).then_inc(L.s_gd.inc(16), 16)
            g.dma_start(out=L.bsw_sb[:, :], in_=L.bswr).then_inc(L.s_gd.inc(16), 16)
            g.memset(L.onesc[:, :], 1.0)
        self.gp(loads, inc=True)
        self.onescf = self.sbuf("onescf", 128, 1, F32)
        self.onesrf = self.sbuf("onesrf", 128, 128, F32)
        self.gp(lambda g: g.memset(L.onescf[:, :], 1.0), inc=True)
        self.gp(lambda g: g.memset(L.onesrf[:, :], 1.0), inc=True)
        self.barrier()

        def mm(t):
            t.matmul(L.psA[:, 0:L.cfg.E], L.onesrf[0:1, 0:128], L.bsw_sb[0:1, :],
                     start=True, stop=True)
        self.pe(mm)

        def cp(v):
            L.wpe(v)
            v.tensor_copy(L.bswb[:, :], L.psA[:, 0:L.cfg.E])
        self.dv(cp)
        self.barrier()

    def alloc_attn(self, ph):
        nc, c = self.nc, self.cfg
        en = ph.enter_context
        self.x_sb = en(nc.sbuf_tensor("x_sb", [128, c.DC * c.S], F32))
        self.xb_sb = en(nc.sbuf_tensor("xb_sb", [128, c.DC * c.S], BF16))
        self.wg = en(nc.sbuf_tensor("wg", [128, c.DC * c.WS], BF16))
        self.stg_b = en(nc.sbuf_tensor("stg_b", [128, c.DC * 512], BF16))
        self.sqc = en(nc.sbuf_tensor("sqc", [128, c.DC * 512], BF16))
        self.aoj = en(nc.sbuf_tensor("aoj", [128, c.DC * 512], BF16))
        GA = c.GA
        self.kbuf = en(nc.sbuf_tensor("kbuf", [128, (GA // 2) * c.N + 2], BF16))
        self.vbuf = en(nc.sbuf_tensor("vbuf", [128, (c.TT + 1) * GA * 65], BF16))
        self.qbuf = en(nc.sbuf_tensor("qbuf", [128, (GA // 2) * c.QCW], BF16))
        self.expS = en(nc.sbuf_tensor("expS", [128, 2 * c.QCW], BF16))
        self.zrow = en(nc.sbuf_tensor("zrow", [128, 512], F32))
        self.zbb = en(nc.sbuf_tensor("zbb", [128, 512], F32))
        self.mvb = en(nc.sbuf_tensor("mvb", [128, 1024], F32))
        self.gq = en(nc.sbuf_tensor("gq", [128, GA], BF16))
        self.gcol = en(nc.sbuf_tensor("gcol", [128, c.DC], BF16))

    def ln_apply(self, mu_row, rs_row, src_sb, dst_sb, dst_dtype_note=None,
                 residual_sb=None):
        c, L = self.cfg, self
        for j in range(c.QC):
            c0 = j * 512
            self.row_bcast(rs_row[0:1, c0:c0 + 512], 512, L.zbb[:, 0:512])
            self.row_bcast(mu_row[0:1, c0:c0 + 512], 512, L.zrow[:, 0:512])

            def ap(v, c0=c0):
                L.wdv(v)
                for dc in range(c.DC):
                    sl = slice(dc * c.S + c0, dc * c.S + c0 + 512)
                    v.tensor_tensor(L.mvb[:, 0:512], src_sb[:, sl],
                                    L.zrow[:, 0:512], ALU.subtract)
                    if residual_sb is None:
                        v.tensor_tensor(dst_sb[:, sl], L.mvb[:, 0:512],
                                        L.zbb[:, 0:512], ALU.mult)
                    else:
                        v.tensor_tensor(L.mvb[:, 512:1024], L.mvb[:, 0:512],
                                        L.zbb[:, 0:512], ALU.mult)
                        v.tensor_tensor(dst_sb[:, sl], residual_sb[:, sl],
                                        L.mvb[:, 512:1024], ALU.add)
            self.dv(ap)

    def spill_fm(self, sb_src, hbm_dst, b, ncols, engine="gp"):
        c, L = self.cfg, self

        def go(g):
            L.wdv(g)
            for dc in range(c.DC):
                g.dma_start(out=hbm_dst[b, dc, :, 0:ncols],
                            in_=sb_src[:, dc * ncols:(dc + 1) * ncols]) \
                 .then_inc((L.s_gd if engine == "gp" else L.s_hd).inc(16), 16)
        if engine == "gp":
            self.gp(go)
        else:
            self.sy(go)

    def _dbg_spill(self, b):
        L = self
        c = self.cfg
        S, DC = c.S, c.DC

        def spdbg(sy, b=b):
            self.wait_all(sy)
            for dc in range(DC):
                sy.dma_start(out=L.x1_out[b, dc],
                             in_=L.x_sb[:, dc * S:(dc + 1) * S]) \
                  .then_inc(L.s_hd.inc(16), 16)
            sy.wait_ge(L.s_hd.h, L.s_hd.v)
        self.sy(spdbg)

    def attn_batch(self, b):
        c, L = self.cfg, self
        S, D, DC, QC, TT, GA = c.S, c.D, c.DC, c.QC, c.TT, c.GA

        def ldx(sy):
            for dc in range(DC):
                sy.dma_start(out=L.x_sb[:, dc * S:(dc + 1) * S],
                             in_=L.xT[b, dc]).then_inc(L.s_hd.inc(16), 16)
        self.sy(ldx)
        if self.upto == "load":
            self._dbg_spill(b)
            return
        self.stats(lambda dc: L.x_sb[:, dc * S:(dc + 1) * S], S, L.mu1, L.rs1,
                   wf=lambda e: L.whd(e))
        self.barrier()
        if self.upto == "stats":
            self._dbg_spill(b)
            return

        self.ln_apply(L.mu1, L.rs1, L.x_sb, L.xb_sb)

        def gl(g):
            for dc in range(DC):
                g.dma_start(out=L.gcol[:, dc:dc + 1],
                            in_=L.gtok[dc].rearrange("(p o) -> p o", o=1)) \
                 .then_inc(L.s_gd.inc(16), 16)
        self.gp(gl)
        self.barrier()

        if self.upto == "ln1":
            def spdbg(sy, b=b):
                self.wait_all(sy)
                for dc in range(DC):
                    sy.dma_start(out=L.x1_out[b, dc],
                                 in_=L.x_sb[:, dc * S:(dc + 1) * S]) \
                      .then_inc(L.s_hd.inc(16), 16)
            self.sy(spdbg)
            return
        for grp in range(c.H // GA):
            h0 = grp * GA
            self.attn_group(b, h0)
        self.barrier()
        if self.upto == "grp":
            def spdbg(sy, b=b):
                self.wait_all(sy)
                for dc in range(DC):
                    sy.dma_start(out=L.x1_out[b, dc],
                                 in_=L.x_sb[:, dc * S:(dc + 1) * S]) \
                      .then_inc(L.s_hd.inc(16), 16)
            self.sy(spdbg)
            return
        self.attn_tail(b)

    def attn_group(self, b, h0):
        c, L = self.cfg, self
        S, D, DC, QC, TT, GA = c.S, c.D, c.DC, c.QC, c.TT, c.GA
        N = c.N
        WS = c.WS
        G64 = GA * 64

        def ldw(g, h0=h0):
            L.wpe(g)
            for dc in range(DC):
                for ti, off in enumerate((h0 * 64, D + h0 * 64,
                                          2 * D + h0 * 64)):
                    g.dma_start(
                        out=L.wg[:, dc * WS + ti * G64:dc * WS + (ti + 1) * G64],
                        in_=L.qkv_wT[dc][:, off:off + G64]) \
                     .then_inc(L.s_gd.inc(16), 16)
        self.gp(ldw)

        def mmg(t, h0=h0):
            L.wgd(t)
            L.wdv(t)
            for p in range(GA // 2):
                for dc in range(DC):
                    t.matmul(L.psC[:, p:p + 1],
                             L.wg[:, dc * WS + G64 + p * 128:dc * WS + G64 + (p + 1) * 128],
                             L.gcol[:, dc:dc + 1],
                             start=(dc == 0), stop=(dc == DC - 1))
                for dc in range(DC):
                    t.matmul(L.psC[:, GA // 2 + p:GA // 2 + p + 1],
                             L.wg[:, dc * WS + 2 * G64 + p * 128:dc * WS + 2 * G64 + (p + 1) * 128],
                             L.gcol[:, dc:dc + 1],
                             start=(dc == 0), stop=(dc == DC - 1))
        self.pe(mmg)

        def cpgq(v):
            L.wpe(v)
            v.tensor_copy(L.gq[:, 0:GA], L.psC[:, 0:GA])
        self.dv(cpgq)

        for p in range(GA // 2):
            for j0 in range(0, QC, 2):
                js = [j for j in (j0, j0 + 1) if j < QC]
                kb = [L.psA, L.psB]

                def mm(t, p=p, js=js):
                    L.wdv(t)
                    for ix, j in enumerate(js):
                        for dc in range(DC):
                            t.matmul(kb[ix][:, 0:512],
                                     L.wg[:, dc * WS + G64 + p * 128:dc * WS + G64 + (p + 1) * 128],
                                     L.xb_sb[:, dc * S + j * 512:dc * S + (j + 1) * 512],
                                     start=(dc == 0), stop=(dc == DC - 1))
                self.pe(mm)

                def cp(v, p=p, js=js):
                    L.wpe(v)
                    for ix, j in enumerate(js):
                        v.tensor_copy(L.kbuf[:, p * N + j * 512:p * N + (j + 1) * 512],
                                      kb[ix][:, 0:512])
                self.dv(cp)
            def cpg(v, p=p):
                v.tensor_copy(L.kbuf[:, p * N + S:p * N + S + 1],
                              L.gq[:, p:p + 1])
            self.dv(cpg)

        for t0 in range(0, TT, 2):
            tts = [tt for tt in (t0, t0 + 1) if tt < TT]

            def mm(t, tts=tts, h0=h0):
                L.wdv(t)
                for ix, tt in enumerate(tts):
                    for dc in range(DC):
                        t.matmul(L.psB[:, ix * G64:ix * G64 + G64],
                                 L.xb_sb[:, dc * S + tt * 128:dc * S + (tt + 1) * 128],
                                 L.wg[:, dc * WS + 2 * G64:dc * WS + 3 * G64],
                                 start=(dc == 0), stop=(dc == DC - 1))
            self.pe(mm)

            def cp(v, tts=tts):
                L.wpe(v)
                for ix, tt in enumerate(tts):
                    for i in range(GA):
                        base = (tt * GA + i) * 65
                        v.tensor_copy(L.vbuf[:, base:base + 64],
                                      L.psB[:, ix * G64 + i * 64:ix * G64 + (i + 1) * 64])
                    v.memset(L.vbuf[:, tt * GA * 65:(tt + 1) * GA * 65].rearrange(
                        "p (i c) -> p i c", i=GA)[:, :, 64:65].rearrange("p i c -> p (i c)"), 1.0)
            self.dv(cp)
        for i in range(GA):
            p_, sub = divmod(i, 2)
            oc = GA // 2 + p_
            r0 = sub * 64

            def tp(t, i=i, oc=oc, r0=r0):
                L.wdv(t)
                t.transpose(L.tp_ps[0:1, 0:64], L.gq[r0:r0 + 64, oc:oc + 1],
                            L.idb[r0:r0 + 64, r0:r0 + 64])
            self.pe(tp)

            def cp(v, i=i):
                L.wpe(v)
                base = (TT * GA + i) * 65
                v.tensor_copy(L.vbuf[0:1, base:base + 64], L.tp_ps[0:1, 0:64])
                v.memset(L.vbuf[0:1, base + 64:base + 65], 1.0)
            self.dv(cp)
        self.barrier()

        for qc in range(QC):
            qb = [L.psA, L.psB]

            def mmq(t, qc=qc):
                L.wdv(t)
                for p in range(GA // 2):
                    for dc in range(DC):
                        t.matmul(qb[p][:, 0:512],
                                 L.wg[:, dc * WS + p * 128:dc * WS + (p + 1) * 128],
                                 L.xb_sb[:, dc * S + qc * 512:dc * S + (qc + 1) * 512],
                                 start=(dc == 0), stop=(dc == DC - 1))
            self.pe(mmq)

            def cpq(v):
                L.wpe(v)
                for p in range(GA // 2):
                    v.tensor_scalar(L.qbuf[:, p * 512:(p + 1) * 512],
                                    qb[p][:, 0:512], 0.125, None, ALU.mult)
            self.dv(cpq)

            ktn = min(TT, 4 * qc + 5)
            for i in range(GA):
                p, sub = divmod(i, 2)
                r0 = sub * 64
                first = [True]
                for k2 in range(0, ktn, 2):
                    kts = [kt for kt in (k2, k2 + 1) if kt < ktn]
                    qlos = [max(0, kt - 4 * qc - 1) * 128 for kt in kts]
                    banks = [L.psB, L.psD]

                    def sc_mm(t, p=p, r0=r0, kts=kts, qlos=qlos, qc=qc):
                        L.wdv(t)
                        for ix, kt in enumerate(kts):
                            t.matmul(banks[ix][:, qlos[ix]:512],
                                     L.kbuf[r0:r0 + 64, p * N + kt * 128:p * N + (kt + 1) * 128],
                                     L.qbuf[r0:r0 + 64, p * 512 + qlos[ix]:p * 512 + 512],
                                     start=True, stop=True)
                    self.pe(sc_mm)

                    def ex(e, kts=kts, qlos=qlos):
                        L.wpe(e)
                        for ix in range(len(kts)):
                            e.activation(L.expS[:, ix * 512 + qlos[ix]:ix * 512 + 512],
                                         banks[ix][:, qlos[ix]:512], AF.Exp)
                    self.ac(ex)

                    def msk(v, kts=kts, qc=qc):
                        L.wac(v)
                        for ix, kt in enumerate(kts):
                            qs0 = kt - 4 * qc
                            qs1 = kt - 4 * qc - 1
                            if 0 <= qs0 < 4:
                                v.tensor_tensor(
                                    L.expS[:, ix * 512 + qs0 * 128:ix * 512 + (qs0 + 1) * 128],
                                    L.expS[:, ix * 512 + qs0 * 128:ix * 512 + (qs0 + 1) * 128],
                                    L.am0[:, :], ALU.mult)
                            if 0 <= qs1 < 4:
                                v.tensor_tensor(
                                    L.expS[:, ix * 512 + qs1 * 128:ix * 512 + (qs1 + 1) * 128],
                                    L.expS[:, ix * 512 + qs1 * 128:ix * 512 + (qs1 + 1) * 128],
                                    L.am1[:, :], ALU.mult)
                    self.dv(msk)

                    def av(t, i=i, kts=kts, qlos=qlos, first=first):
                        L.wdv(t)
                        for ix, kt in enumerate(kts):
                            t.matmul(L.psC[0:65, qlos[ix]:512],
                                     L.vbuf[:, (kt * GA + i) * 65:(kt * GA + i + 1) * 65],
                                     L.expS[:, ix * 512 + qlos[ix]:ix * 512 + 512],
                                     start=first[0], stop=False)
                            first[0] = False
                    self.pe(av, inc=False)

                def sc_g(t, p=p, r0=r0):
                    t.matmul(L.psB[0:1, 0:512],
                             L.kbuf[r0:r0 + 64, p * N + S:p * N + S + 1],
                             L.qbuf[r0:r0 + 64, p * 512:p * 512 + 512],
                             start=True, stop=True)
                self.pe(sc_g)

                def ex_g(e):
                    L.wpe(e)
                    e.activation(L.expS[0:1, 0:512], L.psB[0:1, 0:512], AF.Exp)
                self.ac(ex_g)

                def av_g(t, i=i):
                    L.wac(t)
                    t.matmul(L.psC[0:65, 0:512],
                             L.vbuf[0:1, (TT * GA + i) * 65:(TT * GA + i + 1) * 65],
                             L.expS[0:1, 0:512],
                             start=False, stop=True)
                self.pe(av_g)

                def rz(v):
                    L.wpe(v)
                    v.reciprocal(L.zrow[64:65, 0:512], L.psC[64:65, 0:512])
                self.dv(rz)

                def zb(t):
                    L.wdv(t)
                    t.matmul(L.psB[0:64, 0:512], L.onesrf[64:65, 0:64],
                             L.zrow[64:65, 0:512], start=True, stop=True)
                self.pe(zb)

                def ev(v, i=i, qc=qc, h0=h0, b=b):
                    L.wpe(v)
                    L.wgd(v)
                    v.tensor_copy(L.zbb[0:64, 0:512], L.psB[0:64, 0:512])
                    hh = h0 + i
                    r0o = (hh % 2) * 64
                    v.scalar_tensor_tensor(
                        L.aoj[r0o:r0o + 64, i * 512:(i + 1) * 512],
                        L.psC[0:64, 0:512], 1.0, L.zbb[0:64, 0:512],
                        ALU.mult, ALU.mult)
                self.dv(ev)

                def sp(g, i=i, qc=qc, h0=h0, b=b):
                    L.wdv(g)
                    hh = h0 + i
                    dc = hh // 2
                    r0o = (hh % 2) * 64
                    g.dma_start(
                        out=L.ao_hbm[b, dc, r0o:r0o + 64, qc * 512:(qc + 1) * 512],
                        in_=L.aoj[r0o:r0o + 64, i * 512:(i + 1) * 512]) \
                     .then_inc(L.s_gd.inc(16), 16)
                self.gp(sp)
                self.pe(lambda t: L.wdv(t), inc=False)

    def attn_tail(self, b):
        c, L = self.cfg, self
        S, D, DC, QC = c.S, c.D, c.DC, c.QC
        WS = c.WS

        def ldw(g):
            for dc in range(DC):
                g.dma_start(out=L.wg[:, dc * WS:dc * WS + D],
                            in_=L.proj_wT[dc]).then_inc(L.s_gd.inc(16), 16)
            g.dma_start(out=L.zrow[:, 0:DC], in_=L.proj_bT) \
             .then_inc(L.s_gd.inc(16), 16)
        self.gp(ldw)
        self.barrier()

        for j in range(QC):
            def lda(g, j=j, b=b):
                L.wpe(g)
                for dc in range(DC):
                    g.dma_start(out=L.aoj[:, dc * 512:(dc + 1) * 512],
                                in_=L.ao_hbm[b, dc, :, j * 512:(j + 1) * 512]) \
                     .then_inc(L.s_gd.inc(16), 16)
            self.gp(lda)
            pb = [L.psA, L.psB]
            for oc0 in range(0, DC, 2):
                ocs = [oc for oc in (oc0, oc0 + 1) if oc < DC]

                def mm(t, j=j, ocs=ocs):
                    L.wdv(t)
                    L.wgd(t)
                    for ix, oc in enumerate(ocs):
                        for dc in range(DC):
                            t.matmul(pb[ix][:, 0:512],
                                     L.wg[:, dc * WS + oc * 128:dc * WS + (oc + 1) * 128],
                                     L.aoj[:, dc * 512:(dc + 1) * 512],
                                     start=(dc == 0), stop=(dc == DC - 1))
                self.pe(mm)

                def cp(v, j=j, ocs=ocs):
                    L.wpe(v)
                    for ix, oc in enumerate(ocs):
                        v.tensor_scalar(L.xb_sb[:, oc * S + j * 512:oc * S + (j + 1) * 512],
                                        pb[ix][:, 0:512], L.zrow[:, oc:oc + 1],
                                        None, ALU.add)
                self.dv(cp)
        self.barrier()

        self.stats(lambda dc: L.xb_sb[:, dc * S:(dc + 1) * S], S, L.mu3, L.rs3,
                   src_bf16=True)
        self.barrier()
        self.ln_apply(L.mu3, L.rs3, L.xb_sb, L.x_sb, residual_sb=L.x_sb)
        self.barrier()
        def spx(sy, b=b):
            for dc in range(DC):
                sy.dma_start(out=L.x1_out[b, dc],
                             in_=L.x_sb[:, dc * S:(dc + 1) * S]) \
                  .then_inc(L.s_hd.inc(16), 16)
        self.sy(spx)

        self.stats(lambda dc: L.x_sb[:, dc * S:(dc + 1) * S], S, L.mu3, L.rs3)
        self.barrier()

        def sp3(g, b=b):
            g.dma_start(out=L.ln3_out[b, 0:1, :], in_=L.mu3[0:1, :]) \
             .then_inc(L.s_gd.inc(16), 16)
            g.dma_start(out=L.ln3_out[b, 1:2, :], in_=L.rs3[0:1, :]) \
             .then_inc(L.s_gd.inc(16), 16)
        self.gp(sp3)

        self.ln_apply(L.mu3, L.rs3, L.x_sb, L.xb_sb)
        self.barrier()
        self.spill_fm(L.xb_sb, L.e3_hbm, b, S)
        self.barrier()

    def alloc_ffn(self, ph):
        nc, c = self.nc, self.cfg
        en = ph.enter_context
        GF = c.GA
        self.GF = GF
        self.e_sb = en(nc.sbuf_tensor("e_sb", [128, c.DC * c.S], BF16))
        self.wg2 = en(nc.sbuf_tensor("wg2", [128, c.DC * c.WS2], BF16))
        self.kbuf = en(nc.sbuf_tensor("kbuf2", [128, (GF // 2) * c.S], BF16))
        self.vue = en(nc.sbuf_tensor("vue", [128, c.TT * GF * 73], BF16))
        self.vsuf = en(nc.sbuf_tensor("vsuf", [128, GF * c.S], BF16))
        self.tsuf = en(nc.sbuf_tensor("tsuf", [128, 2 * GF], F32))
        self.qbuf = en(nc.sbuf_tensor("qbuf2", [128, (GF // 2) * c.QCW], BF16))
        self.expS = en(nc.sbuf_tensor("expS2", [128, 2 * c.QCW], BF16))
        self.av_sb = en(nc.sbuf_tensor("av_sb", [128, c.QCW], BF16))
        self.hid_tm = en(nc.sbuf_tensor("hid_tm", [128, c.TT * c.D], BF16))
        self.lacc = en(nc.sbuf_tensor("lacc", [128, c.TT * c.E], F32))
        self.rzcol = en(nc.sbuf_tensor("rzcol", [128, 2], F32))
        self.bvcol = en(nc.sbuf_tensor("bvcol", [128, 1], BF16))
        self.M2 = en(nc.sbuf_tensor("M2", [128, c.DC * c.H * c.E], BF16))
        self.ubias = en(nc.sbuf_tensor("ubias", [c.E, c.H + 2], F32))
        self.bvb = en(nc.sbuf_tensor("bvb", [128, c.H * 64], F32))
        self.bvr_sb = en(nc.sbuf_tensor("bvr_sb", [1, c.H * 64], F32))
        self.zrow = en(nc.sbuf_tensor("zrow2", [128, 512], F32))
        self.mvb = en(nc.sbuf_tensor("mvb2", [128, 512], F32))
        self.iota8 = en(nc.sbuf_tensor("iota8", [128, c.E], F32))
        self.iota8m = en(nc.sbuf_tensor("iota8m", [128, c.E], F32))
        self.iotap = en(nc.sbuf_tensor("iotap", [128, 1], F32))
        self.probs_sb = en(nc.sbuf_tensor("probs_sb", [128, c.TT], F32))
        self.routes_sb = en(nc.sbuf_tensor("routes_sb", [128, c.TT * 8], U32))
        self.rf32_sb = en(nc.sbuf_tensor("rf32_sb", [128, c.TT], F32))
        self.max8 = en(nc.sbuf_tensor("max8", [128, 8], F32))
        self.mi8 = en(nc.sbuf_tensor("mi8", [128, 8], U32))
        self.rscr = en(nc.sbuf_tensor("rscr", [128, 10], F32))
        self.ohf = en(nc.sbuf_tensor("ohf", [128, c.E], F32))
        self.ohb = en(nc.sbuf_tensor("ohb", [128, c.E], BF16))
        self.ohT = en(nc.sbuf_tensor("ohT", [c.E, c.S], F32))
        self.scan = en(nc.sbuf_tensor("scan", [c.E, c.S], F32))
        self.pos_sb = en(nc.sbuf_tensor("pos_sb", [128, c.TT], F32))
        self.cntc = en(nc.sbuf_tensor("cntc", [c.E, 4], F32))
        self.cnt_u = en(nc.sbuf_tensor("cnt_u", [c.E, 1], U32))
        self.base_r = en(nc.sbuf_tensor("base_r", [1, c.E], F32))
        self.baseb = en(nc.sbuf_tensor("baseb", [128, c.E], F32))
        self.slotu = en(nc.sbuf_tensor("slotu", [128, 1], U32))
        self.slotu2 = en(nc.sbuf_tensor("slotu2", [128, 2], U32))
        self.tokid = en(nc.sbuf_tensor("tokid", [128, 1], U32))
        self.sentc = en(nc.sbuf_tensor("sentc", [128, c.NCH], U32))
        L = self

        def pro(g):
            g.iota(L.iota8[:, :], [[1, c.E]], base=0, channel_multiplier=0,
                   allow_small_or_imprecise_dtypes=True)
            g.iota(L.iota8m[:, :], [[1, c.E]], base=-c.E, channel_multiplier=0,
                   allow_small_or_imprecise_dtypes=True)
            g.iota(L.iotap[:, :], [[0, 1]], base=0, channel_multiplier=1,
                   allow_small_or_imprecise_dtypes=True)
            g.iota(L.iotaNCH[:, :], [[128, c.NCH]], base=0, channel_multiplier=0,
                   allow_small_or_imprecise_dtypes=True)
            g.memset(L.sentc[:, :], c.SENT)
            g.memset(L.lacc[:, :], 0.0)
            g.dma_start(out=L.bvr_sb[:, :], in_=L.bvr).then_inc(L.s_gd.inc(16), 16)
            for r in range(c.H * 64 // 128):
                g.dma_start(out=L.hid_tm[:, r * c.D:(r + 1) * c.D],
                            in_=L.Wv_nat[r * 128:(r + 1) * 128, :]) \
                 .then_inc(L.s_gd.inc(16), 16)
            for dc in range(c.DC):
                g.dma_start(out=L.vsuf[:, dc * c.E:(dc + 1) * c.E],
                            in_=L.WswT[dc * 128:(dc + 1) * 128, :]) \
                 .then_inc(L.s_gd.inc(16), 16)
        self.gp(pro, inc=True)
        self.barrier()

        for j in range(ceil_div(c.H * 64, 512)):
            w = min(512, c.H * 64 - j * 512)
            self.row_bcast(L.bvr_sb[0:1, j * 512:j * 512 + w], w,
                           L.bvb[:, j * 512:j * 512 + w])
        for h in range(c.H):
            r, sub = divmod(h, 2)
            rb = sub * 64
            for dc in range(c.DC):
                def mm(t, h=h, r=r, rb=rb, dc=dc):
                    L.wdv(t)
                    t.matmul(L.psA[:, 0:c.E],
                             L.hid_tm[rb:rb + 64, r * c.D + dc * 128:r * c.D + (dc + 1) * 128],
                             L.vsuf[rb:rb + 64, (dc if False else ((h * 64) // 128) * c.E):((h * 64) // 128) * c.E + c.E]
                             if False else L.vsuf[rb:rb + 64, ((h * 64) // 128) * c.E:((h * 64) // 128 + 1) * c.E],
                             start=True, stop=True)
                self.pe(mm)

                def cp(v, h=h, dc=dc):
                    L.wpe(v)
                    v.tensor_copy(L.M2[:, (dc * c.H + h) * c.E:(dc * c.H + h + 1) * c.E],
                                  L.psA[:, 0:c.E])
                self.dv(cp)
            def cpb(v, h=h):
                v.tensor_copy(L.bvcol[:, 0:1],
                              L.bqkv_sb[:, 2 * (c.H // 2) + h // 2:2 * (c.H // 2) + h // 2 + 1])
            self.dv(cpb)

            def mmu(t, h=h, rb=rb):
                L.wdv(t)
                t.matmul(L.psB[0:c.E, 0:1],
                         L.vsuf[rb:rb + 64, ((h * 64) // 128) * c.E:((h * 64) // 128 + 1) * c.E],
                         L.bvcol[rb:rb + 64, 0:1],
                         start=True, stop=True)
            self.pe(mmu)

            def cpu_(v, h=h):
                L.wpe(v)
                v.tensor_copy(L.ubias[:, h:h + 1], L.psB[0:c.E, 0:1])
            self.dv(cpu_)
        self.barrier()
        def us(v):
            v.reduce_sum(L.ubias[:, c.H:c.H + 1], L.ubias[:, 0:c.H], axis=AX)
        self.dv(us)

        def ut(t):
            L.wdv(t)
            t.transpose(L.psB[0:1, 0:c.E], L.ubias[0:c.E, c.H:c.H + 1],
                        L.idf[0:c.E, 0:c.E])
        self.pe(ut)

        def uc(v):
            L.wpe(v)
            v.tensor_copy(L.rowb[0:1, 0:c.E], L.psB[0:1, 0:c.E])
        self.dv(uc)

        def ub(t):
            L.wdv(t)
            t.matmul(L.psB[:, 0:c.E], L.onesrf[0:1, 0:128], L.rowb[0:1, 0:c.E],
                     start=True, stop=True)
        self.pe(ub)

        def ua(v):
            L.wpe(v)
            v.tensor_tensor(L.bswb[:, :], L.bswb[:, :], L.psB[:, 0:c.E], ALU.add)
        self.dv(ua)
        self.barrier()

    def ffn_batch(self, b):
        c, L = self.cfg, self
        S, D, DC, QC, TT, E = c.S, c.D, c.DC, c.QC, c.TT, c.E
        GF = self.GF
        HP2 = c.H // 2

        def lde(g, b=b):
            for dc in range(DC):
                g.dma_start(out=L.e_sb[:, dc * S:(dc + 1) * S],
                            in_=L.e3_hbm[b, dc]).then_inc(L.s_gd.inc(16), 16)
        self.gp(lde)
        self.dv(lambda v: v.memset(L.lacc[:, :], 0.0))
        self.barrier()
        if self.upto == "e3chk":
            def dump(g, b=b):
                for dc in range(DC):
                    g.dma_start(
                        out=L.hid_out[b].rearrange("s d -> (s d)").rearrange(
                            "(c p f) -> c p f", c=DC, p=128)[dc],
                        in_=L.e_sb[:, dc * S:(dc + 1) * S]) \
                     .then_inc(L.s_gd.inc(16), 16)
            self.gp(dump)
            self.barrier()
            return

        for grp in range(c.H // GF):
            self.ffn_group(b, grp * GF)
        self.barrier()
        if self.upto == "grp2":
            self.ffn_route(b)
            return
        self.ffn_route(b)

    def ffn_group(self, b, h0):
        c, L = self.cfg, self
        S, D, DC, QC, TT, E = c.S, c.D, c.DC, c.QC, c.TT, c.E
        GF = self.GF
        HP2 = c.H // 2
        WS2 = c.WS2
        G64 = GF * 64

        def ldw(g, h0=h0):
            L.wpe(g)
            for dc in range(DC):
                for ti, off in enumerate((h0 * 64, D + h0 * 64,
                                          2 * D + h0 * 64)):
                    g.dma_start(
                        out=L.wg2[:, dc * WS2 + ti * G64:dc * WS2 + (ti + 1) * G64],
                        in_=L.WqkvT[dc][:, off:off + G64]) \
                     .then_inc(L.s_gd.inc(16), 16)
        self.gp(ldw)

        for p in range(GF // 2):
            hh = h0 + 2 * p
            bcol = HP2 + hh // 2
            kb2 = [L.psA, L.psB]
            for j0 in range(0, QC, 2):
                js = [j for j in (j0, j0 + 1) if j < QC]

                def mm(t, p=p, js=js):
                    L.wdv(t)
                    L.wgd(t)
                    for ix, j in enumerate(js):
                        for dc in range(DC):
                            t.matmul(kb2[ix][:, 0:512],
                                     L.wg2[:, dc * WS2 + G64 + p * 128:dc * WS2 + G64 + (p + 1) * 128],
                                     L.e_sb[:, dc * S + j * 512:dc * S + (j + 1) * 512],
                                     start=(dc == 0), stop=(dc == DC - 1))
                self.pe(mm)

                def cp(v, p=p, js=js, bcol=bcol):
                    L.wpe(v)
                    for ix, j in enumerate(js):
                        v.tensor_scalar(L.kbuf[:, p * S + j * 512:p * S + (j + 1) * 512],
                                        kb2[ix][:, 0:512],
                                        L.bqkv_sb[:, bcol:bcol + 1], None, ALU.add)
                self.dv(cp)

        for t0 in range(0, TT, 2):
            tts = [tt for tt in (t0, t0 + 1) if tt < TT]

            def mmv(t, tts=tts, h0=h0):
                L.wdv(t)
                for ix, tt in enumerate(tts):
                    for dc in range(DC):
                        t.matmul(L.psB[:, ix * G64:ix * G64 + G64],
                                 L.e_sb[:, dc * S + tt * 128:dc * S + (tt + 1) * 128],
                                 L.wg2[:, dc * WS2 + 2 * G64:dc * WS2 + 3 * G64],
                                 start=(dc == 0), stop=(dc == DC - 1))
                    for dc in range(DC):
                        t.matmul(L.psC[:, ix * GF * E:(ix + 1) * GF * E],
                                 L.e_sb[:, dc * S + tt * 128:dc * S + (tt + 1) * 128],
                                 L.M2[:, (dc * c.H + h0) * E:(dc * c.H + h0 + GF) * E],
                                 start=(dc == 0), stop=(dc == DC - 1))
            self.pe(mmv)

            def cpv(v, tts=tts, h0=h0):
                L.wpe(v)
                for ix, tt in enumerate(tts):
                    for i in range(GF):
                        base = (tt * GF + i) * 73
                        v.tensor_tensor(L.vue[:, base:base + 64],
                                        L.psB[:, ix * G64 + i * 64:ix * G64 + (i + 1) * 64],
                                        L.bvb[:, (h0 + i) * 64:(h0 + i + 1) * 64],
                                        ALU.add)
                        v.memset(L.vue[:, base + 64:base + 65], 1.0)
                        v.tensor_copy(L.vue[:, base + 65:base + 73],
                                      L.psC[:, ix * GF * E + i * E:ix * GF * E + (i + 1) * E])
            self.dv(cpv)
        self.barrier()

        for i in range(GF):
            def z0(v, i=i):
                v.memset(L.tsuf[:, 2 * i:2 * i + 1], 0.0)
            self.dv(z0)
        for tt in range(TT - 1, -1, -1):
            for i0 in range(0, GF, 2):
                iis = [i for i in (i0, i0 + 1) if i < GF]

                def mm(t, tt=tt, iis=iis):
                    L.wdv(t)
                    for ix, i in enumerate(iis):
                        t.matmul(L.psD[0:73, ix * 129:ix * 129 + 129],
                                 L.vue[:, (tt * GF + i) * 73:(tt * GF + i + 1) * 73],
                                 L.sfo[:, :], start=True, stop=True)
                self.pe(mm)

                def fin(v, tt=tt, iis=iis):
                    L.wpe(v)
                    for ix, i in enumerate(iis):
                        v.tensor_scalar(L.vsuf[0:73, i * S + tt * 128:i * S + (tt + 1) * 128],
                                        L.psD[0:73, ix * 129:ix * 129 + 128],
                                        L.tsuf[0:73, 2 * i:2 * i + 1], None, ALU.add)
                self.dv(fin)

                def fint(v, tt=tt, iis=iis):
                    L.wdv(v)
                    for ix, i in enumerate(iis):
                        v.tensor_tensor(L.tsuf[0:73, 2 * i:2 * i + 1],
                                        L.tsuf[0:73, 2 * i:2 * i + 1],
                                        L.psD[0:73, ix * 129 + 128:ix * 129 + 129],
                                        ALU.add)
                self.dv(fint)
        self.barrier()

        for qc in range(QC):
            qb2 = [L.psA, L.psB]

            def mmq(t, qc=qc, h0=h0):
                L.wdv(t)
                for p in range(GF // 2):
                    for dc in range(DC):
                        t.matmul(qb2[p][:, 0:512],
                                 L.wg2[:, dc * WS2 + p * 128:dc * WS2 + (p + 1) * 128],
                                 L.e_sb[:, dc * S + qc * 512:dc * S + (qc + 1) * 512],
                                 start=(dc == 0), stop=(dc == DC - 1))
            self.pe(mmq)

            def cpq(v, h0=h0):
                L.wpe(v)
                for p in range(GF // 2):
                    bcol = (h0 + 2 * p) // 2
                    v.tensor_scalar(L.qbuf[:, p * 512:(p + 1) * 512],
                                    qb2[p][:, 0:512],
                                    L.bqkv_sb[:, bcol:bcol + 1],
                                    1.0 / math.sqrt(D), ALU.add, ALU.mult)
            self.dv(cpq)

            ktn = min(TT, 4 * qc + 4)
            for i in range(GF):
                p, sub = divmod(i, 2)
                r0 = sub * 64
                first = [True]
                banks2 = None
                for k2 in range(0, ktn, 2):
                    kts = [kt for kt in (k2, k2 + 1) if kt < ktn]
                    qlos = [max(0, kt - 4 * qc) * 128 for kt in kts]
                    banks2 = [L.psB, L.psD]

                    def smm(t, p=p, r0=r0, kts=kts, qlos=qlos):
                        L.wdv(t)
                        for ix, kt in enumerate(kts):
                            t.matmul(banks2[ix][:, qlos[ix]:512],
                                     L.kbuf[r0:r0 + 64, p * S + kt * 128:p * S + (kt + 1) * 128],
                                     L.qbuf[r0:r0 + 64, p * 512 + qlos[ix]:p * 512 + 512],
                                     start=True, stop=True)
                    self.pe(smm)

                    def ex(e, kts=kts, qlos=qlos):
                        L.wpe(e)
                        for ix in range(len(kts)):
                            e.activation(L.expS[:, ix * 512 + qlos[ix]:ix * 512 + 512],
                                         banks2[ix][:, qlos[ix]:512], AF.Exp)
                    self.ac(ex)

                    def msk(v, kts=kts, qc=qc):
                        L.wac(v)
                        for ix, kt in enumerate(kts):
                            qs0 = kt - 4 * qc
                            if 0 <= qs0 < 4:
                                v.tensor_tensor(
                                    L.expS[:, ix * 512 + qs0 * 128:ix * 512 + (qs0 + 1) * 128],
                                    L.expS[:, ix * 512 + qs0 * 128:ix * 512 + (qs0 + 1) * 128],
                                    L.fm0[:, :], ALU.mult)
                    self.dv(msk)

                    def av(t, i=i, kts=kts, qlos=qlos, first=first, ktn=ktn):
                        L.wdv(t)
                        for ix, kt in enumerate(kts):
                            t.matmul(L.psC[0:73, qlos[ix]:512],
                                     L.vue[:, (kt * GF + i) * 73:(kt * GF + i + 1) * 73],
                                     L.expS[:, ix * 512 + qlos[ix]:ix * 512 + 512],
                                     start=first[0], stop=(kt == ktn - 1))
                            first[0] = False
                    self.pe(av, inc=(kts[-1] == ktn - 1))

                def fin(v, i=i, qc=qc):
                    L.wpe(v)
                    v.tensor_tensor(L.av_sb[0:73, 0:512], L.psC[0:73, 0:512],
                                    L.vsuf[0:73, i * S + qc * 512:i * S + (qc + 1) * 512],
                                    ALU.add)
                self.dv(fin)
                for qs in range(4):
                    def tp(t, qs=qs):
                        L.wdv(t)
                        t.transpose(L.tp_ps[:, 0:73],
                                    L.av_sb[0:73, qs * 128:(qs + 1) * 128],
                                    L.idb[0:73, 0:73])
                    self.pe(tp)

                    def rcp(v, qs=qs):
                        L.wpe(v)
                        v.reciprocal(L.rzcol[:, 0:1], L.tp_ps[:, 64:65])
                    self.dv(rcp)

                    def nrm(v, i=i, qc=qc, qs=qs, h0=h0):
                        L.wdv(v)
                        tile = qc * 4 + qs
                        hh = h0 + i
                        v.tensor_scalar(L.hid_tm[:, tile * D + hh * 64:tile * D + hh * 64 + 64],
                                        L.tp_ps[:, 0:64],
                                        L.rzcol[:, 0:1], None, ALU.mult)
                        v.scalar_tensor_tensor(
                            L.lacc[:, tile * E:(tile + 1) * E],
                            L.tp_ps[:, 65:73], L.rzcol[:, 0:1],
                            L.lacc[:, tile * E:(tile + 1) * E],
                            ALU.mult, ALU.add)
                    self.dv(nrm)
                self.pe(lambda t: L.wdv(t), inc=False)

    def ffn_route(self, b):
        c, L = self.cfg, self
        S, D, DC, TT, E, NCH = c.S, c.D, c.DC, c.TT, c.E, c.NCH

        def sph(g, b=b):
            L.wdv(g)
            for tt in range(TT):
                g.dma_start(out=L.hid_out[b, tt * 128:(tt + 1) * 128, :],
                            in_=L.hid_tm[:, tt * D:(tt + 1) * D]) \
                 .then_inc(L.s_gd.inc(16), 16)
        self.gp(sph)

        for tt in range(TT):
            def fin(v, tt=tt):
                v.tensor_tensor(L.lacc[:, tt * E:(tt + 1) * E],
                                L.lacc[:, tt * E:(tt + 1) * E],
                                L.bswb[:, :], ALU.add)
            self.dv(fin)

            def fin_m(v, tt=tt):
                L.wdv(v)
                v.reduce_max(L.rscr[:, 0:1], L.lacc[:, tt * E:(tt + 1) * E], axis=AX)
            self.dv(fin_m)

            def fin2(v, tt=tt):
                L.wdv(v)
                v.tensor_scalar(L.rscr[:, 2:2 + E], L.lacc[:, tt * E:(tt + 1) * E],
                                L.rscr[:, 0:1], None, ALU.subtract)
            self.dv(fin2)

            def ex(e, tt=tt):
                L.wdv(e)
                e.activation(L.rscr[:, 2:2 + E], L.rscr[:, 2:2 + E], AF.Exp,
                             accum_out=L.rscr[:, 1:2])
            self.ac(ex)

            def rcp2(v, tt=tt):
                L.wac(v)
                v.reciprocal(L.probs_sb[:, tt:tt + 1], L.rscr[:, 1:2])
            self.dv(rcp2)

            def rt_a(v, tt=tt):
                L.wdv(v)
                v.tensor_scalar(L.max8[:, :], L.lacc[:, tt * E:(tt + 1) * E],
                                L.rscr[:, 0:1], None, ALU.is_equal)
            self.dv(rt_a)

            def rt_b(v, tt=tt):
                L.wdv(v)
                v.tensor_tensor(L.max8[:, :], L.max8[:, :], L.iota8m[:, :],
                                ALU.mult)
            self.dv(rt_b)

            def rt_c(v, tt=tt):
                L.wdv(v)
                v.tensor_reduce(L.rscr[:, 3:4], L.max8[:, :], AX, ALU.min)
            self.dv(rt_c)

            def rt_d(v, tt=tt):
                L.wdv(v)
                v.tensor_scalar(L.rf32_sb[:, tt:tt + 1], L.rscr[:, 3:4],
                                float(E), None, ALU.add)
            self.dv(rt_d)

            def rt2(v, tt=tt):
                L.wdv(v)
                v.tensor_scalar(L.ohf[:, :], L.iota8[:, :], L.rf32_sb[:, tt:tt + 1],
                                None, ALU.is_equal)
            self.dv(rt2)

            def rt2b(v, tt=tt):
                L.wdv(v)
                v.tensor_copy(L.ohb[:, :], L.ohf[:, :])
            self.dv(rt2b)

            def tp(t, tt=tt):
                L.wdv(t)
                t.transpose(L.tp_ps[0:E, 0:128], L.ohb[:, :], L.idb[:, :])
            self.pe(tp)

            def cpo(v, tt=tt):
                L.wpe(v)
                v.tensor_copy(L.ohT[:, tt * 128:(tt + 1) * 128], L.tp_ps[0:E, 0:128])
            self.dv(cpo)
        self.barrier()

        def sc0(v):
            v.tensor_tensor_scan(L.scan[:, :], L.ohT[:, :], L.ohT[:, :], 0.0,
                                 ALU.add, ALU.bypass)
        self.dv(sc0)

        steps_sc = [
            lambda v: v.tensor_copy(L.cntc[:, 0:1], L.scan[:, S - 1:S]),
            lambda v: v.tensor_scalar(L.cntc[:, 2:3], L.cntc[:, 0:1], 127.0,
                                      None, ALU.add),
            lambda v: v.tensor_copy(L.cnt_u[:, 0:1], L.cntc[:, 2:3]),
            lambda v: v.tensor_scalar(L.cnt_u[:, 0:1], L.cnt_u[:, 0:1], 7,
                                      None, ALU.logical_shift_right),
            lambda v: v.tensor_scalar(L.cnt_u[:, 0:1], L.cnt_u[:, 0:1], 7,
                                      None, ALU.logical_shift_left),
            lambda v: v.tensor_copy(L.cntc[:, 1:2], L.cnt_u[:, 0:1]),
        ]
        for st_ in steps_sc:
            self.dv(lambda v, st_=st_: (L.wdv(v), st_(v)))

        def spc(g, b=b):
            L.wdv(g)
            g.dma_start(out=L.cnt_out[b], in_=L.cntc[0:c.E, 0:1]) \
             .then_inc(L.s_gd.inc(16), 16)
            g.dma_start(out=L.probs_out[b], in_=L.probs_sb[:, :]) \
             .then_inc(L.s_gd.inc(16), 16)
        self.gp(spc)

        def bmm(t):
            L.wdv(t)
            t.matmul(L.psD[0:1, 0:E], L.cntc[0:E, 1:2], L.lones[:, :],
                     start=True, stop=True)
        self.pe(bmm)

        def bcp(v):
            L.wpe(v)
            v.tensor_copy(L.base_r[0:1, :], L.psD[0:1, 0:E])
        self.dv(bcp)
        self.row_bcast(L.base_r[0:1, 0:E], E, L.baseb[:, 0:E])
        self.barrier()

        def pf(g, b=b):
            g.dma_start(out=L.gtbl_out[b], in_=L.sentc[:, 0:NCH]) \
             .then_inc(L.s_gd.inc(16), 16)
        self.gp(pf)

        for tt in range(TT):
            def prod(v, tt=tt):
                L.wgd(v)
                v.tensor_tensor(L.rscr[:, 0:0 + E] if False else L.ohT[:, tt * 128:(tt + 1) * 128],
                                L.ohT[:, tt * 128:(tt + 1) * 128],
                                L.scan[:, tt * 128:(tt + 1) * 128], ALU.mult)
            self.dv(prod)

            def tpp(t, tt=tt):
                L.wdv(t)
                t.transpose(L.psD[:, 0:E], L.ohT[0:E, tt * 128:(tt + 1) * 128],
                            L.idf[0:E, 0:E])
            self.pe(tpp)

            def mkslot0(v, tt=tt):
                L.wpe(v)
                v.reduce_sum(L.pos_sb[:, tt:tt + 1], L.psD[:, 0:E], axis=AX)
            self.dv(mkslot0)

            steps_mk = [
                lambda v, tt=tt: v.tensor_scalar(L.ohf[:, :], L.iota8[:, :],
                                                 L.rf32_sb[:, tt:tt + 1],
                                                 None, ALU.is_equal),
                lambda v: v.tensor_tensor(L.rscr[:, 2:2 + E], L.ohf[:, :],
                                          L.baseb[:, 0:E], ALU.mult),
                lambda v: v.reduce_sum(L.rscr[:, 1:2], L.rscr[:, 2:2 + E],
                                       axis=AX),
                lambda v, tt=tt: v.tensor_tensor(L.rscr[:, 0:1], L.rscr[:, 1:2],
                                                 L.pos_sb[:, tt:tt + 1], ALU.add),
                lambda v: v.tensor_scalar(L.rscr[:, 0:1], L.rscr[:, 0:1], -1.0,
                                          None, ALU.add),
                lambda v: v.tensor_copy(L.slotu2[:, 0:1], L.rscr[:, 0:1]),
                lambda v: v.tensor_scalar(L.slotu2[:, 1:2], L.slotu2[:, 0:1],
                                          127, None, ALU.bitwise_and),
                lambda v: v.tensor_scalar(L.slotu2[:, 0:1], L.slotu2[:, 0:1],
                                          7, None, ALU.logical_shift_right),
                lambda v: v.tensor_copy(L.rscr[:, 1:2], L.slotu2[:, 1:2]),
                lambda v: v.tensor_copy(L.rscr[:, 2:3], L.slotu2[:, 0:1]),
                lambda v: v.tensor_scalar(L.rscr[:, 1:2], L.rscr[:, 1:2],
                                          float(NCH), None, ALU.mult),
                lambda v: v.tensor_tensor(L.rscr[:, 0:1], L.rscr[:, 1:2],
                                          L.rscr[:, 2:3], ALU.add),
                lambda v: v.tensor_copy(L.slotu[:, 0:1], L.rscr[:, 0:1]),
            ]
            for st_ in steps_mk:
                self.dv(lambda v, st_=st_: (L.wdv(v), st_(v)))

            def sct(g, tt=tt, b=b):
                L.wdv(g)
                L.wgd(g)
                g.iota(L.tokid[:, :], [[0, 1]], base=tt * 128, channel_multiplier=1,
                       allow_small_or_imprecise_dtypes=True)
                g.indirect_dma_start(
                    out=L.gtbl_out.rearrange("b p q -> (b p q)").rearrange(
                        "(n o) -> n o", o=1),
                    out_offset=IOA(L.slotu[:, 0:1], 0),
                    in_=L.tokid[:, 0:1],
                    in_offset=None,
                    element_offset=b * 128 * NCH,
                ).then_inc(L.s_gd.inc(16), 16)
            self.gp(sct)
        self.barrier()

        def rb(g, b=b):
            g.dma_start(out=L.gidx[:, b * NCH:(b + 1) * NCH], in_=L.gtbl_out[b]) \
             .then_inc(L.s_gd.inc(16), 16)
        self.gp(rb)

        def cmp_(v):
            v.tensor_scalar(L.ohT[0:E, 0:NCH], L.iotaNCH[0:E, :],
                            L.base_r[0:1, 0:E].rearrange("o e -> e o") if False
                            else L.cntc[0:E, 3:4], None, ALU.is_ge)
        def bt(t):
            L.wdv(t)
            t.transpose(L.psD[0:E, 0:1], L.base_r[0:1, 0:E], L.idf[0:1, 0:1])
        self.pe(bt)

        def btc(v):
            L.wpe(v)
            v.tensor_copy(L.cntc[:, 3:4], L.psD[0:E, 0:1])
        self.dv(btc)
        self.dv(cmp_)

        def emm(t):
            L.wdv(t)
            t.matmul(L.psD[0:1, 0:NCH], L.onescf[0:E, 0:1], L.ohT[0:E, 0:NCH],
                     start=True, stop=True)
        self.pe(emm)

        def etb(v, b=b):
            L.wpe(v)
            v.tensor_scalar(L.ech_f[0:1, b * NCH:(b + 1) * NCH],
                            L.psD[0:1, 0:NCH], -1.0, None, ALU.add)
        self.dv(etb)

        def etb2(v, b=b):
            L.wdv(v)
            v.tensor_copy(L.echw[0:1, b * NCH:(b + 1) * NCH],
                          L.ech_f[0:1, b * NCH:(b + 1) * NCH])
        self.dv(etb2)
        self.barrier()

    def alloc_moe(self, ph):
        nc, c = self.nc, self.cfg
        en = ph.enter_context
        self.weT_sb = en(nc.sbuf_tensor("weT_sb", [128, c.E * c.DC * c.D], BF16))
        self.be_sb = en(nc.sbuf_tensor("be_sb", [1, c.E * c.D], BF16))
        self.onesr_b = en(nc.sbuf_tensor("onesr_b", [1, 128], BF16))
        self.stg_tm = en(nc.sbuf_tensor("stg_tm", [128, c.D], BF16))
        self.stg_fm = en(nc.sbuf_tensor("stg_fm", [128, c.D], BF16))
        self.eo_sb = en(nc.sbuf_tensor("eo_sb", [128, c.D], BF16))
        self.gcf = en(nc.sbuf_tensor("gcf", [128, c.NCH], F32))
        self.r_e = nc.alloc_register(mybir.EngineType.PE, "moe_re")
        L = self

        def pro(g):
            g.memset(L.onesr_b[:, :], 1.0)
            g.memset(L.stg_tm[:, :], 0.0)
            g.dma_start(out=L.be_sb[:, :], in_=L.ber).then_inc(L.s_gd.inc(16), 16)
            for e in range(c.E):
                for dc in range(c.DC):
                    g.dma_start(
                        out=L.weT_sb[:, (e * c.DC + dc) * c.D:(e * c.DC + dc + 1) * c.D],
                        in_=L.WeT[e, dc]).then_inc(L.s_gd.inc(16), 16)
        self.gp(pro, inc=True)
        self.barrier()

    def moe_batch(self, b):
        c, L = self.cfg, self
        D, DC, E, NCH, S = c.D, c.DC, c.E, c.NCH, c.S
        nhalf = ceil_div(D, 512)

        steps_cl = [
            lambda v: v.tensor_copy(L.gcf[:, :],
                                    L.gidx[:, b * NCH:(b + 1) * NCH]),
            lambda v: v.tensor_scalar(L.gcf[:, :], L.gcf[:, :],
                                      float(S - 1), None, ALU.min),
            lambda v: v.tensor_copy(L.gidx[:, b * NCH:(b + 1) * NCH],
                                    L.gcf[:, :]),
        ]
        for st_ in steps_cl:
            self.dv(lambda v, st_=st_: (L.wdv(v), st_(v)))
        self.barrier()

        for ch in range(NCH):
            def ga(g, ch=ch, b=b):
                L.wdv(g)
                g.indirect_dma_start(
                    out=L.stg_tm[:, :],
                    out_offset=None,
                    in_=L.hid_out.rearrange("b s d -> (b s) d"),
                    in_offset=IOA(L.gidx[:, b * NCH + ch:b * NCH + ch + 1], 0),
                    element_offset=b * S * D,
                ).then_inc(L.s_gd.inc(16), 16)
            self.gp(ga)

            for dc in range(DC):
                def tp(t, dc=dc):
                    L.wgd(t)
                    L.wdv(t)
                    t.transpose(L.tp_ps[:, 0:128],
                                L.stg_tm[:, dc * 128:(dc + 1) * 128],
                                L.idb[:, :])
                self.pe(tp)

                def cp(v, dc=dc):
                    L.wpe(v)
                    v.tensor_copy(L.stg_fm[:, dc * 128:(dc + 1) * 128],
                                  L.tp_ps[:, 0:128])
                self.dv(cp)

            def mm(t, ch=ch, b=b):
                L.wdv(t)
                r_e = L.r_e
                t.reg_load(r_e, L.echw[0:1, b * NCH + ch:b * NCH + ch + 1])
                for e in range(E):
                    with t.If_eq(r_e, e):
                        for half in range(nhalf):
                            w = min(512, D - half * 512)
                            pp = L.psA if half == 0 else L.psB
                            for dc in range(DC):
                                co = (e * DC + dc) * D + half * 512
                                t.matmul(pp[:, 0:w],
                                         L.stg_fm[:, dc * 128:(dc + 1) * 128],
                                         L.weT_sb[:, co:co + w],
                                         start=(dc == 0), stop=False)
                        for half in range(nhalf):
                            w = min(512, D - half * 512)
                            pp = L.psA if half == 0 else L.psB
                            t.matmul(pp[:, 0:w],
                                     L.onesr_b[0:1, 0:128],
                                     L.be_sb[0:1, e * D + half * 512:e * D + half * 512 + w],
                                     start=False, stop=True)
                t.end_ifs()
            self.pe(mm)

            def ev(v, ch=ch, b=b):
                L.wpe(v)
                L.wgd(v)
                for half in range(nhalf):
                    w = min(512, D - half * 512)
                    pp = L.psA if half == 0 else L.psB
                    v.tensor_copy(L.eo_sb[:, half * 512:half * 512 + w], pp[:, 0:w])
            self.dv(ev)

            def st(g, ch=ch, b=b):
                L.wdv(g)
                g.dma_start(out=L.eout_out[b, ch * 128:(ch + 1) * 128, :],
                            in_=L.eo_sb[:, :]).then_inc(L.s_gd.inc(16), 16)
            self.gp(st)
        self.barrier()


def build_launch1(cfg, upto="all"):
    return L1(cfg, upto).build()



class L2:
    def __init__(self, cfg: CFG):
        self.cfg = cfg
        nc = bass.Bass("TRN2", target_bir_lowering=False)
        nc.detect_race_conditions = False
        self.nc = nc
        c = cfg
        S, D, E, NB, DC, TT, NCH, NSLOT = (c.S, c.D, c.E, c.NB, c.DC, c.TT,
                                           c.NCH, c.NSLOT)
        NBG = c.n_cores * NB
        self.NBG = NBG
        dt = nc.dram_tensor
        self.x1_in = dt("x1_in", [NB, DC, 128, S], F32, kind="ExternalInput").ap()
        self.hid_in = dt("hid_in", [NB, S, D], BF16, kind="ExternalInput").ap()
        self.eout_in = dt("eout_in", [NB, NSLOT, D], BF16, kind="ExternalInput").ap()
        self.gtbl_in = dt("gtbl_in", [NB, 128, NCH], U32, kind="ExternalInput").ap()
        self.probs_in = dt("probs_in", [NB, 128, TT], F32, kind="ExternalInput").ap()
        self.ln3_in = dt("ln3_in", [NB, 2, S], F32, kind="ExternalInput").ap()
        self.allcnt = dt("allcnt", [NBG, E], F32, kind="ExternalInput").ap()
        self.offmask = dt("offmask", [NBG, 2 * NB], F32, kind="ExternalInput").ap()
        self.identf = dt("identf2", [128, 128], F32, kind="ExternalInput").ap()
        self.lonesr = dt("lonesr", [E, E], F32, kind="ExternalInput").ap()
        self.gammaT = dt("gammaT", [128, DC], F32, kind="ExternalInput").ap()
        self.betaT = dt("betaT", [128, DC], F32, kind="ExternalInput").ap()
        self.out = dt("out", [NB, DC, 128, S], F32, kind="ExternalOutput").ap()
        self.y_hbm = dt("y_hbm", [NB * S + 128, D], BF16).ap()

        es = contextlib.ExitStack()
        self.es = es

        def sbuf(name, p, cc, dty):
            return es.enter_context(nc.sbuf_tensor(name, [p, cc], dty))
        self.sbuf = sbuf
        self.onesc = sbuf("onesc", 128, 1, BF16)
        self.onesrf = sbuf("onesrf", 128, 128, F32)
        self.lones = sbuf("lones", E, E, F32)
        self.rowb = sbuf("rowb", 1, 2048, F32)
        self.mu4 = sbuf("mu4", 1, S, F32)
        self.rs4 = sbuf("rs4", 1, S, F32)
        self.mu3 = sbuf("mu3", 1, S, F32)
        self.rs3 = sbuf("rs3", 1, S, F32)
        self.acnt = sbuf("acnt", NBG, E, F32)
        self.mycc = sbuf("mycc", E, 4, F32)
        self.mycu = sbuf("mycu", E, 1, U32)
        self.offc = sbuf("offc", E, 2, F32)
        self.maskc = sbuf("maskc", NBG, 2 * NB, F32)
        self.idb = sbuf("idb", 128, 128, BF16)
        self.iotaN = sbuf("iotaN", E, NCH, F32)
        self.iotaP = sbuf("iotaP", 128, 1, F32)
        self.base_r = sbuf("base_r", 1, E, F32)
        self.rows = sbuf("rows", 1, 6 * NCH, F32)
        self.cmpA = sbuf("cmpA", E, NCH, F32)
        self.cmpB = sbuf("cmpB", E, NCH, F32)
        self.thrb = sbuf("thrb", 128, NCH, F32)
        self.keptm = sbuf("keptm", 128, NCH, F32)
        self.gidxf = sbuf("gidxf", 128, NCH, F32)
        self.sidx = sbuf("sidx", 128, NCH, U32)
        self.gidx = sbuf("gidx", 128, NCH, U32)
        self.eo_sb = sbuf("eo_sb", 128, D, BF16)
        self.yt_sb = sbuf("yt_sb", 128, D, BF16)
        self.gam = sbuf("gam", 128, DC, F32)
        self.bet = sbuf("bet", 128, DC, F32)
        self.zbb = sbuf("zbb", 128, 512, F32)
        self.zrow = sbuf("zrow", 128, 512, F32)
        self.mvb = sbuf("mvb", 128, 1024, F32)
        self.x1c = sbuf("x1c", 128, DC * 512, F32)
        self.x2_sb = sbuf("x2_sb", 128, DC * S, F32)
        self.xb_sb = sbuf("xb_sb", 128, DC * S, BF16)
        self.stg_b = sbuf("stg_b", 128, DC * 512, BF16)
        self.sqc = sbuf("sqc", 128, DC * 512, BF16)

        self.psA = es.enter_context(nc.psum_tensor("psA", [128, 512], F32))
        self.psD = es.enter_context(nc.psum_tensor("psD", [128, 512], F32))
        self.tp_ps = es.enter_context(nc.psum_tensor("tp_ps", [128, 512], BF16))
        self.st_ps = es.enter_context(nc.psum_tensor("st_ps", [1, 1024], F32))

        self.s_hd = Sem(es, nc, "s_hd")
        self.s_gd = Sem(es, nc, "s_gd")
        self.s_pe = Sem(es, nc, "s_pe")
        self.s_dv = Sem(es, nc, "s_dv")
        self.s_ac = Sem(es, nc, "s_ac")
        self.s_gp = Sem(es, nc, "s_gp")
        self.sems = [self.s_hd, self.s_gd, self.s_pe, self.s_dv, self.s_ac,
                     self.s_gp]
        self.block = es.enter_context(nc.Block())
        self.items = {"tensor": [], "vector": [], "scalar": []}
        self._cursnap = None

    wait_all = L1.wait_all
    _curv = L1._curv
    _defer = L1._defer
    flush_items = L1.flush_items
    pe = L1.pe
    dv = L1.dv
    ac = L1.ac
    gp = L1.gp
    sy = L1.sy
    barrier = L1.barrier
    wpe = L1.wpe
    wdv = L1.wdv
    wac = L1.wac
    wgd = L1.wgd
    whd = L1.whd
    wgp = L1.wgp
    row_bcast = L1.row_bcast
    stats = L1.stats
    ln_apply = L1.ln_apply

    def build(self):
        c, L = self.cfg, self
        S, D, E, NB, DC, TT, NCH = c.S, c.D, c.E, c.NB, c.DC, c.TT, c.NCH
        NBG = self.NBG

        def pro(g):
            g.memset(L.onesc[:, :], 1.0)
            g.memset(L.onesrf[:, :], 1.0)
            g.iota(L.iotaN[:, :], [[128, NCH]], base=0, channel_multiplier=0,
                   allow_small_or_imprecise_dtypes=True)
            g.iota(L.iotaP[:, :], [[0, 1]], base=0, channel_multiplier=1,
                   allow_small_or_imprecise_dtypes=True)
            g.dma_start(out=L.lones[:, :], in_=L.lonesr).then_inc(L.s_gd.inc(16), 16)
            g.dma_start(out=L.acnt[:, :], in_=L.allcnt).then_inc(L.s_gd.inc(16), 16)
            g.dma_start(out=L.maskc[:, :], in_=L.offmask) \
             .then_inc(L.s_gd.inc(16), 16)
            g.dma_start(out=L.gam[:, :], in_=L.gammaT).then_inc(L.s_gd.inc(16), 16)
            g.dma_start(out=L.bet[:, :], in_=L.betaT).then_inc(L.s_gd.inc(16), 16)
            g.dma_start(out=L.idb[:, :], in_=L.identf).then_inc(L.s_gd.inc(16), 16)
        self.gp(pro, inc=True)
        self.barrier()

        for b in range(NB):
            self.batch(b)
        self.flush_items()
        self.sy(lambda e: self.wait_all(e))
        self.es.close()
        return self.nc

    def batch(self, b):
        c, L = self.cfg, self
        S, D, E, NB, DC, TT, NCH = c.S, c.D, c.E, c.NB, c.DC, c.TT, c.NCH
        NBG = self.NBG
        myrow = None

        def t0(v, b=b):
            pass
        def mm0(t, b=b):
            self.wait_all(t)
            t.matmul(L.psA[0:E, 0:2], L.acnt[:, :],
                     L.maskc[:, 2 * b:2 * b + 2], start=True, stop=True)
        self.pe(mm0)

        steps_c0 = [
            lambda v: v.tensor_copy(L.offc[:, 0:1], L.psA[0:E, 0:1]),
            lambda v: v.tensor_copy(L.mycc[:, 0:1], L.psA[0:E, 1:2]),
            lambda v: v.tensor_scalar(L.mycc[:, 2:3], L.mycc[:, 0:1], 127.0,
                                      None, ALU.add),
            lambda v: v.tensor_copy(L.mycu[:, 0:1], L.mycc[:, 2:3]),
            lambda v: v.tensor_scalar(L.mycu[:, 0:1], L.mycu[:, 0:1], 7, None,
                                      ALU.logical_shift_right),
            lambda v: v.tensor_scalar(L.mycu[:, 0:1], L.mycu[:, 0:1], 7, None,
                                      ALU.logical_shift_left),
            lambda v: v.tensor_copy(L.mycc[:, 1:2], L.mycu[:, 0:1]),
        ]
        first_c0 = [True]
        for st_ in steps_c0:
            def item(v, st_=st_, first_c0=first_c0):
                if first_c0[0]:
                    L.wpe(v)
                    first_c0[0] = False
                else:
                    L.wdv(v)
                st_(v)
            self.dv(item)

        def mm1(t):
            L.wdv(t)
            t.matmul(L.psD[0:1, 0:E], L.mycc[0:E, 1:2], L.lones[:, :],
                     start=True, stop=True)
        self.pe(mm1)

        def cp1(v):
            L.wpe(v)
            v.tensor_copy(L.base_r[0:1, :], L.psD[0:1, 0:E])
        self.dv(cp1)

        def tp1(t):
            L.wdv(t)
            t.transpose(L.psD[0:E, 0:1], L.base_r[0:1, 0:E], L.onesrf[0:1, 0:1])
        self.pe(tp1)

        def cp2a(v):
            L.wpe(v)
            v.tensor_copy(L.mycc[:, 3:4], L.psD[0:E, 0:1])
        self.dv(cp2a)

        def cp2a2(v):
            L.wdv(v)
            v.tensor_tensor(L.offc[:, 1:2], L.mycc[:, 3:4], L.mycc[:, 1:2],
                            ALU.add)
        self.dv(cp2a2)

        def cp2(v):
            L.wdv(v)
            v.tensor_scalar(L.cmpA[:, :], L.iotaN[:, :], L.mycc[:, 3:4], None,
                            ALU.is_ge)
            v.tensor_scalar(L.cmpB[:, :], L.iotaN[:, :], L.offc[:, 1:2], None,
                            ALU.is_ge)
        self.dv(cp2)

        def cp2b(v):
            L.wdv(v)
            v.tensor_tensor(L.cmpB[:, :], L.cmpA[:, :], L.cmpB[:, :],
                            ALU.subtract)
        self.dv(cp2b)

        def mm2(t):
            L.wdv(t)
            t.matmul(L.psD[0:1, 0:NCH], L.offc[0:E, 0:1], L.cmpB[:, :],
                     start=True, stop=True)
        self.pe(mm2)

        def cp3(v):
            L.wpe(v)
            v.tensor_copy(L.rows[0:1, NCH:2 * NCH], L.psD[0:1, 0:NCH])
            v.tensor_tensor(L.cmpA[:, :], L.cmpA[:, :], L.cmpB[:, :],
                            ALU.subtract)
        self.dv(cp3)

        def mm3(t):
            L.wdv(t)
            t.matmul(L.psD[0:1, 0:NCH], L.mycc[0:E, 1:2], L.cmpA[:, :],
                     start=True, stop=True)
        self.pe(mm3)

        steps_c4 = [
            lambda v: v.tensor_copy(L.rows[0:1, 0:NCH], L.psD[0:1, 0:NCH]),
            lambda v: v.tensor_tensor(L.rows[0:1, 2 * NCH:3 * NCH],
                                      L.rows[0:1, 0:NCH],
                                      L.rows[0:1, NCH:2 * NCH], ALU.subtract),
            lambda v: v.tensor_scalar(L.rows[0:1, 2 * NCH:3 * NCH],
                                      L.rows[0:1, 2 * NCH:3 * NCH],
                                      float(c.capacity), None, ALU.add),
            lambda v: v.tensor_tensor(L.rows[0:1, 2 * NCH:3 * NCH],
                                      L.rows[0:1, 2 * NCH:3 * NCH],
                                      L.iotaN[0:1, 0:NCH], ALU.subtract),
        ]
        first_c4 = [True]
        for st_ in steps_c4:
            def item(v, st_=st_, first_c4=first_c4):
                if first_c4[0]:
                    L.wpe(v)
                    first_c4[0] = False
                else:
                    L.wdv(v)
                st_(v)
            self.dv(item)
        self.row_bcast(L.rows[0:1, 2 * NCH:3 * NCH], NCH, L.thrb[:, :])

        def ld_gt(g, b=b):
            g.dma_start(out=L.gidx[:, :], in_=L.gtbl_in[b]) \
             .then_inc(L.s_gd.inc(16), 16)
        self.gp(ld_gt)
        self.barrier()

        trash = float((NB - b) * S)
        steps_kp = [
            lambda v: v.tensor_scalar(L.keptm[:, :], L.thrb[:, :],
                                      L.iotaP[:, 0:1], None, ALU.is_gt),
            lambda v: v.tensor_copy(L.gidxf[:, :], L.gidx[:, :]),
            lambda v: v.tensor_scalar(L.gidxf[:, :], L.gidxf[:, :],
                                      -float(c.SENT), None, ALU.add),
            lambda v: v.tensor_tensor(L.gidxf[:, :], L.keptm[:, :],
                                      L.gidxf[:, :], ALU.mult),
            lambda v: v.tensor_scalar(L.gidxf[:, :], L.gidxf[:, :],
                                      float(c.SENT), None, ALU.add),
            lambda v: v.tensor_scalar(L.gidxf[:, :], L.gidxf[:, :],
                                      trash, None, ALU.min),
            lambda v: v.tensor_copy(L.sidx[:, :], L.gidxf[:, :]),
        ]
        for st_ in steps_kp:
            self.dv(lambda v, st_=st_: (L.wdv(v), st_(v)))

        for tt in range(TT):
            def cpy(g, tt=tt, b=b):
                L.wgd(g)
                g.dma_start(out=L.yt_sb[:, :],
                            in_=L.hid_in[b, tt * 128:(tt + 1) * 128, :]) \
                 .then_inc(L.s_gd.inc(16), 16)
                g.wait_ge(L.s_gd.h, L.s_gd.v)
                g.dma_start(out=L.y_hbm[b * S + tt * 128:b * S + (tt + 1) * 128, :],
                            in_=L.yt_sb[:, :]).then_inc(L.s_gd.inc(16), 16)
            self.gp(cpy)
        self.barrier()

        for ch in range(NCH):
            def sc(g, ch=ch, b=b):
                L.wgd(g)
                g.dma_start(out=L.eo_sb[:, :],
                            in_=L.eout_in[b, ch * 128:(ch + 1) * 128, :]) \
                 .then_inc(L.s_gd.inc(16), 16)
                g.wait_ge(L.s_gd.h, L.s_gd.v)
                g.indirect_dma_start(
                    out=L.y_hbm,
                    out_offset=IOA(L.sidx[:, ch:ch + 1], 0),
                    in_=L.eo_sb[:, :],
                    in_offset=None,
                    element_offset=b * S * D,
                ).then_inc(L.s_gd.inc(16), 16)
            self.gp(sc)
        self.barrier()

        def ld3(g, b=b):
            g.dma_start(out=L.mu3[0:1, :], in_=L.ln3_in[b, 0:1, :]) \
             .then_inc(L.s_gd.inc(16), 16)
            g.dma_start(out=L.rs3[0:1, :], in_=L.ln3_in[b, 1:2, :]) \
             .then_inc(L.s_gd.inc(16), 16)
            g.dma_start(out=L.thrb[:, 0:TT], in_=L.probs_in[b]) \
             .then_inc(L.s_gd.inc(16), 16)
        self.gp(ld3)
        self.barrier()

        for tt in range(TT):
            def ldy(g, tt=tt, b=b):
                L.wpe(g)
                g.dma_start(out=L.yt_sb[:, :],
                            in_=L.y_hbm[b * S + tt * 128:b * S + (tt + 1) * 128, :]) \
                 .then_inc(L.s_gd.inc(16), 16)
            self.gp(ldy)

            def sc_y(v, tt=tt):
                L.wgd(v)
                v.tensor_scalar(L.yt_sb[:, :], L.yt_sb[:, :],
                                L.thrb[:, tt:tt + 1], None, ALU.mult)
            self.dv(sc_y)

            for dc in range(DC):
                def tp(t, tt=tt, dc=dc):
                    L.wdv(t)
                    t.transpose(L.tp_ps[:, 0:128],
                                L.yt_sb[:, dc * 128:(dc + 1) * 128],
                                L.idb[:, :])
                self.pe(tp)

                def cp(v, tt=tt, dc=dc):
                    L.wpe(v)
                    v.tensor_copy(L.xb_sb[:, dc * S + tt * 128:dc * S + (tt + 1) * 128],
                                  L.tp_ps[:, 0:128])
                self.dv(cp)
        self.barrier()
        for j in range(c.QC):
            c0 = j * 512

            def ldx1(sy_, j=j, c0=c0, b=b):
                L.wdv(sy_)
                for dc in range(DC):
                    sy_.dma_start(out=L.x1c[:, dc * 512:(dc + 1) * 512],
                                  in_=L.x1_in[b, dc, :, c0:c0 + 512]) \
                       .then_inc(L.s_hd.inc(16), 16)
            self.sy(ldx1)
            self.row_bcast(L.rs3[0:1, c0:c0 + 512], 512, L.zbb[:, 0:512])
            self.row_bcast(L.mu3[0:1, c0:c0 + 512], 512, L.zrow[:, 0:512])

            def x2c(v, j=j, c0=c0):
                L.wdv(v)
                L.whd(v)
                for dc in range(DC):
                    sl = slice(dc * S + c0, dc * S + c0 + 512)
                    v.tensor_tensor(L.mvb[:, 0:512], L.x1c[:, dc * 512:(dc + 1) * 512],
                                    L.zrow[:, 0:512], ALU.subtract)
                    v.tensor_tensor(L.mvb[:, 512:1024], L.mvb[:, 0:512],
                                    L.zbb[:, 0:512], ALU.mult)
                    v.tensor_tensor(L.x2_sb[:, sl], L.mvb[:, 512:1024],
                                    L.xb_sb[:, sl], ALU.add)
            self.dv(x2c)
        self.barrier()
        self.stats(lambda dc: L.x2_sb[:, dc * S:(dc + 1) * S], S, L.mu4, L.rs4)
        self.barrier()
        for j in range(c.QC):
            c0 = j * 512
            self.row_bcast(L.rs4[0:1, c0:c0 + 512], 512, L.zbb[:, 0:512])
            self.row_bcast(L.mu4[0:1, c0:c0 + 512], 512, L.zrow[:, 0:512])

            def f4(v, j=j, c0=c0):
                L.wdv(v)
                for dc in range(DC):
                    sl = slice(dc * S + c0, dc * S + c0 + 512)
                    v.tensor_tensor(L.mvb[:, 0:512], L.x2_sb[:, sl],
                                    L.zrow[:, 0:512], ALU.subtract)
                    v.tensor_tensor(L.mvb[:, 512:1024], L.mvb[:, 0:512],
                                    L.zbb[:, 0:512], ALU.mult)
                    v.tensor_scalar(L.xb_sb[:, sl], L.mvb[:, 512:1024],
                                    L.gam[:, dc:dc + 1], L.bet[:, dc:dc + 1],
                                    ALU.mult, ALU.add)
            self.dv(f4)
        self.barrier()

        self.stats(lambda dc: L.xb_sb[:, dc * S:(dc + 1) * S], S, L.mu4, L.rs4,
                   src_bf16=True)
        self.barrier()
        for j in range(c.QC):
            c0 = j * 512

            def ldx1(sy_, j=j, c0=c0, b=b):
                L.whd(sy_)
                L.wdv(sy_)
                for dc in range(DC):
                    sy_.dma_start(out=L.x1c[:, dc * 512:(dc + 1) * 512],
                                  in_=L.x1_in[b, dc, :, c0:c0 + 512]) \
                       .then_inc(L.s_hd.inc(16), 16)
            self.sy(ldx1)
            self.row_bcast(L.rs4[0:1, c0:c0 + 512], 512, L.zbb[:, 0:512])
            self.row_bcast(L.mu4[0:1, c0:c0 + 512], 512, L.zrow[:, 0:512])

            def oc_(v, j=j, c0=c0):
                L.wdv(v)
                L.whd(v)
                for dc in range(DC):
                    sl = slice(dc * S + c0, dc * S + c0 + 512)
                    v.tensor_tensor(L.mvb[:, 0:512], L.xb_sb[:, sl],
                                    L.zrow[:, 0:512], ALU.subtract)
                    v.tensor_tensor(L.mvb[:, 512:1024], L.mvb[:, 0:512],
                                    L.zbb[:, 0:512], ALU.mult)
                    v.tensor_tensor(L.x1c[:, dc * 512:(dc + 1) * 512],
                                    L.x1c[:, dc * 512:(dc + 1) * 512],
                                    L.mvb[:, 512:1024], ALU.add)
            self.dv(oc_)

            def spo(sy_, j=j, c0=c0, b=b):
                L.wdv(sy_)
                for dc in range(DC):
                    sy_.dma_start(out=L.out[b, dc, :, c0:c0 + 512],
                                  in_=L.x1c[:, dc * 512:(dc + 1) * 512]) \
                       .then_inc(L.s_hd.inc(16), 16)
            self.sy(spo)
        self.sy(lambda e: e.wait_ge(L.s_hd.h, L.s_hd.v))
        self.barrier()


def build_launch2(cfg):
    return L2(cfg).build()



def prep_inputs(cfg, inputs):
    E, D, H, NB, S, DC = cfg.E, cfg.D, cfg.H, cfg.NB, cfg.S, cfg.DC
    HP = H // 2
    x = np.ascontiguousarray(inputs["L0_attn_x"])
    qkv_w = np.asarray(inputs["L0_attn_qkv_w"])
    common = {}
    common["gtok"] = np.ascontiguousarray(
        np.asarray(inputs["L0_attn_global_token"]).reshape(D).reshape(DC, 128))
    common["qkv_wT"] = np.ascontiguousarray(qkv_w.T).reshape(DC, 128, 3 * D)
    common["proj_wT"] = np.ascontiguousarray(
        np.asarray(inputs["L0_attn_proj_w"]).T).reshape(DC, 128, D)
    common["proj_bT"] = np.ascontiguousarray(
        np.asarray(inputs["L0_attn_proj_b"]).reshape(DC, 128).T)
    Wq = np.asarray(inputs["L0_ffn_Wq"]).reshape(H * 64, D)
    Wk = np.asarray(inputs["L0_ffn_Wk"]).reshape(H * 64, D)
    Wv = np.asarray(inputs["L0_ffn_Wv"]).reshape(H * 64, D)
    Wqkv = np.concatenate([Wq, Wk, Wv], 0)
    common["WqkvT"] = np.ascontiguousarray(Wqkv.T).reshape(DC, 128, 3 * D)
    common["Wv_nat"] = np.ascontiguousarray(Wv)
    common["WswT"] = np.ascontiguousarray(np.asarray(inputs["L0_ffn_Wsw"]).T)
    common["bswr"] = np.ascontiguousarray(
        np.asarray(inputs["L0_ffn_bsw"]).reshape(1, E))
    bq = np.asarray(inputs["L0_ffn_bq"]).reshape(HP, 128).T
    bk = np.asarray(inputs["L0_ffn_bk"]).reshape(HP, 128).T
    bv = np.asarray(inputs["L0_ffn_bv"]).reshape(HP, 128).T
    common["bqkvT"] = np.ascontiguousarray(np.concatenate([bq, bk, bv], 1))
    common["bvr"] = np.ascontiguousarray(np.asarray(inputs["L0_ffn_bv"]).reshape(1, H * 64))
    common["WeT"] = np.ascontiguousarray(
        np.transpose(np.asarray(inputs["L0_ffn_We"]), (0, 2, 1))).reshape(
            E, DC, 128, D)
    common["ber"] = np.ascontiguousarray(
        np.asarray(inputs["L0_ffn_be"]).reshape(1, E * D))
    common["identf"] = np.eye(128, dtype=np.float32)
    ki = np.arange(128)[:, None]
    qi = np.arange(128)[None, :]
    common["amask0"] = (((ki + 1) // 64) <= ((qi + 1) // 64)).astype(np.float32)
    common["amask1"] = (((ki + 129) // 64) <= ((qi + 1) // 64)).astype(np.float32)
    common["fmask0"] = (ki <= qi).astype(np.float32)
    suf = np.zeros((128, 129), np.float32)
    suf[:, :128] = (np.arange(128)[:, None] > np.arange(128)[None, :])
    suf[:, 128] = 1.0
    common["sufones"] = suf
    common["lonesr"] = (np.arange(E)[:, None] < np.arange(E)[None, :]).astype(
        np.float32)

    maps = []
    for ci in range(cfg.n_cores):
        m = dict(common)
        xs = np.asarray(x[ci * NB:(ci + 1) * NB])
        m["xT"] = np.ascontiguousarray(xs.transpose(0, 2, 1)).reshape(
            NB, DC, 128, S)
        maps.append(m)
    return maps


def kernel(**inputs):
    cfg = FULL
    return _run(cfg, inputs)


def _run(cfg, inputs):
    maps1 = prep_inputs(cfg, inputs)
    nc1 = build_launch1(cfg, upto="all")
    res1 = run_bass_kernel_spmd(nc1, maps1, core_ids=list(range(cfg.n_cores)))

    maps2 = prep_launch2(cfg, inputs, res1.results)
    nc2 = build_launch2(cfg)
    res2 = run_bass_kernel_spmd(nc2, maps2, core_ids=list(range(cfg.n_cores)))

    outs = []
    for ci in range(cfg.n_cores):
        o = res2.results[ci]["out"].reshape(cfg.NB, cfg.D, cfg.S)
        outs.append(np.ascontiguousarray(o.transpose(0, 2, 1)))
    return np.concatenate(outs, 0).astype(np.float32)


def prep_launch2(cfg, inputs, res1):
    NB, E, D, DC = cfg.NB, cfg.E, cfg.D, cfg.DC
    NBG = cfg.n_cores * NB
    allcnt = np.concatenate(
        [np.asarray(r["cnt_out"]).reshape(NB, E) for r in res1], 0).astype(
            np.float32)
    lones = (np.arange(E)[:, None] < np.arange(E)[None, :]).astype(np.float32)
    gammaT = np.ascontiguousarray(
        np.asarray(inputs["L0_ffn_gamma"]).reshape(DC, 128).T)
    betaT = np.ascontiguousarray(
        np.asarray(inputs["L0_ffn_beta"]).reshape(DC, 128).T)
    identf = np.eye(128, dtype=np.float32)
    maps = []
    for ci in range(cfg.n_cores):
        m = {}
        r = res1[ci]
        m["x1_in"] = np.asarray(r["x1_out"])
        m["hid_in"] = np.asarray(r["hid_out"])
        m["eout_in"] = np.asarray(r["eout_out"])
        m["gtbl_in"] = np.asarray(r["gtbl_out"])
        m["probs_in"] = np.asarray(r["probs_out"])
        m["ln3_in"] = np.asarray(r["ln3_out"])
        m["allcnt"] = allcnt
        om = np.zeros((NBG, 2 * NB), np.float32)
        for b in range(NB):
            g = ci * NB + b
            om[:g, 2 * b] = 1.0
            om[g, 2 * b + 1] = 1.0
        m["offmask"] = om
        m["lonesr"] = lones
        m["gammaT"] = gammaT
        m["betaT"] = betaT
        m["identf2"] = identf
        maps.append(m)
    return maps


# version 04
# speedup vs baseline: 1.6581x; 1.1479x over previous
import contextlib
import math
import numpy as np

import concourse.bass as bass
import concourse.mybir as mybir
from concourse.bass import IndirectOffsetOnAxis as IOA
from concourse.bass_utils import run_bass_kernel_spmd
from concourse.bass import create_sync_update

F32 = mybir.dt.float32
BF16 = mybir.dt.bfloat16
U32 = mybir.dt.uint32
AF = mybir.ActivationFunctionType
ALU = mybir.AluOpType
AX = mybir.AxisListType.X


class CFG:
    def __init__(self, S=2048, D=1024, H=16, E=8, NB=2, n_cores=8,
                 cap_factor=1.2):
        self.S = S
        self.D = D
        self.H = H
        self.DH = 64
        self.E = E
        self.NB = NB
        self.n_cores = n_cores
        self.DC = D // 128
        self.TT = S // 128
        self.QCW = 512
        self.QC = S // self.QCW
        self.N = S + 1
        self.capacity = int(cap_factor * n_cores * NB * S / E)
        self.NCH = (S + E * 127 + 127) // 128
        self.NSLOT = self.NCH * 128
        self.SENT = 1 << 20
        self.GA = 4 if H >= 4 else 2
        self.WS = max(3 * self.GA * 64, D)
        self.WS2 = 3 * self.GA * 64


FULL = CFG()


class Sem:
    def __init__(self, es, nc, name):
        self.h = es.enter_context(nc.semaphore(name))
        self.v = 0

    def inc(self, n=1):
        self.v += n
        return self.h


def ceil_div(a, b):
    return (a + b - 1) // b



class L1:
    def __init__(self, cfg: CFG, upto="all"):
        self.cfg = cfg
        self.upto = upto
        nc = bass.Bass("TRN2", target_bir_lowering=False)
        nc.detect_race_conditions = False
        self.nc = nc
        c = cfg
        S, D, H, E, NB = c.S, c.D, c.H, c.E, c.NB
        DC, NCH, NSLOT = c.DC, c.NCH, c.NSLOT
        HP = H // 2

        dt = nc.dram_tensor
        self.xT = dt("xT", [NB, DC, 128, S], F32, kind="ExternalInput").ap()
        self.gtok = dt("gtok", [DC, 128], F32, kind="ExternalInput").ap()
        self.qkv_wT = dt("qkv_wT", [DC, 128, 3 * D], F32, kind="ExternalInput").ap()
        self.proj_wT = dt("proj_wT", [DC, 128, D], F32, kind="ExternalInput").ap()
        self.proj_bT = dt("proj_bT", [128, DC], F32, kind="ExternalInput").ap()
        self.WqkvT = dt("WqkvT", [DC, 128, 3 * D], F32, kind="ExternalInput").ap()
        self.Wv_nat = dt("Wv_nat", [H * 64, D], F32, kind="ExternalInput").ap()
        self.WswT = dt("WswT", [D, E], F32, kind="ExternalInput").ap()
        self.bswr = dt("bswr", [1, E], F32, kind="ExternalInput").ap()
        self.bqkvT = dt("bqkvT", [128, 3 * HP], F32, kind="ExternalInput").ap()
        self.bvr = dt("bvr", [1, H * 64], F32, kind="ExternalInput").ap()
        self.WeT = dt("WeT", [E, DC, 128, D], F32, kind="ExternalInput").ap()
        self.ber = dt("ber", [1, E * D], F32, kind="ExternalInput").ap()
        self.identf = dt("identf", [128, 128], F32, kind="ExternalInput").ap()
        self.amask0 = dt("amask0", [128, 128], F32, kind="ExternalInput").ap()
        self.amask1 = dt("amask1", [128, 128], F32, kind="ExternalInput").ap()
        self.fmask0 = dt("fmask0", [128, 128], F32, kind="ExternalInput").ap()
        self.sufones = dt("sufones", [128, 129], F32, kind="ExternalInput").ap()
        self.lonesr = dt("lonesr", [E, E], F32, kind="ExternalInput").ap()

        self.x1_out = dt("x1_out", [NB, DC, 128, S], F32, kind="ExternalOutput").ap()
        self.hid_out = dt("hid_out", [NB, S, D], BF16, kind="ExternalOutput").ap()
        self.eout_out = dt("eout_out", [NB, NSLOT, D], BF16, kind="ExternalOutput").ap()
        self.gtbl_out = dt("gtbl_out", [NB, 128, NCH], U32, kind="ExternalOutput").ap()
        self.cnt_out = dt("cnt_out", [NB, E, 1], F32, kind="ExternalOutput").ap()
        self.probs_out = dt("probs_out", [NB, 128, c.TT], F32, kind="ExternalOutput").ap()
        self.ln3_out = dt("ln3_out", [NB, 2, S], F32, kind="ExternalOutput").ap()

        self.ao_hbm = dt("ao_hbm", [NB, DC, 128, S], BF16).ap()
        self.e3_hbm = dt("e3_hbm", [NB, DC, 128, S], BF16).ap()

        es = contextlib.ExitStack()
        self.es = es

        def sbuf(name, p, cc, dty):
            return es.enter_context(nc.sbuf_tensor(name, [p, cc], dty))
        self.sbuf = sbuf

        self.idf = sbuf("idf", 128, 128, F32)
        self.idb = sbuf("idb", 128, 128, BF16)
        self.am0 = sbuf("am0", 128, 128, BF16)
        self.am1 = sbuf("am1", 128, 128, BF16)
        self.fm0 = sbuf("fm0", 128, 128, BF16)
        self.sfo = sbuf("sfo", 128, 129, BF16)
        self.lones = sbuf("lones", E, E, F32)
        self.onesc = sbuf("onesc", 128, 1, BF16)
        self.rowb = sbuf("rowb", 1, 1024, F32)
        self.mu1 = sbuf("mu1", 1, S, F32)
        self.rs1 = sbuf("rs1", 1, S, F32)
        self.mu3 = sbuf("mu3", 1, S, F32)
        self.rs3 = sbuf("rs3", 1, S, F32)
        self.bqkv_sb = sbuf("bqkv_sb", 128, 3 * HP, F32)
        self.bsw_sb = sbuf("bsw_sb", 1, E, F32)
        self.bswb = sbuf("bswb", 128, E, F32)
        self.gidx = sbuf("gidx", 128, NB * NCH, U32)
        self.echw = sbuf("echw", 1, NB * NCH, U32)
        self.echb = sbuf("echb", 1, NB * NCH, U32)
        self.ech_f = sbuf("ech_f", 1, NB * NCH, F32)
        self.iotaNCH = sbuf("iotaNCH", E, NCH, F32)

        self.psA = es.enter_context(nc.psum_tensor("psA", [128, 512], F32))
        self.psB = es.enter_context(nc.psum_tensor("psB", [128, 512], F32))
        self.psC = es.enter_context(nc.psum_tensor("psC", [128, 512], F32))
        self.psD = es.enter_context(nc.psum_tensor("psD", [128, 512], F32))
        self.tp_ps = es.enter_context(nc.psum_tensor("tp_ps", [128, 512], BF16))
        self.st_ps = es.enter_context(nc.psum_tensor("st_ps", [1, 1024], F32))

        self.s_hd = Sem(es, nc, "s_hd")
        self.s_gd = Sem(es, nc, "s_gd")
        self.s_pe = Sem(es, nc, "s_pe")
        self.s_dv = Sem(es, nc, "s_dv")
        self.s_ac = Sem(es, nc, "s_ac")
        self.s_gp = Sem(es, nc, "s_gp")
        self.sems = [self.s_hd, self.s_gd, self.s_pe, self.s_dv, self.s_ac,
                     self.s_gp]

        self.block = es.enter_context(nc.Block())
        self.items = {"tensor": [], "vector": [], "scalar": []}
        self._cursnap = None

    def _curv(self, s):
        if self._cursnap is not None:
            return self._cursnap[id(s)]
        return s.v

    def wait_all(self, e):
        for s in self.sems:
            e.wait_ge(s.h, self._curv(s))

    def _defer(self, eng, f, inc):
        snap = {id(s): s.v for s in self.sems}
        s = {"tensor": self.s_pe, "vector": self.s_dv,
             "scalar": self.s_ac}[eng]
        if inc:
            s.v += 1
        self.items[eng].append((f, snap, inc, s))

    def flush_items(self):
        for eng, emit in (("tensor", self.block.tensor),
                          ("vector", self.block.vector),
                          ("scalar", self.block.scalar)):
            lst = self.items[eng]
            if not lst:
                continue
            self.items[eng] = []

            def run(e, lst=lst):
                for f, snap, inc, s in lst:
                    self._cursnap = snap
                    f(e)
                    if inc:
                        e.maybe_drain_then_inc((s.h, 1))
                self._cursnap = None
            emit(run)

    def pe(self, f, inc=True):
        self._defer("tensor", f, inc)

    def dv(self, f, inc=True):
        self._defer("vector", f, inc)

    def ac(self, f, inc=True):
        self._defer("scalar", f, inc)

    def gp(self, f, inc=False):
        s = self.s_gp

        def g(e):
            f(e)
            if inc:
                e.maybe_drain_then_inc((s.h, 1))
                s.inc()
        self.block.gpsimd(g)

    def sy(self, f):
        self.block.sync(f)

    def barrier(self):
        self.pe(lambda t: None, inc=True)
        self.dv(lambda v: None, inc=True)
        self.ac(lambda e: None, inc=True)
        self.gp(lambda e: None, inc=True)
        for emit in (self.pe, self.dv, self.ac, self.gp):
            emit(lambda e: self.wait_all(e), inc=False)
        self.sy(lambda e: self.wait_all(e))

    def wpe(self, e, lag=0):
        e.wait_ge(self.s_pe.h, max(0, self._curv(self.s_pe) - lag))

    def wdv(self, e, lag=0):
        e.wait_ge(self.s_dv.h, max(0, self._curv(self.s_dv) - lag))

    def wac(self, e, lag=0):
        e.wait_ge(self.s_ac.h, max(0, self._curv(self.s_ac) - lag))

    def wgd(self, e):
        e.wait_ge(self.s_gd.h, self._curv(self.s_gd))

    def whd(self, e):
        e.wait_ge(self.s_hd.h, self._curv(self.s_hd))

    def wgp(self, e):
        e.wait_ge(self.s_gp.h, self._curv(self.s_gp))

    def row_bcast(self, row_ap, w, out_ap):
        def mm(t):
            self.wdv(t)
            t.matmul(self.psD[:, 0:w], self.onesrf[0:1, 0:128], row_ap,
                     start=True, stop=True)
        self.pe(mm)

        def cp(v):
            self.wpe(v)
            v.tensor_copy(out_ap, self.psD[:, 0:w])
        self.dv(cp)

    def stats(self, src_fn, n_tok, mu_row, rs_row, src_bf16=False, wf=None):
        c = self.cfg
        for j in range(ceil_div(n_tok, 512)):
            w = min(512, n_tok - j * 512)
            c0 = j * 512

            if not src_bf16:
                def cpb(v, w=w, c0=c0):
                    self.wpe(v)
                    if wf is not None:
                        wf(v)
                    for dc in range(c.DC):
                        v.tensor_copy(self.stg_b[:, dc * 512:dc * 512 + w],
                                      src_fn(dc)[:, c0:c0 + w])
                self.dv(cpb)

            def sqx(e, w=w, c0=c0):
                self.wpe(e)
                self.wdv(e)
                if wf is not None:
                    wf(e)
                for dc in range(c.DC):
                    e.activation(self.sqc[:, dc * 512:dc * 512 + w],
                                 src_fn(dc)[:, c0:c0 + w], AF.Square)
            self.ac(sqx)

            def mm(t, w=w, c0=c0):
                self.wdv(t)
                self.wac(t)
                for dc in range(c.DC):
                    sap = (src_fn(dc)[:, c0:c0 + w] if src_bf16
                           else self.stg_b[:, dc * 512:dc * 512 + w])
                    t.matmul(self.st_ps[0:1, 0:w], self.onesc[:, 0:1],
                             sap, start=(dc == 0), stop=(dc == c.DC - 1))
                for dc in range(c.DC):
                    t.matmul(self.st_ps[0:1, 512:512 + w], self.onesc[:, 0:1],
                             self.sqc[:, dc * 512:dc * 512 + w],
                             start=(dc == 0), stop=(dc == c.DC - 1))
            self.pe(mm)

            def fin(v, w=w, c0=c0):
                self.wpe(v)
                v.tensor_scalar(mu_row[0:1, c0:c0 + w], self.st_ps[0:1, 0:w],
                                1.0 / c.D, None, ALU.mult)
                v.tensor_scalar(self.rowb[0:1, 0:w], self.st_ps[0:1, 512:512 + w],
                                1.0 / c.D, None, ALU.mult)
            self.dv(fin)

            def fin_b(v, w=w, c0=c0):
                self.wdv(v)
                v.tensor_tensor(self.rowb[0:1, 512:512 + w],
                                mu_row[0:1, c0:c0 + w], mu_row[0:1, c0:c0 + w],
                                ALU.mult)
            self.dv(fin_b)

            def fin_c(v, w=w, c0=c0):
                self.wdv(v)
                v.tensor_tensor(self.rowb[0:1, 0:w], self.rowb[0:1, 0:w],
                                self.rowb[0:1, 512:512 + w], ALU.subtract)
            self.dv(fin_c)

            def fin_d(v, w=w, c0=c0):
                self.wdv(v)
                v.tensor_scalar(rs_row[0:1, c0:c0 + w], self.rowb[0:1, 0:w],
                                1e-5, None, ALU.add)
            self.dv(fin_d)
        def sq(e, n=n_tok):
            self.wdv(e)
            e.activation(rs_row[0:1, 0:n], rs_row[0:1, 0:n], AF.Sqrt)
        self.ac(sq)

        def rec(v, n=n_tok):
            self.wac(v)
            v.reciprocal(rs_row[0:1, 0:n], rs_row[0:1, 0:n])
        self.dv(rec)

    def build(self):
        self.prologue()
        with contextlib.ExitStack() as ph:
            self.alloc_attn(ph)
            for b in range(self.cfg.NB):
                self.attn_batch(b)
                if self.upto in ("ln1", "grp", "load", "stats"):
                    break
            self.flush_items()
        if self.upto in ("attn", "ln1", "grp", "load", "stats"):
            self.sy(lambda e: self.wait_all(e))
            self.es.close()
            return self.nc
        with contextlib.ExitStack() as ph:
            self.alloc_ffn(ph)
            for b in range(self.cfg.NB):
                self.ffn_batch(b)
                if self.upto in ("e3chk", "grp2"):
                    break
            self.flush_items()
        if self.upto in ("ffn", "e3chk", "grp2"):
            self.sy(lambda e: self.wait_all(e))
            self.es.close()
            return self.nc
        with contextlib.ExitStack() as ph:
            self.alloc_moe(ph)
            for b in range(self.cfg.NB):
                self.moe_batch(b)
            self.flush_items()
        self.sy(lambda e: self.wait_all(e))
        self.es.close()
        return self.nc

    def prologue(self):
        L = self

        def loads(g):
            g.dma_start(out=L.idf[:, :], in_=L.identf).then_inc(L.s_gd.inc(16), 16)
            g.dma_start(out=L.idb[:, :], in_=L.identf).then_inc(L.s_gd.inc(16), 16)
            g.dma_start(out=L.am0[:, :], in_=L.amask0).then_inc(L.s_gd.inc(16), 16)
            g.dma_start(out=L.am1[:, :], in_=L.amask1).then_inc(L.s_gd.inc(16), 16)
            g.dma_start(out=L.fm0[:, :], in_=L.fmask0).then_inc(L.s_gd.inc(16), 16)
            g.dma_start(out=L.sfo[:, :], in_=L.sufones).then_inc(L.s_gd.inc(16), 16)
            g.dma_start(out=L.lones[:, :], in_=L.lonesr).then_inc(L.s_gd.inc(16), 16)
            g.dma_start(out=L.bqkv_sb[:, :], in_=L.bqkvT).then_inc(L.s_gd.inc(16), 16)
            g.dma_start(out=L.bsw_sb[:, :], in_=L.bswr).then_inc(L.s_gd.inc(16), 16)
            g.memset(L.onesc[:, :], 1.0)
        self.gp(loads, inc=True)
        self.onescf = self.sbuf("onescf", 128, 1, F32)
        self.onesrf = self.sbuf("onesrf", 128, 128, F32)
        self.gp(lambda g: g.memset(L.onescf[:, :], 1.0), inc=True)
        self.gp(lambda g: g.memset(L.onesrf[:, :], 1.0), inc=True)
        self.barrier()

        def mm(t):
            t.matmul(L.psA[:, 0:L.cfg.E], L.onesrf[0:1, 0:128], L.bsw_sb[0:1, :],
                     start=True, stop=True)
        self.pe(mm)

        def cp(v):
            L.wpe(v)
            v.tensor_copy(L.bswb[:, :], L.psA[:, 0:L.cfg.E])
        self.dv(cp)
        self.barrier()

    def alloc_attn(self, ph):
        nc, c = self.nc, self.cfg
        en = ph.enter_context
        self.x_sb = en(nc.sbuf_tensor("x_sb", [128, c.DC * c.S], F32))
        self.xb_sb = en(nc.sbuf_tensor("xb_sb", [128, c.DC * c.S], BF16))
        self.wg = en(nc.sbuf_tensor("wg", [128, c.DC * c.WS], BF16))
        self.stg_b = en(nc.sbuf_tensor("stg_b", [128, c.DC * 512], BF16))
        self.sqc = en(nc.sbuf_tensor("sqc", [128, c.DC * 512], BF16))
        self.aoj = en(nc.sbuf_tensor("aoj", [128, c.DC * 512], BF16))
        GA = c.GA
        self.kbuf = en(nc.sbuf_tensor("kbuf", [128, (GA // 2) * c.N + 2], BF16))
        self.vbuf = en(nc.sbuf_tensor("vbuf", [128, (c.TT + 1) * GA * 65], BF16))
        self.qbuf = en(nc.sbuf_tensor("qbuf", [128, (GA // 2) * c.QCW], BF16))
        self.expS = en(nc.sbuf_tensor("expS", [128, 2 * c.QCW], BF16))
        self.zrow = en(nc.sbuf_tensor("zrow", [128, 512], F32))
        self.zbb = en(nc.sbuf_tensor("zbb", [128, 512], F32))
        self.mvb = en(nc.sbuf_tensor("mvb", [128, 1024], F32))
        self.gq = en(nc.sbuf_tensor("gq", [128, GA], BF16))
        self.gcol = en(nc.sbuf_tensor("gcol", [128, c.DC], BF16))

    def ln_apply(self, mu_row, rs_row, src_sb, dst_sb, dst_dtype_note=None,
                 residual_sb=None):
        c, L = self.cfg, self
        for j in range(c.QC):
            c0 = j * 512
            self.row_bcast(rs_row[0:1, c0:c0 + 512], 512, L.zbb[:, 0:512])
            self.row_bcast(mu_row[0:1, c0:c0 + 512], 512, L.zrow[:, 0:512])

            def ap(v, c0=c0):
                L.wdv(v)
                for dc in range(c.DC):
                    sl = slice(dc * c.S + c0, dc * c.S + c0 + 512)
                    v.tensor_tensor(L.mvb[:, 0:512], src_sb[:, sl],
                                    L.zrow[:, 0:512], ALU.subtract)
                    if residual_sb is None:
                        v.tensor_tensor(dst_sb[:, sl], L.mvb[:, 0:512],
                                        L.zbb[:, 0:512], ALU.mult)
                    else:
                        v.tensor_tensor(L.mvb[:, 512:1024], L.mvb[:, 0:512],
                                        L.zbb[:, 0:512], ALU.mult)
                        v.tensor_tensor(dst_sb[:, sl], residual_sb[:, sl],
                                        L.mvb[:, 512:1024], ALU.add)
            self.dv(ap)

    def spill_fm(self, sb_src, hbm_dst, b, ncols, engine="gp"):
        c, L = self.cfg, self

        def go(g):
            L.wdv(g)
            for dc in range(c.DC):
                g.dma_start(out=hbm_dst[b, dc, :, 0:ncols],
                            in_=sb_src[:, dc * ncols:(dc + 1) * ncols]) \
                 .then_inc((L.s_gd if engine == "gp" else L.s_hd).inc(16), 16)
        if engine == "gp":
            self.gp(go)
        else:
            self.sy(go)

    def _dbg_spill(self, b):
        L = self
        c = self.cfg
        S, DC = c.S, c.DC

        def spdbg(sy, b=b):
            self.wait_all(sy)
            for dc in range(DC):
                sy.dma_start(out=L.x1_out[b, dc],
                             in_=L.x_sb[:, dc * S:(dc + 1) * S]) \
                  .then_inc(L.s_hd.inc(16), 16)
            sy.wait_ge(L.s_hd.h, L.s_hd.v)
        self.sy(spdbg)

    def attn_batch(self, b):
        c, L = self.cfg, self
        S, D, DC, QC, TT, GA = c.S, c.D, c.DC, c.QC, c.TT, c.GA

        def ldx(sy):
            for dc in range(DC):
                sy.dma_start(out=L.x_sb[:, dc * S:(dc + 1) * S],
                             in_=L.xT[b, dc]).then_inc(L.s_hd.inc(16), 16)
        self.sy(ldx)
        self.emit_ldw_attn(0, "kvq")
        if self.upto == "load":
            self._dbg_spill(b)
            return
        self.stats(lambda dc: L.x_sb[:, dc * S:(dc + 1) * S], S, L.mu1, L.rs1,
                   wf=lambda e: L.whd(e))
        self.barrier()
        if self.upto == "stats":
            self._dbg_spill(b)
            return

        self.ln_apply(L.mu1, L.rs1, L.x_sb, L.xb_sb)

        def gl(g):
            for dc in range(DC):
                g.dma_start(out=L.gcol[:, dc:dc + 1],
                            in_=L.gtok[dc].rearrange("(p o) -> p o", o=1)) \
                 .then_inc(L.s_gd.inc(16), 16)
        self.gp(gl)
        self.barrier()

        if self.upto == "ln1":
            def spdbg(sy, b=b):
                self.wait_all(sy)
                for dc in range(DC):
                    sy.dma_start(out=L.x1_out[b, dc],
                                 in_=L.x_sb[:, dc * S:(dc + 1) * S]) \
                      .then_inc(L.s_hd.inc(16), 16)
            self.sy(spdbg)
            return
        for grp in range(c.H // GA):
            h0 = grp * GA
            self.attn_group(b, h0)
        self.barrier()
        if self.upto == "grp":
            def spdbg(sy, b=b):
                self.wait_all(sy)
                for dc in range(DC):
                    sy.dma_start(out=L.x1_out[b, dc],
                                 in_=L.x_sb[:, dc * S:(dc + 1) * S]) \
                      .then_inc(L.s_hd.inc(16), 16)
            self.sy(spdbg)
            return
        self.attn_tail(b)

    def emit_ldw_attn(self, h0, parts, wait_pe=False):
        c, L = self.cfg, self
        D, DC, WS, GA = c.D, c.DC, c.WS, c.GA
        G64 = GA * 64
        offs = []
        if "q" in parts:
            offs.append((0, h0 * 64))
        if "k" in parts:
            offs.append((1, D + h0 * 64))
        if "v" in parts:
            offs.append((2, 2 * D + h0 * 64))

        def ldw(g):
            if wait_pe:
                L.wpe(g)
            for dc in range(DC):
                for ti, off in offs:
                    g.dma_start(
                        out=L.wg[:, dc * WS + ti * G64:dc * WS + (ti + 1) * G64],
                        in_=L.qkv_wT[dc][:, off:off + G64]) \
                     .then_inc(L.s_gd.inc(16), 16)
        self.gp(ldw)

    def attn_group(self, b, h0):
        c, L = self.cfg, self
        S, D, DC, QC, TT, GA = c.S, c.D, c.DC, c.QC, c.TT, c.GA
        N = c.N
        WS = c.WS
        G64 = GA * 64

        def mmg(t, h0=h0):
            L.wgd(t)
            L.wdv(t)
            for p in range(GA // 2):
                for dc in range(DC):
                    t.matmul(L.psC[:, p:p + 1],
                             L.wg[:, dc * WS + G64 + p * 128:dc * WS + G64 + (p + 1) * 128],
                             L.gcol[:, dc:dc + 1],
                             start=(dc == 0), stop=(dc == DC - 1))
                for dc in range(DC):
                    t.matmul(L.psC[:, GA // 2 + p:GA // 2 + p + 1],
                             L.wg[:, dc * WS + 2 * G64 + p * 128:dc * WS + 2 * G64 + (p + 1) * 128],
                             L.gcol[:, dc:dc + 1],
                             start=(dc == 0), stop=(dc == DC - 1))
        self.pe(mmg)

        def cpgq(v):
            L.wpe(v)
            v.tensor_copy(L.gq[:, 0:GA], L.psC[:, 0:GA])
        self.dv(cpgq)

        kpairs = [[L.psA, L.psB], [L.psC, L.psD]]
        it = 0
        for p in range(GA // 2):
            for j0 in range(0, QC, 2):
                js = [j for j in (j0, j0 + 1) if j < QC]
                kb = kpairs[it % 2]
                it += 1

                def mm(t, p=p, js=js, kb=kb):
                    L.wdv(t, lag=1)
                    for ix, j in enumerate(js):
                        for dc in range(DC):
                            t.matmul(kb[ix][:, 0:512],
                                     L.wg[:, dc * WS + G64 + p * 128:dc * WS + G64 + (p + 1) * 128],
                                     L.xb_sb[:, dc * S + j * 512:dc * S + (j + 1) * 512],
                                     start=(dc == 0), stop=(dc == DC - 1))
                self.pe(mm)

                def cp(v, p=p, js=js, kb=kb):
                    L.wpe(v)
                    for ix, j in enumerate(js):
                        v.tensor_copy(L.kbuf[:, p * N + j * 512:p * N + (j + 1) * 512],
                                      kb[ix][:, 0:512])
                self.dv(cp)
            def cpg(v, p=p):
                v.tensor_copy(L.kbuf[:, p * N + S:p * N + S + 1],
                              L.gq[:, p:p + 1])
            self.dv(cpg)

        for t0 in range(0, TT, 2):
            tts = [tt for tt in (t0, t0 + 1) if tt < TT]
            vb = L.psB if (t0 // 2) % 2 == 0 else L.psD

            def mm(t, tts=tts, h0=h0, vb=vb):
                L.wdv(t, lag=1)
                for ix, tt in enumerate(tts):
                    for dc in range(DC):
                        t.matmul(vb[:, ix * G64:ix * G64 + G64],
                                 L.xb_sb[:, dc * S + tt * 128:dc * S + (tt + 1) * 128],
                                 L.wg[:, dc * WS + 2 * G64:dc * WS + 3 * G64],
                                 start=(dc == 0), stop=(dc == DC - 1))
            self.pe(mm)

            def cp(v, tts=tts, vb=vb):
                L.wpe(v)
                for ix, tt in enumerate(tts):
                    for i in range(GA):
                        base = (tt * GA + i) * 65
                        v.tensor_copy(L.vbuf[:, base:base + 64],
                                      vb[:, ix * G64 + i * 64:ix * G64 + (i + 1) * 64])
                    v.memset(L.vbuf[:, tt * GA * 65:(tt + 1) * GA * 65].rearrange(
                        "p (i c) -> p i c", i=GA)[:, :, 64:65].rearrange("p i c -> p (i c)"), 1.0)
            self.dv(cp)
        for i in range(GA):
            p_, sub = divmod(i, 2)
            oc = GA // 2 + p_
            r0 = sub * 64

            def tp(t, i=i, oc=oc, r0=r0):
                L.wdv(t)
                t.transpose(L.tp_ps[0:1, 0:64], L.gq[r0:r0 + 64, oc:oc + 1],
                            L.idb[r0:r0 + 64, r0:r0 + 64])
            self.pe(tp)

            def cp(v, i=i):
                L.wpe(v)
                base = (TT * GA + i) * 65
                v.tensor_copy(L.vbuf[0:1, base:base + 64], L.tp_ps[0:1, 0:64])
                v.memset(L.vbuf[0:1, base + 64:base + 65], 1.0)
            self.dv(cp)
        self.barrier()
        nxt = h0 + GA if h0 + GA < self.cfg.H else None
        if nxt is not None:
            self.emit_ldw_attn(nxt, "kv")

        for qc in range(QC):
            qb = [L.psA, L.psB]

            def mmq(t, qc=qc):
                L.wdv(t)
                for p in range(GA // 2):
                    for dc in range(DC):
                        t.matmul(qb[p][:, 0:512],
                                 L.wg[:, dc * WS + p * 128:dc * WS + (p + 1) * 128],
                                 L.xb_sb[:, dc * S + qc * 512:dc * S + (qc + 1) * 512],
                                 start=(dc == 0), stop=(dc == DC - 1))
            self.pe(mmq)
            if qc == QC - 1 and nxt is not None:
                self.emit_ldw_attn(nxt, "q", wait_pe=True)

            def cpq(v):
                L.wpe(v)
                for p in range(GA // 2):
                    v.tensor_scalar(L.qbuf[:, p * 512:(p + 1) * 512],
                                    qb[p][:, 0:512], 0.125, None, ALU.mult)
            self.dv(cpq)

            ktn = min(TT, 4 * qc + 5)
            sbanks = [L.psA, L.psB, L.psD]
            for i in range(GA):
                p, sub = divmod(i, 2)
                r0 = sub * 64
                first = [True]

                def mk_av(kt, qlo, lag):
                    def av(t, i=i, kt=kt, qlo=qlo, first=first, lag=lag):
                        L.wdv(t, lag=lag)
                        t.matmul(L.psC[0:65, qlo:512],
                                 L.vbuf[:, (kt * GA + i) * 65:(kt * GA + i + 1) * 65],
                                 L.expS[:, (kt % 2) * 512 + qlo:(kt % 2) * 512 + 512],
                                 start=first[0], stop=False)
                        first[0] = False
                    return av

                prev = None
                for kt in range(ktn):
                    bank = sbanks[kt % 3]
                    qlo = max(0, kt - 4 * qc - 1) * 128

                    def sc_mm(t, p=p, r0=r0, kt=kt, qlo=qlo, bank=bank):
                        if kt < 3:
                            L.wac(t)
                            if kt == 0:
                                L.wdv(t)
                        else:
                            L.wac(t, lag=2)
                        t.matmul(bank[:, qlo:512],
                                 L.kbuf[r0:r0 + 64, p * N + kt * 128:p * N + (kt + 1) * 128],
                                 L.qbuf[r0:r0 + 64, p * 512 + qlo:p * 512 + 512],
                                 start=True, stop=True)
                    self.pe(sc_mm)

                    def ex(e, kt=kt, qlo=qlo, bank=bank):
                        L.wpe(e)
                        e.activation(L.expS[:, (kt % 2) * 512 + qlo:(kt % 2) * 512 + 512],
                                     bank[:, qlo:512], AF.Exp)
                    self.ac(ex)

                    def msk(v, kt=kt, qc=qc):
                        L.wac(v)
                        qs0 = kt - 4 * qc
                        qs1 = kt - 4 * qc - 1
                        h = (kt % 2) * 512
                        if 0 <= qs0 < 4:
                            v.tensor_tensor(
                                L.expS[:, h + qs0 * 128:h + (qs0 + 1) * 128],
                                L.expS[:, h + qs0 * 128:h + (qs0 + 1) * 128],
                                L.am0[:, :], ALU.mult)
                        if 0 <= qs1 < 4:
                            v.tensor_tensor(
                                L.expS[:, h + qs1 * 128:h + (qs1 + 1) * 128],
                                L.expS[:, h + qs1 * 128:h + (qs1 + 1) * 128],
                                L.am1[:, :], ALU.mult)
                    self.dv(msk)

                    if prev is not None:
                        self.pe(mk_av(prev[0], prev[1], 1), inc=False)
                    prev = (kt, qlo)
                self.pe(mk_av(prev[0], prev[1], 0), inc=False)

                def sc_g(t, p=p, r0=r0):
                    t.matmul(L.psB[0:1, 0:512],
                             L.kbuf[r0:r0 + 64, p * N + S:p * N + S + 1],
                             L.qbuf[r0:r0 + 64, p * 512:p * 512 + 512],
                             start=True, stop=True)
                self.pe(sc_g)

                def ex_g(e):
                    L.wpe(e)
                    e.activation(L.expS[0:1, 0:512], L.psB[0:1, 0:512], AF.Exp)
                self.ac(ex_g)

                def av_g(t, i=i):
                    L.wac(t)
                    t.matmul(L.psC[0:65, 0:512],
                             L.vbuf[0:1, (TT * GA + i) * 65:(TT * GA + i + 1) * 65],
                             L.expS[0:1, 0:512],
                             start=False, stop=True)
                self.pe(av_g)

                def rz(v):
                    L.wpe(v)
                    v.reciprocal(L.zrow[64:65, 0:512], L.psC[64:65, 0:512])
                self.dv(rz)

                def zb(t):
                    L.wdv(t)
                    t.matmul(L.psB[0:64, 0:512], L.onesrf[64:65, 0:64],
                             L.zrow[64:65, 0:512], start=True, stop=True)
                self.pe(zb)

                def ev(v, i=i, qc=qc, h0=h0, b=b):
                    L.wpe(v)
                    L.wgd(v)
                    v.tensor_copy(L.zbb[0:64, 0:512], L.psB[0:64, 0:512])
                    hh = h0 + i
                    r0o = (hh % 2) * 64
                    v.scalar_tensor_tensor(
                        L.aoj[r0o:r0o + 64, i * 512:(i + 1) * 512],
                        L.psC[0:64, 0:512], 1.0, L.zbb[0:64, 0:512],
                        ALU.mult, ALU.mult)
                self.dv(ev)

                def sp(g, i=i, qc=qc, h0=h0, b=b):
                    L.wdv(g)
                    hh = h0 + i
                    dc = hh // 2
                    r0o = (hh % 2) * 64
                    g.dma_start(
                        out=L.ao_hbm[b, dc, r0o:r0o + 64, qc * 512:(qc + 1) * 512],
                        in_=L.aoj[r0o:r0o + 64, i * 512:(i + 1) * 512]) \
                     .then_inc(L.s_gd.inc(16), 16)
                self.gp(sp)
                self.pe(lambda t: L.wdv(t), inc=False)

    def attn_tail(self, b):
        c, L = self.cfg, self
        S, D, DC, QC = c.S, c.D, c.DC, c.QC
        WS = c.WS

        def ldw(g):
            L.wpe(g)
            h1 = min(512, D)
            for dc in range(DC):
                g.dma_start(out=L.stg_b[:, dc * 512:dc * 512 + h1],
                            in_=L.proj_wT[dc][:, 0:h1]) \
                 .then_inc(L.s_gd.inc(16), 16)
                if D > 512:
                    g.dma_start(out=L.sqc[:, dc * 512:dc * 512 + D - 512],
                                in_=L.proj_wT[dc][:, 512:D]) \
                     .then_inc(L.s_gd.inc(16), 16)
            g.dma_start(out=L.zrow[:, 0:DC], in_=L.proj_bT) \
             .then_inc(L.s_gd.inc(16), 16)
        self.gp(ldw)
        self.barrier()

        def pwsl(dc, oc):
            col = oc * 128
            if col < 512:
                return L.stg_b[:, dc * 512 + col:dc * 512 + col + 128]
            return L.sqc[:, dc * 512 + col - 512:dc * 512 + col - 512 + 128]

        for j in range(QC):
            def lda(g, j=j, b=b):
                L.wpe(g)
                for dc in range(DC):
                    g.dma_start(out=L.aoj[:, dc * 512:(dc + 1) * 512],
                                in_=L.ao_hbm[b, dc, :, j * 512:(j + 1) * 512]) \
                     .then_inc(L.s_gd.inc(16), 16)
            self.gp(lda)
            ppairs = [[L.psA, L.psB], [L.psC, L.psD]]
            for oc0 in range(0, DC, 2):
                ocs = [oc for oc in (oc0, oc0 + 1) if oc < DC]
                pb = ppairs[(oc0 // 2) % 2]

                def mm(t, j=j, ocs=ocs, pb=pb):
                    L.wdv(t, lag=1)
                    L.wgd(t)
                    for ix, oc in enumerate(ocs):
                        for dc in range(DC):
                            t.matmul(pb[ix][:, 0:512],
                                     pwsl(dc, oc),
                                     L.aoj[:, dc * 512:(dc + 1) * 512],
                                     start=(dc == 0), stop=(dc == DC - 1))
                self.pe(mm)

                def cp(v, j=j, ocs=ocs, pb=pb):
                    L.wpe(v)
                    for ix, oc in enumerate(ocs):
                        v.tensor_scalar(L.xb_sb[:, oc * S + j * 512:oc * S + (j + 1) * 512],
                                        pb[ix][:, 0:512], L.zrow[:, oc:oc + 1],
                                        None, ALU.add)
                self.dv(cp)
        self.barrier()

        self.stats(lambda dc: L.xb_sb[:, dc * S:(dc + 1) * S], S, L.mu3, L.rs3,
                   src_bf16=True)
        self.barrier()
        self.ln_apply(L.mu3, L.rs3, L.xb_sb, L.x_sb, residual_sb=L.x_sb)
        self.barrier()
        def spx(sy, b=b):
            for dc in range(DC):
                sy.dma_start(out=L.x1_out[b, dc],
                             in_=L.x_sb[:, dc * S:(dc + 1) * S]) \
                  .then_inc(L.s_hd.inc(16), 16)
        self.sy(spx)

        self.stats(lambda dc: L.x_sb[:, dc * S:(dc + 1) * S], S, L.mu3, L.rs3)
        self.barrier()

        def sp3(g, b=b):
            g.dma_start(out=L.ln3_out[b, 0:1, :], in_=L.mu3[0:1, :]) \
             .then_inc(L.s_gd.inc(16), 16)
            g.dma_start(out=L.ln3_out[b, 1:2, :], in_=L.rs3[0:1, :]) \
             .then_inc(L.s_gd.inc(16), 16)
        self.gp(sp3)

        self.ln_apply(L.mu3, L.rs3, L.x_sb, L.xb_sb)
        self.barrier()
        self.spill_fm(L.xb_sb, L.e3_hbm, b, S)
        self.barrier()

    def alloc_ffn(self, ph):
        nc, c = self.nc, self.cfg
        en = ph.enter_context
        GF = c.GA
        self.GF = GF
        self.e_sb = en(nc.sbuf_tensor("e_sb", [128, c.DC * c.S], BF16))
        self.wg2 = en(nc.sbuf_tensor("wg2", [128, c.DC * c.WS2], BF16))
        self.kbuf = en(nc.sbuf_tensor("kbuf2", [128, (GF // 2) * c.S], BF16))
        self.vue = en(nc.sbuf_tensor("vue", [128, c.TT * GF * 73], BF16))
        self.vsuf = en(nc.sbuf_tensor("vsuf", [128, GF * c.S], BF16))
        self.tsuf = en(nc.sbuf_tensor("tsuf", [128, 2 * GF], F32))
        self.qbuf = en(nc.sbuf_tensor("qbuf2", [128, (GF // 2) * c.QCW], BF16))
        self.expS = en(nc.sbuf_tensor("expS2", [128, 2 * c.QCW], BF16))
        self.av_sb = en(nc.sbuf_tensor("av_sb", [128, c.QCW], BF16))
        self.hid_tm = en(nc.sbuf_tensor("hid_tm", [128, c.TT * c.D], BF16))
        self.lacc = en(nc.sbuf_tensor("lacc", [128, c.TT * c.E], F32))
        self.rzcol = en(nc.sbuf_tensor("rzcol", [128, 2], F32))
        self.bvcol = en(nc.sbuf_tensor("bvcol", [128, 1], BF16))
        self.M2 = en(nc.sbuf_tensor("M2", [128, c.DC * c.H * c.E], BF16))
        self.ubias = en(nc.sbuf_tensor("ubias", [c.E, c.H + 2], F32))
        self.bvb = en(nc.sbuf_tensor("bvb", [128, c.H * 64], F32))
        self.bvr_sb = en(nc.sbuf_tensor("bvr_sb", [1, c.H * 64], F32))
        self.zrow = en(nc.sbuf_tensor("zrow2", [128, 512], F32))
        self.mvb = en(nc.sbuf_tensor("mvb2", [128, 512], F32))
        self.iota8 = en(nc.sbuf_tensor("iota8", [128, c.E], F32))
        self.iota8m = en(nc.sbuf_tensor("iota8m", [128, c.E], F32))
        self.iotap = en(nc.sbuf_tensor("iotap", [128, 1], F32))
        self.probs_sb = en(nc.sbuf_tensor("probs_sb", [128, c.TT], F32))
        self.routes_sb = en(nc.sbuf_tensor("routes_sb", [128, c.TT * 8], U32))
        self.rf32_sb = en(nc.sbuf_tensor("rf32_sb", [128, c.TT], F32))
        self.max8 = en(nc.sbuf_tensor("max8", [128, 8], F32))
        self.mi8 = en(nc.sbuf_tensor("mi8", [128, 8], U32))
        self.rscr = en(nc.sbuf_tensor("rscr", [128, 10], F32))
        self.ohf = en(nc.sbuf_tensor("ohf", [128, c.E], F32))
        self.ohb = en(nc.sbuf_tensor("ohb", [128, c.E], BF16))
        self.ohT = en(nc.sbuf_tensor("ohT", [c.E, c.S], F32))
        self.scan = en(nc.sbuf_tensor("scan", [c.E, c.S], F32))
        self.pos_sb = en(nc.sbuf_tensor("pos_sb", [128, c.TT], F32))
        self.cntc = en(nc.sbuf_tensor("cntc", [c.E, 4], F32))
        self.cnt_u = en(nc.sbuf_tensor("cnt_u", [c.E, 1], U32))
        self.base_r = en(nc.sbuf_tensor("base_r", [1, c.E], F32))
        self.baseb = en(nc.sbuf_tensor("baseb", [128, c.E], F32))
        self.slotu = en(nc.sbuf_tensor("slotu", [128, 1], U32))
        self.slotu2 = en(nc.sbuf_tensor("slotu2", [128, 2], U32))
        self.tokid = en(nc.sbuf_tensor("tokid", [128, 1], U32))
        self.sentc = en(nc.sbuf_tensor("sentc", [128, c.NCH], U32))
        L = self

        def pro(g):
            g.iota(L.iota8[:, :], [[1, c.E]], base=0, channel_multiplier=0,
                   allow_small_or_imprecise_dtypes=True)
            g.iota(L.iota8m[:, :], [[1, c.E]], base=-c.E, channel_multiplier=0,
                   allow_small_or_imprecise_dtypes=True)
            g.iota(L.iotap[:, :], [[0, 1]], base=0, channel_multiplier=1,
                   allow_small_or_imprecise_dtypes=True)
            g.iota(L.iotaNCH[:, :], [[128, c.NCH]], base=0, channel_multiplier=0,
                   allow_small_or_imprecise_dtypes=True)
            g.memset(L.sentc[:, :], c.SENT)
            g.memset(L.lacc[:, :], 0.0)
            g.dma_start(out=L.bvr_sb[:, :], in_=L.bvr).then_inc(L.s_gd.inc(16), 16)
            for r in range(c.H * 64 // 128):
                g.dma_start(out=L.hid_tm[:, r * c.D:(r + 1) * c.D],
                            in_=L.Wv_nat[r * 128:(r + 1) * 128, :]) \
                 .then_inc(L.s_gd.inc(16), 16)
            for dc in range(c.DC):
                g.dma_start(out=L.vsuf[:, dc * c.E:(dc + 1) * c.E],
                            in_=L.WswT[dc * 128:(dc + 1) * 128, :]) \
                 .then_inc(L.s_gd.inc(16), 16)
        self.gp(pro, inc=True)
        self.barrier()

        for j in range(ceil_div(c.H * 64, 512)):
            w = min(512, c.H * 64 - j * 512)
            self.row_bcast(L.bvr_sb[0:1, j * 512:j * 512 + w], w,
                           L.bvb[:, j * 512:j * 512 + w])
        for h in range(c.H):
            r, sub = divmod(h, 2)
            rb = sub * 64
            for dc in range(c.DC):
                def mm(t, h=h, r=r, rb=rb, dc=dc):
                    L.wdv(t)
                    t.matmul(L.psA[:, 0:c.E],
                             L.hid_tm[rb:rb + 64, r * c.D + dc * 128:r * c.D + (dc + 1) * 128],
                             L.vsuf[rb:rb + 64, (dc if False else ((h * 64) // 128) * c.E):((h * 64) // 128) * c.E + c.E]
                             if False else L.vsuf[rb:rb + 64, ((h * 64) // 128) * c.E:((h * 64) // 128 + 1) * c.E],
                             start=True, stop=True)
                self.pe(mm)

                def cp(v, h=h, dc=dc):
                    L.wpe(v)
                    v.tensor_copy(L.M2[:, (dc * c.H + h) * c.E:(dc * c.H + h + 1) * c.E],
                                  L.psA[:, 0:c.E])
                self.dv(cp)
            def cpb(v, h=h):
                v.tensor_copy(L.bvcol[:, 0:1],
                              L.bqkv_sb[:, 2 * (c.H // 2) + h // 2:2 * (c.H // 2) + h // 2 + 1])
            self.dv(cpb)

            def mmu(t, h=h, rb=rb):
                L.wdv(t)
                t.matmul(L.psB[0:c.E, 0:1],
                         L.vsuf[rb:rb + 64, ((h * 64) // 128) * c.E:((h * 64) // 128 + 1) * c.E],
                         L.bvcol[rb:rb + 64, 0:1],
                         start=True, stop=True)
            self.pe(mmu)

            def cpu_(v, h=h):
                L.wpe(v)
                v.tensor_copy(L.ubias[:, h:h + 1], L.psB[0:c.E, 0:1])
            self.dv(cpu_)
        self.barrier()
        def us(v):
            v.reduce_sum(L.ubias[:, c.H:c.H + 1], L.ubias[:, 0:c.H], axis=AX)
        self.dv(us)

        def ut(t):
            L.wdv(t)
            t.transpose(L.psB[0:1, 0:c.E], L.ubias[0:c.E, c.H:c.H + 1],
                        L.idf[0:c.E, 0:c.E])
        self.pe(ut)

        def uc(v):
            L.wpe(v)
            v.tensor_copy(L.rowb[0:1, 0:c.E], L.psB[0:1, 0:c.E])
        self.dv(uc)

        def ub(t):
            L.wdv(t)
            t.matmul(L.psB[:, 0:c.E], L.onesrf[0:1, 0:128], L.rowb[0:1, 0:c.E],
                     start=True, stop=True)
        self.pe(ub)

        def ua(v):
            L.wpe(v)
            v.tensor_tensor(L.bswb[:, :], L.bswb[:, :], L.psB[:, 0:c.E], ALU.add)
        self.dv(ua)
        self.barrier()

    def ffn_batch(self, b):
        c, L = self.cfg, self
        S, D, DC, QC, TT, E = c.S, c.D, c.DC, c.QC, c.TT, c.E
        GF = self.GF
        HP2 = c.H // 2

        def lde(g, b=b):
            for dc in range(DC):
                g.dma_start(out=L.e_sb[:, dc * S:(dc + 1) * S],
                            in_=L.e3_hbm[b, dc]).then_inc(L.s_gd.inc(16), 16)
        self.gp(lde)
        self.emit_ldw_ffn(0, "kvq")
        self.dv(lambda v: v.memset(L.lacc[:, :], 0.0))
        self.barrier()
        if self.upto == "e3chk":
            def dump(g, b=b):
                for dc in range(DC):
                    g.dma_start(
                        out=L.hid_out[b].rearrange("s d -> (s d)").rearrange(
                            "(c p f) -> c p f", c=DC, p=128)[dc],
                        in_=L.e_sb[:, dc * S:(dc + 1) * S]) \
                     .then_inc(L.s_gd.inc(16), 16)
            self.gp(dump)
            self.barrier()
            return

        for grp in range(c.H // GF):
            self.ffn_group(b, grp * GF)
        self.barrier()
        if self.upto == "grp2":
            self.ffn_route(b)
            return
        self.ffn_route(b)

    def emit_ldw_ffn(self, h0, parts, wait_pe=False):
        c, L = self.cfg, self
        D, DC, WS2, GF = c.D, c.DC, c.WS2, self.GF
        G64 = GF * 64
        offs = []
        if "q" in parts:
            offs.append((0, h0 * 64))
        if "k" in parts:
            offs.append((1, D + h0 * 64))
        if "v" in parts:
            offs.append((2, 2 * D + h0 * 64))

        def ldw(g):
            if wait_pe:
                L.wpe(g)
            for dc in range(DC):
                for ti, off in offs:
                    g.dma_start(
                        out=L.wg2[:, dc * WS2 + ti * G64:dc * WS2 + (ti + 1) * G64],
                        in_=L.WqkvT[dc][:, off:off + G64]) \
                     .then_inc(L.s_gd.inc(16), 16)
        self.gp(ldw)

    def ffn_group(self, b, h0):
        c, L = self.cfg, self
        S, D, DC, QC, TT, E = c.S, c.D, c.DC, c.QC, c.TT, c.E
        GF = self.GF
        HP2 = c.H // 2
        WS2 = c.WS2
        G64 = GF * 64

        kpairs2 = [[L.psA, L.psB], [L.psC, L.psD]]
        it2 = 0
        for p in range(GF // 2):
            hh = h0 + 2 * p
            bcol = HP2 + hh // 2
            for j0 in range(0, QC, 2):
                js = [j for j in (j0, j0 + 1) if j < QC]
                kb2 = kpairs2[it2 % 2]
                it2 += 1

                def mm(t, p=p, js=js, kb2=kb2):
                    L.wdv(t, lag=1)
                    L.wgd(t)
                    for ix, j in enumerate(js):
                        for dc in range(DC):
                            t.matmul(kb2[ix][:, 0:512],
                                     L.wg2[:, dc * WS2 + G64 + p * 128:dc * WS2 + G64 + (p + 1) * 128],
                                     L.e_sb[:, dc * S + j * 512:dc * S + (j + 1) * 512],
                                     start=(dc == 0), stop=(dc == DC - 1))
                self.pe(mm)

                def cp(v, p=p, js=js, bcol=bcol, kb2=kb2):
                    L.wpe(v)
                    for ix, j in enumerate(js):
                        v.tensor_scalar(L.kbuf[:, p * S + j * 512:p * S + (j + 1) * 512],
                                        kb2[ix][:, 0:512],
                                        L.bqkv_sb[:, bcol:bcol + 1], None, ALU.add)
                self.dv(cp)

        vpairs = [[L.psA, L.psB], [L.psD, L.psC]]
        for t0 in range(0, TT, 2):
            tts = [tt for tt in (t0, t0 + 1) if tt < TT]
            vb, ub = vpairs[(t0 // 2) % 2]

            def mmv(t, tts=tts, h0=h0, vb=vb, ub=ub):
                L.wdv(t, lag=1)
                for ix, tt in enumerate(tts):
                    for dc in range(DC):
                        t.matmul(vb[:, ix * G64:ix * G64 + G64],
                                 L.e_sb[:, dc * S + tt * 128:dc * S + (tt + 1) * 128],
                                 L.wg2[:, dc * WS2 + 2 * G64:dc * WS2 + 3 * G64],
                                 start=(dc == 0), stop=(dc == DC - 1))
                    for dc in range(DC):
                        t.matmul(ub[:, ix * GF * E:(ix + 1) * GF * E],
                                 L.e_sb[:, dc * S + tt * 128:dc * S + (tt + 1) * 128],
                                 L.M2[:, (dc * c.H + h0) * E:(dc * c.H + h0 + GF) * E],
                                 start=(dc == 0), stop=(dc == DC - 1))
            self.pe(mmv)

            def cpv(v, tts=tts, h0=h0, vb=vb, ub=ub):
                L.wpe(v)
                for ix, tt in enumerate(tts):
                    for i in range(GF):
                        base = (tt * GF + i) * 73
                        v.tensor_tensor(L.vue[:, base:base + 64],
                                        vb[:, ix * G64 + i * 64:ix * G64 + (i + 1) * 64],
                                        L.bvb[:, (h0 + i) * 64:(h0 + i + 1) * 64],
                                        ALU.add)
                        v.memset(L.vue[:, base + 64:base + 65], 1.0)
                        v.tensor_copy(L.vue[:, base + 65:base + 73],
                                      ub[:, ix * GF * E + i * E:ix * GF * E + (i + 1) * E])
            self.dv(cpv)
        self.barrier()

        for i in range(GF):
            def z0(v, i=i):
                v.memset(L.tsuf[:, 2 * i:2 * i + 1], 0.0)
            self.dv(z0)
        for tt in range(TT - 1, -1, -1):
            for i0 in range(0, GF, 2):
                iis = [i for i in (i0, i0 + 1) if i < GF]

                def mm(t, tt=tt, iis=iis):
                    L.wdv(t)
                    for ix, i in enumerate(iis):
                        t.matmul(L.psD[0:73, ix * 136:ix * 136 + 129],
                                 L.vue[:, (tt * GF + i) * 73:(tt * GF + i + 1) * 73],
                                 L.sfo[:, :], start=True, stop=True)
                self.pe(mm)

                def fin(v, tt=tt, iis=iis):
                    L.wpe(v)
                    for ix, i in enumerate(iis):
                        v.tensor_scalar(L.vsuf[0:73, i * S + tt * 128:i * S + (tt + 1) * 128],
                                        L.psD[0:73, ix * 136:ix * 136 + 128],
                                        L.tsuf[0:73, 2 * i:2 * i + 1], None, ALU.add)
                self.dv(fin)

                def fint(v, tt=tt, iis=iis):
                    L.wdv(v)
                    for ix, i in enumerate(iis):
                        v.tensor_tensor(L.tsuf[0:73, 2 * i:2 * i + 1],
                                        L.tsuf[0:73, 2 * i:2 * i + 1],
                                        L.psD[0:73, ix * 136 + 128:ix * 136 + 129],
                                        ALU.add)
                self.dv(fint)
        self.barrier()
        nxt = h0 + GF if h0 + GF < c.H else None
        if nxt is not None:
            self.emit_ldw_ffn(nxt, "kv")

        for qc in range(QC):
            qb2 = [L.psA, L.psB]

            def mmq(t, qc=qc, h0=h0):
                L.wdv(t)
                for p in range(GF // 2):
                    for dc in range(DC):
                        t.matmul(qb2[p][:, 0:512],
                                 L.wg2[:, dc * WS2 + p * 128:dc * WS2 + (p + 1) * 128],
                                 L.e_sb[:, dc * S + qc * 512:dc * S + (qc + 1) * 512],
                                 start=(dc == 0), stop=(dc == DC - 1))
            self.pe(mmq)
            if qc == QC - 1 and nxt is not None:
                self.emit_ldw_ffn(nxt, "q", wait_pe=True)

            def cpq(v, h0=h0):
                L.wpe(v)
                for p in range(GF // 2):
                    bcol = (h0 + 2 * p) // 2
                    v.tensor_scalar(L.qbuf[:, p * 512:(p + 1) * 512],
                                    qb2[p][:, 0:512],
                                    L.bqkv_sb[:, bcol:bcol + 1],
                                    1.0 / math.sqrt(D), ALU.add, ALU.mult)
            self.dv(cpq)

            ktn = min(TT, 4 * qc + 4)
            sbanks2 = [L.psA, L.psB, L.psD]
            for i in range(GF):
                p, sub = divmod(i, 2)
                r0 = sub * 64
                first = [True]

                def mk_av(kt, qlo, lag, last):
                    def av(t, i=i, kt=kt, qlo=qlo, first=first, lag=lag,
                           last=last):
                        L.wdv(t, lag=lag)
                        t.matmul(L.psC[0:73, qlo:512],
                                 L.vue[:, (kt * GF + i) * 73:(kt * GF + i + 1) * 73],
                                 L.expS[:, (kt % 2) * 512 + qlo:(kt % 2) * 512 + 512],
                                 start=first[0], stop=last)
                        first[0] = False
                    return av

                prev = None
                for kt in range(ktn):
                    bank = sbanks2[kt % 3]
                    qlo = max(0, kt - 4 * qc) * 128

                    def smm(t, p=p, r0=r0, kt=kt, qlo=qlo, bank=bank):
                        if kt < 3:
                            L.wac(t)
                            if kt == 0:
                                L.wdv(t)
                        else:
                            L.wac(t, lag=2)
                        t.matmul(bank[:, qlo:512],
                                 L.kbuf[r0:r0 + 64, p * S + kt * 128:p * S + (kt + 1) * 128],
                                 L.qbuf[r0:r0 + 64, p * 512 + qlo:p * 512 + 512],
                                 start=True, stop=True)
                    self.pe(smm)

                    def ex(e, kt=kt, qlo=qlo, bank=bank):
                        L.wpe(e)
                        e.activation(L.expS[:, (kt % 2) * 512 + qlo:(kt % 2) * 512 + 512],
                                     bank[:, qlo:512], AF.Exp)
                    self.ac(ex)

                    def msk(v, kt=kt, qc=qc):
                        L.wac(v)
                        qs0 = kt - 4 * qc
                        h = (kt % 2) * 512
                        if 0 <= qs0 < 4:
                            v.tensor_tensor(
                                L.expS[:, h + qs0 * 128:h + (qs0 + 1) * 128],
                                L.expS[:, h + qs0 * 128:h + (qs0 + 1) * 128],
                                L.fm0[:, :], ALU.mult)
                    self.dv(msk)

                    if prev is not None:
                        self.pe(mk_av(prev[0], prev[1], 1, False), inc=False)
                    prev = (kt, qlo)
                self.pe(mk_av(prev[0], prev[1], 0, True), inc=True)

                def fin(v, i=i, qc=qc):
                    L.wpe(v)
                    v.tensor_tensor(L.av_sb[0:73, 0:512], L.psC[0:73, 0:512],
                                    L.vsuf[0:73, i * S + qc * 512:i * S + (qc + 1) * 512],
                                    ALU.add)
                self.dv(fin)
                for qs in range(4):
                    def tp(t, qs=qs):
                        L.wdv(t)
                        t.transpose(L.tp_ps[:, 0:73],
                                    L.av_sb[0:73, qs * 128:(qs + 1) * 128],
                                    L.idb[0:73, 0:73])
                    self.pe(tp)

                    def rcp(v, qs=qs):
                        L.wpe(v)
                        v.reciprocal(L.rzcol[:, 0:1], L.tp_ps[:, 64:65])
                    self.dv(rcp)

                    def nrm(v, i=i, qc=qc, qs=qs, h0=h0):
                        L.wdv(v)
                        tile = qc * 4 + qs
                        hh = h0 + i
                        v.tensor_scalar(L.hid_tm[:, tile * D + hh * 64:tile * D + hh * 64 + 64],
                                        L.tp_ps[:, 0:64],
                                        L.rzcol[:, 0:1], None, ALU.mult)
                        v.scalar_tensor_tensor(
                            L.lacc[:, tile * E:(tile + 1) * E],
                            L.tp_ps[:, 65:73], L.rzcol[:, 0:1],
                            L.lacc[:, tile * E:(tile + 1) * E],
                            ALU.mult, ALU.add)
                    self.dv(nrm)
                self.pe(lambda t: L.wdv(t), inc=False)

    def ffn_route(self, b):
        c, L = self.cfg, self
        S, D, DC, TT, E, NCH = c.S, c.D, c.DC, c.TT, c.E, c.NCH

        def sph(g, b=b):
            L.wdv(g)
            for tt in range(TT):
                g.dma_start(out=L.hid_out[b, tt * 128:(tt + 1) * 128, :],
                            in_=L.hid_tm[:, tt * D:(tt + 1) * D]) \
                 .then_inc(L.s_gd.inc(16), 16)
        self.gp(sph)

        for tt in range(TT):
            def fin(v, tt=tt):
                v.tensor_tensor(L.lacc[:, tt * E:(tt + 1) * E],
                                L.lacc[:, tt * E:(tt + 1) * E],
                                L.bswb[:, :], ALU.add)
            self.dv(fin)

            def fin_m(v, tt=tt):
                L.wdv(v)
                v.reduce_max(L.rscr[:, 0:1], L.lacc[:, tt * E:(tt + 1) * E], axis=AX)
            self.dv(fin_m)

            def fin2(v, tt=tt):
                L.wdv(v)
                v.tensor_scalar(L.rscr[:, 2:2 + E], L.lacc[:, tt * E:(tt + 1) * E],
                                L.rscr[:, 0:1], None, ALU.subtract)
            self.dv(fin2)

            def ex(e, tt=tt):
                L.wdv(e)
                e.activation(L.rscr[:, 2:2 + E], L.rscr[:, 2:2 + E], AF.Exp,
                             accum_out=L.rscr[:, 1:2])
            self.ac(ex)

            def rcp2(v, tt=tt):
                L.wac(v)
                v.reciprocal(L.probs_sb[:, tt:tt + 1], L.rscr[:, 1:2])
            self.dv(rcp2)

            def rt_a(v, tt=tt):
                L.wdv(v)
                v.tensor_scalar(L.max8[:, :], L.lacc[:, tt * E:(tt + 1) * E],
                                L.rscr[:, 0:1], None, ALU.is_equal)
            self.dv(rt_a)

            def rt_b(v, tt=tt):
                L.wdv(v)
                v.tensor_tensor(L.max8[:, :], L.max8[:, :], L.iota8m[:, :],
                                ALU.mult)
            self.dv(rt_b)

            def rt_c(v, tt=tt):
                L.wdv(v)
                v.tensor_reduce(L.rscr[:, 3:4], L.max8[:, :], AX, ALU.min)
            self.dv(rt_c)

            def rt_d(v, tt=tt):
                L.wdv(v)
                v.tensor_scalar(L.rf32_sb[:, tt:tt + 1], L.rscr[:, 3:4],
                                float(E), None, ALU.add)
            self.dv(rt_d)

            def rt2(v, tt=tt):
                L.wdv(v)
                v.tensor_scalar(L.ohf[:, :], L.iota8[:, :], L.rf32_sb[:, tt:tt + 1],
                                None, ALU.is_equal)
            self.dv(rt2)

            def rt2b(v, tt=tt):
                L.wdv(v)
                v.tensor_copy(L.ohb[:, :], L.ohf[:, :])
            self.dv(rt2b)

            def tp(t, tt=tt):
                L.wdv(t)
                t.transpose(L.tp_ps[0:E, 0:128], L.ohb[:, :], L.idb[:, :])
            self.pe(tp)

            def cpo(v, tt=tt):
                L.wpe(v)
                v.tensor_copy(L.ohT[:, tt * 128:(tt + 1) * 128], L.tp_ps[0:E, 0:128])
            self.dv(cpo)
        self.barrier()

        def sc0(v):
            v.tensor_tensor_scan(L.scan[:, :], L.ohT[:, :], L.ohT[:, :], 0.0,
                                 ALU.add, ALU.bypass)
        self.dv(sc0)

        steps_sc = [
            lambda v: v.tensor_copy(L.cntc[:, 0:1], L.scan[:, S - 1:S]),
            lambda v: v.tensor_scalar(L.cntc[:, 2:3], L.cntc[:, 0:1], 127.0,
                                      None, ALU.add),
            lambda v: v.tensor_copy(L.cnt_u[:, 0:1], L.cntc[:, 2:3]),
            lambda v: v.tensor_scalar(L.cnt_u[:, 0:1], L.cnt_u[:, 0:1], 7,
                                      None, ALU.logical_shift_right),
            lambda v: v.tensor_scalar(L.cnt_u[:, 0:1], L.cnt_u[:, 0:1], 7,
                                      None, ALU.logical_shift_left),
            lambda v: v.tensor_copy(L.cntc[:, 1:2], L.cnt_u[:, 0:1]),
        ]
        for st_ in steps_sc:
            self.dv(lambda v, st_=st_: (L.wdv(v), st_(v)))

        def spc(g, b=b):
            L.wdv(g)
            g.dma_start(out=L.cnt_out[b], in_=L.cntc[0:c.E, 0:1]) \
             .then_inc(L.s_gd.inc(16), 16)
            g.dma_start(out=L.probs_out[b], in_=L.probs_sb[:, :]) \
             .then_inc(L.s_gd.inc(16), 16)
        self.gp(spc)

        def bmm(t):
            L.wdv(t)
            t.matmul(L.psD[0:1, 0:E], L.cntc[0:E, 1:2], L.lones[:, :],
                     start=True, stop=True)
        self.pe(bmm)

        def bcp(v):
            L.wpe(v)
            v.tensor_copy(L.base_r[0:1, :], L.psD[0:1, 0:E])
        self.dv(bcp)
        self.row_bcast(L.base_r[0:1, 0:E], E, L.baseb[:, 0:E])
        self.barrier()

        def pf(g, b=b):
            g.dma_start(out=L.gtbl_out[b], in_=L.sentc[:, 0:NCH]) \
             .then_inc(L.s_gd.inc(16), 16)
        self.gp(pf)

        for tt in range(TT):
            def prod(v, tt=tt):
                L.wgd(v)
                v.tensor_tensor(L.rscr[:, 0:0 + E] if False else L.ohT[:, tt * 128:(tt + 1) * 128],
                                L.ohT[:, tt * 128:(tt + 1) * 128],
                                L.scan[:, tt * 128:(tt + 1) * 128], ALU.mult)
            self.dv(prod)

            def tpp(t, tt=tt):
                L.wdv(t)
                t.transpose(L.psD[:, 0:E], L.ohT[0:E, tt * 128:(tt + 1) * 128],
                            L.idf[0:E, 0:E])
            self.pe(tpp)

            def mkslot0(v, tt=tt):
                L.wpe(v)
                v.reduce_sum(L.pos_sb[:, tt:tt + 1], L.psD[:, 0:E], axis=AX)
            self.dv(mkslot0)

            steps_mk = [
                lambda v, tt=tt: v.tensor_scalar(L.ohf[:, :], L.iota8[:, :],
                                                 L.rf32_sb[:, tt:tt + 1],
                                                 None, ALU.is_equal),
                lambda v: v.tensor_tensor(L.rscr[:, 2:2 + E], L.ohf[:, :],
                                          L.baseb[:, 0:E], ALU.mult),
                lambda v: v.reduce_sum(L.rscr[:, 1:2], L.rscr[:, 2:2 + E],
                                       axis=AX),
                lambda v, tt=tt: v.tensor_tensor(L.rscr[:, 0:1], L.rscr[:, 1:2],
                                                 L.pos_sb[:, tt:tt + 1], ALU.add),
                lambda v: v.tensor_scalar(L.rscr[:, 0:1], L.rscr[:, 0:1], -1.0,
                                          None, ALU.add),
                lambda v: v.tensor_scalar(L.rscr[:, 0:1], L.rscr[:, 0:1], 0.0,
                                          None, ALU.max),
                lambda v: v.tensor_scalar(L.rscr[:, 0:1], L.rscr[:, 0:1],
                                          float(c.NSLOT - 1), None, ALU.min),
                lambda v: v.tensor_copy(L.slotu2[:, 0:1], L.rscr[:, 0:1]),
                lambda v: v.tensor_scalar(L.slotu2[:, 1:2], L.slotu2[:, 0:1],
                                          127, None, ALU.bitwise_and),
                lambda v: v.tensor_scalar(L.slotu2[:, 0:1], L.slotu2[:, 0:1],
                                          7, None, ALU.logical_shift_right),
                lambda v: v.tensor_copy(L.rscr[:, 1:2], L.slotu2[:, 1:2]),
                lambda v: v.tensor_copy(L.rscr[:, 2:3], L.slotu2[:, 0:1]),
                lambda v: v.tensor_scalar(L.rscr[:, 1:2], L.rscr[:, 1:2],
                                          float(NCH), None, ALU.mult),
                lambda v: v.tensor_tensor(L.rscr[:, 0:1], L.rscr[:, 1:2],
                                          L.rscr[:, 2:3], ALU.add),
                lambda v: v.tensor_copy(L.slotu[:, 0:1], L.rscr[:, 0:1]),
            ]
            for st_ in steps_mk:
                self.dv(lambda v, st_=st_: (L.wdv(v), st_(v)))

            def sct(g, tt=tt, b=b):
                L.wdv(g)
                L.wgd(g)
                g.iota(L.tokid[:, :], [[0, 1]], base=tt * 128, channel_multiplier=1,
                       allow_small_or_imprecise_dtypes=True)
                g.indirect_dma_start(
                    out=L.gtbl_out.rearrange("b p q -> (b p q)").rearrange(
                        "(n o) -> n o", o=1),
                    out_offset=IOA(L.slotu[:, 0:1], 0),
                    in_=L.tokid[:, 0:1],
                    in_offset=None,
                    element_offset=b * 128 * NCH,
                ).then_inc(L.s_gd.inc(16), 16)
            self.gp(sct)
        self.barrier()

        def rb(g, b=b):
            g.dma_start(out=L.gidx[:, b * NCH:(b + 1) * NCH], in_=L.gtbl_out[b]) \
             .then_inc(L.s_gd.inc(16), 16)
        self.gp(rb)

        def cmp_(v):
            v.tensor_scalar(L.ohT[0:E, 0:NCH], L.iotaNCH[0:E, :],
                            L.base_r[0:1, 0:E].rearrange("o e -> e o") if False
                            else L.cntc[0:E, 3:4], None, ALU.is_ge)
        def bt(t):
            L.wdv(t)
            t.transpose(L.psD[0:E, 0:1], L.base_r[0:1, 0:E], L.idf[0:1, 0:1])
        self.pe(bt)

        def btc(v):
            L.wpe(v)
            v.tensor_copy(L.cntc[:, 3:4], L.psD[0:E, 0:1])
        self.dv(btc)
        self.dv(cmp_)

        def emm(t):
            L.wdv(t)
            t.matmul(L.psD[0:1, 0:NCH], L.onescf[0:E, 0:1], L.ohT[0:E, 0:NCH],
                     start=True, stop=True)
        self.pe(emm)

        def etb(v, b=b):
            L.wpe(v)
            v.tensor_scalar(L.ech_f[0:1, b * NCH:(b + 1) * NCH],
                            L.psD[0:1, 0:NCH], -1.0, None, ALU.add)
        self.dv(etb)

        def etb2(v, b=b):
            L.wdv(v)
            v.tensor_copy(L.echw[0:1, b * NCH:(b + 1) * NCH],
                          L.ech_f[0:1, b * NCH:(b + 1) * NCH])
        self.dv(etb2)
        self.barrier()

    def alloc_moe(self, ph):
        nc, c = self.nc, self.cfg
        en = ph.enter_context
        self.weT_sb = en(nc.sbuf_tensor("weT_sb", [128, c.E * c.DC * c.D], BF16))
        self.be_sb = en(nc.sbuf_tensor("be_sb", [1, c.E * c.D], BF16))
        self.onesr_b = en(nc.sbuf_tensor("onesr_b", [1, 128], BF16))
        self.stg_tm = en(nc.sbuf_tensor("stg_tm", [128, c.D], BF16))
        self.stg_fm = en(nc.sbuf_tensor("stg_fm", [128, c.D], BF16))
        self.eo_sb = en(nc.sbuf_tensor("eo_sb", [128, c.D], BF16))
        self.gcf = en(nc.sbuf_tensor("gcf", [128, c.NCH], F32))
        self.r_e = nc.alloc_register(mybir.EngineType.PE, "moe_re")
        L = self

        def pro(g):
            g.memset(L.onesr_b[:, :], 1.0)
            g.memset(L.stg_tm[:, :], 0.0)
            g.dma_start(out=L.be_sb[:, :], in_=L.ber).then_inc(L.s_gd.inc(16), 16)
            for e in range(c.E):
                for dc in range(c.DC):
                    g.dma_start(
                        out=L.weT_sb[:, (e * c.DC + dc) * c.D:(e * c.DC + dc + 1) * c.D],
                        in_=L.WeT[e, dc]).then_inc(L.s_gd.inc(16), 16)
        self.gp(pro, inc=True)
        self.barrier()

    def moe_batch(self, b):
        c, L = self.cfg, self
        D, DC, E, NCH, S = c.D, c.DC, c.E, c.NCH, c.S
        nhalf = ceil_div(D, 512)

        steps_cl = [
            lambda v: v.tensor_copy(L.gcf[:, :],
                                    L.gidx[:, b * NCH:(b + 1) * NCH]),
            lambda v: v.tensor_scalar(L.gcf[:, :], L.gcf[:, :],
                                      float(S - 1), None, ALU.min),
            lambda v: v.tensor_copy(L.gidx[:, b * NCH:(b + 1) * NCH],
                                    L.gcf[:, :]),
        ]
        for st_ in steps_cl:
            self.dv(lambda v, st_=st_: (L.wdv(v), st_(v)))
        self.barrier()

        for ch in range(NCH):
            def ga(g, ch=ch, b=b):
                L.wdv(g)
                g.indirect_dma_start(
                    out=L.stg_tm[:, :],
                    out_offset=None,
                    in_=L.hid_out.rearrange("b s d -> (b s) d"),
                    in_offset=IOA(L.gidx[:, b * NCH + ch:b * NCH + ch + 1], 0),
                    element_offset=b * S * D,
                ).then_inc(L.s_gd.inc(16), 16)
            self.gp(ga)

            for dc in range(DC):
                def tp(t, dc=dc):
                    L.wgd(t)
                    L.wdv(t)
                    t.transpose(L.tp_ps[:, 0:128],
                                L.stg_tm[:, dc * 128:(dc + 1) * 128],
                                L.idb[:, :])
                self.pe(tp)

                def cp(v, dc=dc):
                    L.wpe(v)
                    v.tensor_copy(L.stg_fm[:, dc * 128:(dc + 1) * 128],
                                  L.tp_ps[:, 0:128])
                self.dv(cp)

            def mm(t, ch=ch, b=b):
                L.wdv(t)
                r_e = L.r_e
                t.reg_load(r_e, L.echw[0:1, b * NCH + ch:b * NCH + ch + 1])
                for e in range(E):
                    with t.If_eq(r_e, e):
                        for half in range(nhalf):
                            w = min(512, D - half * 512)
                            pp = L.psA if half == 0 else L.psB
                            for dc in range(DC):
                                co = (e * DC + dc) * D + half * 512
                                t.matmul(pp[:, 0:w],
                                         L.stg_fm[:, dc * 128:(dc + 1) * 128],
                                         L.weT_sb[:, co:co + w],
                                         start=(dc == 0), stop=False)
                        for half in range(nhalf):
                            w = min(512, D - half * 512)
                            pp = L.psA if half == 0 else L.psB
                            t.matmul(pp[:, 0:w],
                                     L.onesr_b[0:1, 0:128],
                                     L.be_sb[0:1, e * D + half * 512:e * D + half * 512 + w],
                                     start=False, stop=True)
                t.end_ifs()
            self.pe(mm)

            def ev(v, ch=ch, b=b):
                L.wpe(v)
                L.wgd(v)
                for half in range(nhalf):
                    w = min(512, D - half * 512)
                    pp = L.psA if half == 0 else L.psB
                    v.tensor_copy(L.eo_sb[:, half * 512:half * 512 + w], pp[:, 0:w])
            self.dv(ev)

            def st(g, ch=ch, b=b):
                L.wdv(g)
                g.dma_start(out=L.eout_out[b, ch * 128:(ch + 1) * 128, :],
                            in_=L.eo_sb[:, :]).then_inc(L.s_gd.inc(16), 16)
            self.gp(st)
        self.barrier()


def build_launch1(cfg, upto="all"):
    return L1(cfg, upto).build()



class L2:
    def __init__(self, cfg: CFG):
        self.cfg = cfg
        nc = bass.Bass("TRN2", target_bir_lowering=False)
        nc.detect_race_conditions = False
        self.nc = nc
        c = cfg
        S, D, E, NB, DC, TT, NCH, NSLOT = (c.S, c.D, c.E, c.NB, c.DC, c.TT,
                                           c.NCH, c.NSLOT)
        NBG = c.n_cores * NB
        self.NBG = NBG
        dt = nc.dram_tensor
        self.x1_in = dt("x1_in", [NB, DC, 128, S], F32, kind="ExternalInput").ap()
        self.hid_in = dt("hid_in", [NB, S, D], BF16, kind="ExternalInput").ap()
        self.eout_in = dt("eout_in", [NB, NSLOT, D], BF16, kind="ExternalInput").ap()
        self.gtbl_in = dt("gtbl_in", [NB, 128, NCH], U32, kind="ExternalInput").ap()
        self.probs_in = dt("probs_in", [NB, 128, TT], F32, kind="ExternalInput").ap()
        self.ln3_in = dt("ln3_in", [NB, 2, S], F32, kind="ExternalInput").ap()
        self.allcnt = dt("allcnt", [NBG, E], F32, kind="ExternalInput").ap()
        self.offmask = dt("offmask", [NBG, 2 * NB], F32, kind="ExternalInput").ap()
        self.identf = dt("identf2", [128, 128], F32, kind="ExternalInput").ap()
        self.lonesr = dt("lonesr", [E, E], F32, kind="ExternalInput").ap()
        self.gammaT = dt("gammaT", [128, DC], F32, kind="ExternalInput").ap()
        self.betaT = dt("betaT", [128, DC], F32, kind="ExternalInput").ap()
        self.out = dt("out", [NB, DC, 128, S], F32, kind="ExternalOutput").ap()
        self.y_hbm = dt("y_hbm", [NB * S + 128, D], BF16).ap()

        es = contextlib.ExitStack()
        self.es = es

        def sbuf(name, p, cc, dty):
            return es.enter_context(nc.sbuf_tensor(name, [p, cc], dty))
        self.sbuf = sbuf
        self.onesc = sbuf("onesc", 128, 1, BF16)
        self.onesrf = sbuf("onesrf", 128, 128, F32)
        self.lones = sbuf("lones", E, E, F32)
        self.rowb = sbuf("rowb", 1, 2048, F32)
        self.mu4 = sbuf("mu4", 1, S, F32)
        self.rs4 = sbuf("rs4", 1, S, F32)
        self.mu3 = sbuf("mu3", 1, S, F32)
        self.rs3 = sbuf("rs3", 1, S, F32)
        self.acnt = sbuf("acnt", NBG, E, F32)
        self.mycc = sbuf("mycc", E, 4, F32)
        self.mycu = sbuf("mycu", E, 1, U32)
        self.offc = sbuf("offc", E, 2, F32)
        self.maskc = sbuf("maskc", NBG, 2 * NB, F32)
        self.idb = sbuf("idb", 128, 128, BF16)
        self.iotaN = sbuf("iotaN", E, NCH, F32)
        self.iotaP = sbuf("iotaP", 128, 1, F32)
        self.base_r = sbuf("base_r", 1, E, F32)
        self.rows = sbuf("rows", 1, 6 * NCH, F32)
        self.cmpA = sbuf("cmpA", E, NCH, F32)
        self.cmpB = sbuf("cmpB", E, NCH, F32)
        self.thrb = sbuf("thrb", 128, NCH, F32)
        self.keptm = sbuf("keptm", 128, NCH, F32)
        self.gidxf = sbuf("gidxf", 128, NCH, F32)
        self.sidx = sbuf("sidx", 128, NCH, U32)
        self.gidx = sbuf("gidx", 128, NCH, U32)
        self.eo_sb = sbuf("eo_sb", 128, D, BF16)
        self.yt_sb = sbuf("yt_sb", 128, D, BF16)
        self.gam = sbuf("gam", 128, DC, F32)
        self.bet = sbuf("bet", 128, DC, F32)
        self.zbb = sbuf("zbb", 128, 512, F32)
        self.zrow = sbuf("zrow", 128, 512, F32)
        self.mvb = sbuf("mvb", 128, 1024, F32)
        self.x1c = sbuf("x1c", 128, DC * 512, F32)
        self.x2_sb = sbuf("x2_sb", 128, DC * S, F32)
        self.xb_sb = sbuf("xb_sb", 128, DC * S, BF16)
        self.stg_b = sbuf("stg_b", 128, DC * 512, BF16)
        self.sqc = sbuf("sqc", 128, DC * 512, BF16)

        self.psA = es.enter_context(nc.psum_tensor("psA", [128, 512], F32))
        self.psD = es.enter_context(nc.psum_tensor("psD", [128, 512], F32))
        self.tp_ps = es.enter_context(nc.psum_tensor("tp_ps", [128, 512], BF16))
        self.st_ps = es.enter_context(nc.psum_tensor("st_ps", [1, 1024], F32))

        self.s_hd = Sem(es, nc, "s_hd")
        self.s_gd = Sem(es, nc, "s_gd")
        self.s_pe = Sem(es, nc, "s_pe")
        self.s_dv = Sem(es, nc, "s_dv")
        self.s_ac = Sem(es, nc, "s_ac")
        self.s_gp = Sem(es, nc, "s_gp")
        self.sems = [self.s_hd, self.s_gd, self.s_pe, self.s_dv, self.s_ac,
                     self.s_gp]
        self.block = es.enter_context(nc.Block())
        self.items = {"tensor": [], "vector": [], "scalar": []}
        self._cursnap = None

    wait_all = L1.wait_all
    _curv = L1._curv
    _defer = L1._defer
    flush_items = L1.flush_items
    pe = L1.pe
    dv = L1.dv
    ac = L1.ac
    gp = L1.gp
    sy = L1.sy
    barrier = L1.barrier
    wpe = L1.wpe
    wdv = L1.wdv
    wac = L1.wac
    wgd = L1.wgd
    whd = L1.whd
    wgp = L1.wgp
    row_bcast = L1.row_bcast
    stats = L1.stats
    ln_apply = L1.ln_apply

    def build(self):
        c, L = self.cfg, self
        S, D, E, NB, DC, TT, NCH = c.S, c.D, c.E, c.NB, c.DC, c.TT, c.NCH
        NBG = self.NBG

        def pro(g):
            g.memset(L.onesc[:, :], 1.0)
            g.memset(L.onesrf[:, :], 1.0)
            g.iota(L.iotaN[:, :], [[128, NCH]], base=0, channel_multiplier=0,
                   allow_small_or_imprecise_dtypes=True)
            g.iota(L.iotaP[:, :], [[0, 1]], base=0, channel_multiplier=1,
                   allow_small_or_imprecise_dtypes=True)
            g.dma_start(out=L.lones[:, :], in_=L.lonesr).then_inc(L.s_gd.inc(16), 16)
            g.dma_start(out=L.acnt[:, :], in_=L.allcnt).then_inc(L.s_gd.inc(16), 16)
            g.dma_start(out=L.maskc[:, :], in_=L.offmask) \
             .then_inc(L.s_gd.inc(16), 16)
            g.dma_start(out=L.gam[:, :], in_=L.gammaT).then_inc(L.s_gd.inc(16), 16)
            g.dma_start(out=L.bet[:, :], in_=L.betaT).then_inc(L.s_gd.inc(16), 16)
            g.dma_start(out=L.idb[:, :], in_=L.identf).then_inc(L.s_gd.inc(16), 16)
        self.gp(pro, inc=True)
        self.barrier()

        for b in range(NB):
            self.batch(b)
        self.flush_items()
        self.sy(lambda e: self.wait_all(e))
        self.es.close()
        return self.nc

    def batch(self, b):
        c, L = self.cfg, self
        S, D, E, NB, DC, TT, NCH = c.S, c.D, c.E, c.NB, c.DC, c.TT, c.NCH
        NBG = self.NBG
        myrow = None

        def t0(v, b=b):
            pass
        def mm0(t, b=b):
            self.wait_all(t)
            t.matmul(L.psA[0:E, 0:2], L.acnt[:, :],
                     L.maskc[:, 2 * b:2 * b + 2], start=True, stop=True)
        self.pe(mm0)

        steps_c0 = [
            lambda v: v.tensor_copy(L.offc[:, 0:1], L.psA[0:E, 0:1]),
            lambda v: v.tensor_copy(L.mycc[:, 0:1], L.psA[0:E, 1:2]),
            lambda v: v.tensor_scalar(L.mycc[:, 2:3], L.mycc[:, 0:1], 127.0,
                                      None, ALU.add),
            lambda v: v.tensor_copy(L.mycu[:, 0:1], L.mycc[:, 2:3]),
            lambda v: v.tensor_scalar(L.mycu[:, 0:1], L.mycu[:, 0:1], 7, None,
                                      ALU.logical_shift_right),
            lambda v: v.tensor_scalar(L.mycu[:, 0:1], L.mycu[:, 0:1], 7, None,
                                      ALU.logical_shift_left),
            lambda v: v.tensor_copy(L.mycc[:, 1:2], L.mycu[:, 0:1]),
        ]
        first_c0 = [True]
        for st_ in steps_c0:
            def item(v, st_=st_, first_c0=first_c0):
                if first_c0[0]:
                    L.wpe(v)
                    first_c0[0] = False
                else:
                    L.wdv(v)
                st_(v)
            self.dv(item)

        def mm1(t):
            L.wdv(t)
            t.matmul(L.psD[0:1, 0:E], L.mycc[0:E, 1:2], L.lones[:, :],
                     start=True, stop=True)
        self.pe(mm1)

        def cp1(v):
            L.wpe(v)
            v.tensor_copy(L.base_r[0:1, :], L.psD[0:1, 0:E])
        self.dv(cp1)

        def tp1(t):
            L.wdv(t)
            t.transpose(L.psD[0:E, 0:1], L.base_r[0:1, 0:E], L.onesrf[0:1, 0:1])
        self.pe(tp1)

        def cp2a(v):
            L.wpe(v)
            v.tensor_copy(L.mycc[:, 3:4], L.psD[0:E, 0:1])
        self.dv(cp2a)

        def cp2a2(v):
            L.wdv(v)
            v.tensor_tensor(L.offc[:, 1:2], L.mycc[:, 3:4], L.mycc[:, 1:2],
                            ALU.add)
        self.dv(cp2a2)

        def cp2(v):
            L.wdv(v)
            v.tensor_scalar(L.cmpA[:, :], L.iotaN[:, :], L.mycc[:, 3:4], None,
                            ALU.is_ge)
            v.tensor_scalar(L.cmpB[:, :], L.iotaN[:, :], L.offc[:, 1:2], None,
                            ALU.is_ge)
        self.dv(cp2)

        def cp2b(v):
            L.wdv(v)
            v.tensor_tensor(L.cmpB[:, :], L.cmpA[:, :], L.cmpB[:, :],
                            ALU.subtract)
        self.dv(cp2b)

        def mm2(t):
            L.wdv(t)
            t.matmul(L.psD[0:1, 0:NCH], L.offc[0:E, 0:1], L.cmpB[:, :],
                     start=True, stop=True)
        self.pe(mm2)

        def cp3(v):
            L.wpe(v)
            v.tensor_copy(L.rows[0:1, NCH:2 * NCH], L.psD[0:1, 0:NCH])
            v.tensor_tensor(L.cmpA[:, :], L.cmpA[:, :], L.cmpB[:, :],
                            ALU.subtract)
        self.dv(cp3)

        def mm3(t):
            L.wdv(t)
            t.matmul(L.psD[0:1, 0:NCH], L.mycc[0:E, 1:2], L.cmpA[:, :],
                     start=True, stop=True)
        self.pe(mm3)

        steps_c4 = [
            lambda v: v.tensor_copy(L.rows[0:1, 0:NCH], L.psD[0:1, 0:NCH]),
            lambda v: v.tensor_tensor(L.rows[0:1, 2 * NCH:3 * NCH],
                                      L.rows[0:1, 0:NCH],
                                      L.rows[0:1, NCH:2 * NCH], ALU.subtract),
            lambda v: v.tensor_scalar(L.rows[0:1, 2 * NCH:3 * NCH],
                                      L.rows[0:1, 2 * NCH:3 * NCH],
                                      float(c.capacity), None, ALU.add),
            lambda v: v.tensor_tensor(L.rows[0:1, 2 * NCH:3 * NCH],
                                      L.rows[0:1, 2 * NCH:3 * NCH],
                                      L.iotaN[0:1, 0:NCH], ALU.subtract),
        ]
        first_c4 = [True]
        for st_ in steps_c4:
            def item(v, st_=st_, first_c4=first_c4):
                if first_c4[0]:
                    L.wpe(v)
                    first_c4[0] = False
                else:
                    L.wdv(v)
                st_(v)
            self.dv(item)
        self.row_bcast(L.rows[0:1, 2 * NCH:3 * NCH], NCH, L.thrb[:, :])

        def ld_gt(g, b=b):
            g.dma_start(out=L.gidx[:, :], in_=L.gtbl_in[b]) \
             .then_inc(L.s_gd.inc(16), 16)
        self.gp(ld_gt)
        self.barrier()

        trash = float((NB - b) * S)
        steps_kp = [
            lambda v: v.tensor_scalar(L.keptm[:, :], L.thrb[:, :],
                                      L.iotaP[:, 0:1], None, ALU.is_gt),
            lambda v: v.tensor_copy(L.gidxf[:, :], L.gidx[:, :]),
            lambda v: v.tensor_scalar(L.gidxf[:, :], L.gidxf[:, :],
                                      -float(c.SENT), None, ALU.add),
            lambda v: v.tensor_tensor(L.gidxf[:, :], L.keptm[:, :],
                                      L.gidxf[:, :], ALU.mult),
            lambda v: v.tensor_scalar(L.gidxf[:, :], L.gidxf[:, :],
                                      float(c.SENT), None, ALU.add),
            lambda v: v.tensor_scalar(L.gidxf[:, :], L.gidxf[:, :],
                                      trash, None, ALU.min),
            lambda v: v.tensor_copy(L.sidx[:, :], L.gidxf[:, :]),
        ]
        for st_ in steps_kp:
            self.dv(lambda v, st_=st_: (L.wdv(v), st_(v)))

        def cpy(g, b=b):
            g.dma_start(out=L.y_hbm[b * S:(b + 1) * S, :],
                        in_=L.hid_in[b]).then_inc(L.s_gd.inc(16), 16)
        self.gp(cpy)
        self.barrier()

        for ch in range(NCH):
            def sc(g, ch=ch, b=b):
                L.wgd(g)
                g.dma_start(out=L.eo_sb[:, :],
                            in_=L.eout_in[b, ch * 128:(ch + 1) * 128, :]) \
                 .then_inc(L.s_gd.inc(16), 16)
                g.wait_ge(L.s_gd.h, L.s_gd.v)
                g.indirect_dma_start(
                    out=L.y_hbm,
                    out_offset=IOA(L.sidx[:, ch:ch + 1], 0),
                    in_=L.eo_sb[:, :],
                    in_offset=None,
                    element_offset=b * S * D,
                ).then_inc(L.s_gd.inc(16), 16)
            self.gp(sc)
        self.barrier()

        def ld3(g, b=b):
            g.dma_start(out=L.mu3[0:1, :], in_=L.ln3_in[b, 0:1, :]) \
             .then_inc(L.s_gd.inc(16), 16)
            g.dma_start(out=L.rs3[0:1, :], in_=L.ln3_in[b, 1:2, :]) \
             .then_inc(L.s_gd.inc(16), 16)
            g.dma_start(out=L.thrb[:, 0:TT], in_=L.probs_in[b]) \
             .then_inc(L.s_gd.inc(16), 16)
        self.gp(ld3)
        self.barrier()

        for tt in range(TT):
            def ldy(g, tt=tt, b=b):
                L.wpe(g)
                g.dma_start(out=L.yt_sb[:, :],
                            in_=L.y_hbm[b * S + tt * 128:b * S + (tt + 1) * 128, :]) \
                 .then_inc(L.s_gd.inc(16), 16)
            self.gp(ldy)

            def sc_y(v, tt=tt):
                L.wgd(v)
                v.tensor_scalar(L.yt_sb[:, :], L.yt_sb[:, :],
                                L.thrb[:, tt:tt + 1], None, ALU.mult)
            self.dv(sc_y)

            for dc in range(DC):
                def tp(t, tt=tt, dc=dc):
                    L.wdv(t)
                    t.transpose(L.tp_ps[:, 0:128],
                                L.yt_sb[:, dc * 128:(dc + 1) * 128],
                                L.idb[:, :])
                self.pe(tp)

                def cp(v, tt=tt, dc=dc):
                    L.wpe(v)
                    v.tensor_copy(L.xb_sb[:, dc * S + tt * 128:dc * S + (tt + 1) * 128],
                                  L.tp_ps[:, 0:128])
                self.dv(cp)
        self.barrier()
        for j in range(c.QC):
            c0 = j * 512

            def ldx1(sy_, j=j, c0=c0, b=b):
                L.wdv(sy_)
                for dc in range(DC):
                    sy_.dma_start(out=L.x1c[:, dc * 512:(dc + 1) * 512],
                                  in_=L.x1_in[b, dc, :, c0:c0 + 512]) \
                       .then_inc(L.s_hd.inc(16), 16)
            self.sy(ldx1)
            self.row_bcast(L.rs3[0:1, c0:c0 + 512], 512, L.zbb[:, 0:512])
            self.row_bcast(L.mu3[0:1, c0:c0 + 512], 512, L.zrow[:, 0:512])

            def x2c(v, j=j, c0=c0):
                L.wdv(v)
                L.whd(v)
                for dc in range(DC):
                    sl = slice(dc * S + c0, dc * S + c0 + 512)
                    v.tensor_tensor(L.mvb[:, 0:512], L.x1c[:, dc * 512:(dc + 1) * 512],
                                    L.zrow[:, 0:512], ALU.subtract)
                    v.tensor_tensor(L.mvb[:, 512:1024], L.mvb[:, 0:512],
                                    L.zbb[:, 0:512], ALU.mult)
                    v.tensor_tensor(L.x2_sb[:, sl], L.mvb[:, 512:1024],
                                    L.xb_sb[:, sl], ALU.add)
            self.dv(x2c)
        self.barrier()
        self.stats(lambda dc: L.x2_sb[:, dc * S:(dc + 1) * S], S, L.mu4, L.rs4)
        self.barrier()
        for j in range(c.QC):
            c0 = j * 512
            self.row_bcast(L.rs4[0:1, c0:c0 + 512], 512, L.zbb[:, 0:512])
            self.row_bcast(L.mu4[0:1, c0:c0 + 512], 512, L.zrow[:, 0:512])

            def f4(v, j=j, c0=c0):
                L.wdv(v)
                for dc in range(DC):
                    sl = slice(dc * S + c0, dc * S + c0 + 512)
                    v.tensor_tensor(L.mvb[:, 0:512], L.x2_sb[:, sl],
                                    L.zrow[:, 0:512], ALU.subtract)
                    v.tensor_tensor(L.mvb[:, 512:1024], L.mvb[:, 0:512],
                                    L.zbb[:, 0:512], ALU.mult)
                    v.tensor_scalar(L.xb_sb[:, sl], L.mvb[:, 512:1024],
                                    L.gam[:, dc:dc + 1], L.bet[:, dc:dc + 1],
                                    ALU.mult, ALU.add)
            self.dv(f4)
        self.barrier()

        self.stats(lambda dc: L.xb_sb[:, dc * S:(dc + 1) * S], S, L.mu4, L.rs4,
                   src_bf16=True)
        self.barrier()
        for j in range(c.QC):
            c0 = j * 512

            def ldx1(sy_, j=j, c0=c0, b=b):
                L.whd(sy_)
                L.wdv(sy_)
                for dc in range(DC):
                    sy_.dma_start(out=L.x1c[:, dc * 512:(dc + 1) * 512],
                                  in_=L.x1_in[b, dc, :, c0:c0 + 512]) \
                       .then_inc(L.s_hd.inc(16), 16)
            self.sy(ldx1)
            self.row_bcast(L.rs4[0:1, c0:c0 + 512], 512, L.zbb[:, 0:512])
            self.row_bcast(L.mu4[0:1, c0:c0 + 512], 512, L.zrow[:, 0:512])

            def oc_(v, j=j, c0=c0):
                L.wdv(v)
                L.whd(v)
                for dc in range(DC):
                    sl = slice(dc * S + c0, dc * S + c0 + 512)
                    v.tensor_tensor(L.mvb[:, 0:512], L.xb_sb[:, sl],
                                    L.zrow[:, 0:512], ALU.subtract)
                    v.tensor_tensor(L.mvb[:, 512:1024], L.mvb[:, 0:512],
                                    L.zbb[:, 0:512], ALU.mult)
                    v.tensor_tensor(L.x1c[:, dc * 512:(dc + 1) * 512],
                                    L.x1c[:, dc * 512:(dc + 1) * 512],
                                    L.mvb[:, 512:1024], ALU.add)
            self.dv(oc_)

            def spo(sy_, j=j, c0=c0, b=b):
                L.wdv(sy_)
                for dc in range(DC):
                    sy_.dma_start(out=L.out[b, dc, :, c0:c0 + 512],
                                  in_=L.x1c[:, dc * 512:(dc + 1) * 512]) \
                       .then_inc(L.s_hd.inc(16), 16)
            self.sy(spo)
        self.sy(lambda e: e.wait_ge(L.s_hd.h, L.s_hd.v))
        self.barrier()


def build_launch2(cfg):
    return L2(cfg).build()



def prep_inputs(cfg, inputs):
    E, D, H, NB, S, DC = cfg.E, cfg.D, cfg.H, cfg.NB, cfg.S, cfg.DC
    HP = H // 2
    x = np.ascontiguousarray(inputs["L0_attn_x"])
    qkv_w = np.asarray(inputs["L0_attn_qkv_w"])
    common = {}
    common["gtok"] = np.ascontiguousarray(
        np.asarray(inputs["L0_attn_global_token"]).reshape(D).reshape(DC, 128))
    common["qkv_wT"] = np.ascontiguousarray(qkv_w.T).reshape(DC, 128, 3 * D)
    common["proj_wT"] = np.ascontiguousarray(
        np.asarray(inputs["L0_attn_proj_w"]).T).reshape(DC, 128, D)
    common["proj_bT"] = np.ascontiguousarray(
        np.asarray(inputs["L0_attn_proj_b"]).reshape(DC, 128).T)
    Wq = np.asarray(inputs["L0_ffn_Wq"]).reshape(H * 64, D)
    Wk = np.asarray(inputs["L0_ffn_Wk"]).reshape(H * 64, D)
    Wv = np.asarray(inputs["L0_ffn_Wv"]).reshape(H * 64, D)
    Wqkv = np.concatenate([Wq, Wk, Wv], 0)
    common["WqkvT"] = np.ascontiguousarray(Wqkv.T).reshape(DC, 128, 3 * D)
    common["Wv_nat"] = np.ascontiguousarray(Wv)
    common["WswT"] = np.ascontiguousarray(np.asarray(inputs["L0_ffn_Wsw"]).T)
    common["bswr"] = np.ascontiguousarray(
        np.asarray(inputs["L0_ffn_bsw"]).reshape(1, E))
    bq = np.asarray(inputs["L0_ffn_bq"]).reshape(HP, 128).T
    bk = np.asarray(inputs["L0_ffn_bk"]).reshape(HP, 128).T
    bv = np.asarray(inputs["L0_ffn_bv"]).reshape(HP, 128).T
    common["bqkvT"] = np.ascontiguousarray(np.concatenate([bq, bk, bv], 1))
    common["bvr"] = np.ascontiguousarray(np.asarray(inputs["L0_ffn_bv"]).reshape(1, H * 64))
    common["WeT"] = np.ascontiguousarray(
        np.transpose(np.asarray(inputs["L0_ffn_We"]), (0, 2, 1))).reshape(
            E, DC, 128, D)
    common["ber"] = np.ascontiguousarray(
        np.asarray(inputs["L0_ffn_be"]).reshape(1, E * D))
    common["identf"] = np.eye(128, dtype=np.float32)
    ki = np.arange(128)[:, None]
    qi = np.arange(128)[None, :]
    common["amask0"] = (((ki + 1) // 64) <= ((qi + 1) // 64)).astype(np.float32)
    common["amask1"] = (((ki + 129) // 64) <= ((qi + 1) // 64)).astype(np.float32)
    common["fmask0"] = (ki <= qi).astype(np.float32)
    suf = np.zeros((128, 129), np.float32)
    suf[:, :128] = (np.arange(128)[:, None] > np.arange(128)[None, :])
    suf[:, 128] = 1.0
    common["sufones"] = suf
    common["lonesr"] = (np.arange(E)[:, None] < np.arange(E)[None, :]).astype(
        np.float32)

    maps = []
    for ci in range(cfg.n_cores):
        m = dict(common)
        xs = np.asarray(x[ci * NB:(ci + 1) * NB])
        m["xT"] = np.ascontiguousarray(xs.transpose(0, 2, 1)).reshape(
            NB, DC, 128, S)
        maps.append(m)
    return maps


def kernel(**inputs):
    cfg = FULL
    return _run(cfg, inputs)


def _run(cfg, inputs):
    maps1 = prep_inputs(cfg, inputs)
    nc1 = build_launch1(cfg, upto="all")
    res1 = run_bass_kernel_spmd(nc1, maps1, core_ids=list(range(cfg.n_cores)))

    maps2 = prep_launch2(cfg, inputs, res1.results)
    nc2 = build_launch2(cfg)
    res2 = run_bass_kernel_spmd(nc2, maps2, core_ids=list(range(cfg.n_cores)))

    outs = []
    for ci in range(cfg.n_cores):
        o = res2.results[ci]["out"].reshape(cfg.NB, cfg.D, cfg.S)
        outs.append(np.ascontiguousarray(o.transpose(0, 2, 1)))
    return np.concatenate(outs, 0).astype(np.float32)


def prep_launch2(cfg, inputs, res1):
    NB, E, D, DC = cfg.NB, cfg.E, cfg.D, cfg.DC
    NBG = cfg.n_cores * NB
    allcnt = np.concatenate(
        [np.asarray(r["cnt_out"]).reshape(NB, E) for r in res1], 0).astype(
            np.float32)
    lones = (np.arange(E)[:, None] < np.arange(E)[None, :]).astype(np.float32)
    gammaT = np.ascontiguousarray(
        np.asarray(inputs["L0_ffn_gamma"]).reshape(DC, 128).T)
    betaT = np.ascontiguousarray(
        np.asarray(inputs["L0_ffn_beta"]).reshape(DC, 128).T)
    identf = np.eye(128, dtype=np.float32)
    maps = []
    for ci in range(cfg.n_cores):
        m = {}
        r = res1[ci]
        m["x1_in"] = np.asarray(r["x1_out"])
        m["hid_in"] = np.asarray(r["hid_out"])
        m["eout_in"] = np.asarray(r["eout_out"])
        m["gtbl_in"] = np.asarray(r["gtbl_out"])
        m["probs_in"] = np.asarray(r["probs_out"])
        m["ln3_in"] = np.asarray(r["ln3_out"])
        m["allcnt"] = allcnt
        om = np.zeros((NBG, 2 * NB), np.float32)
        for b in range(NB):
            g = ci * NB + b
            om[:g, 2 * b] = 1.0
            om[g, 2 * b + 1] = 1.0
        m["offmask"] = om
        m["lonesr"] = lones
        m["gammaT"] = gammaT
        m["betaT"] = betaT
        m["identf2"] = identf
        maps.append(m)
    return maps
